# version 08
# baseline (speedup 1.0000x reference)
import numpy as np
import ml_dtypes
import concourse.bass as bass
import concourse.mybir as mybir
from concourse.bass_utils import run_bass_kernel_spmd
from contextlib import ExitStack

F32 = mybir.dt.float32
BF16 = mybir.dt.bfloat16
I32 = mybir.dt.int32
AF = mybir.ActivationFunctionType
ALU = mybir.AluOpType
AXX = mybir.AxisListType.X

D = 1024
HID = 3072
C = 128
NCORE = 8

ENG2SEM = {"tensor": "st", "vector": "sv", "scalar": "ss", "gpsimd": "sp"}
NLANE = 8
LANES = tuple(f"ld{i}" for i in range(NLANE))
NGLANE = 2
GLANES = tuple(f"gd{i}" for i in range(NGLANE))
SEMKEYS = ("st", "sv", "ss", "sp", "scc") + LANES + GLANES


class Prog:

    def __init__(self):
        self.sched = []
        self.cnt = {k: 0 for k in SEMKEYS}
        self.collect = None
        self.lane_rr = 0
        self.glane_rr = 0

    def step(self, engine, fn, deps=(), inc=1, semkey=None):
        key = semkey or ENG2SEM[engine]
        deps = tuple(d for d in deps if d is not None)
        self.sched.append((engine, fn, deps, inc, key))
        self.cnt[key] += inc
        ev = (key, self.cnt[key])
        if self.collect is not None:
            self.collect[key] = ev
        return ev

    def dma(self, engine, out, in_, deps=(), **kw):
        if engine == "gpsimd":
            lane = GLANES[self.glane_rr]
            self.glane_rr = (self.glane_rr + 1) % NGLANE
        else:
            lane = LANES[self.lane_rr]
            self.lane_rr = (self.lane_rr + 1) % NLANE
        deps = tuple(d for d in deps if d is not None)
        if self.cnt[lane] > 0:
            deps = deps + ((lane, self.cnt[lane]),)
        return self.step(
            engine,
            lambda e, out=out, in_=in_, kw=kw: e.dma_start(out=out, in_=in_, **kw),
            deps=deps, inc=16, semkey=lane)

    def begin_fence(self):
        self.collect = {}

    def end_fence(self):
        f = tuple(self.collect.values())
        self.collect = None
        return f

    def fence_all(self):
        return tuple((k, v) for k, v in self.cnt.items() if v > 0)

    def finalize(self, block, sems):
        water = {}

        def emit(this_engine, eng):
            for engine, fn, deps, inc, key in self.sched:
                if engine != this_engine:
                    continue
                for (k, v) in deps:
                    if water.get((this_engine, k), 0) < v:
                        eng.wait_ge(sems[k], v)
                        water[(this_engine, k)] = v
                inst = fn(eng)
                if inst is not None:
                    inst.then_inc(sems[key], inc)
                else:
                    eng.nop().then_inc(sems[key], inc)

        @block.sync
        def _(eng):
            emit("sync", eng)
            for k in LANES + GLANES + ("scc",):
                if self.cnt[k] > 0:
                    eng.wait_ge(sems[k], self.cnt[k])

        @block.tensor
        def _(eng):
            emit("tensor", eng)

        @block.vector
        def _(eng):
            emit("vector", eng)

        @block.scalar
        def _(eng):
            emit("scalar", eng)

        @block.gpsimd
        def _(eng):
            emit("gpsimd", eng)


def build_consts():
    con = {}
    con["ones1"] = np.ones((1, 128), np.float32)
    con["ones1_bf"] = np.ones((1, 128), ml_dtypes.bfloat16)
    con["eye_bf"] = np.eye(128, dtype=ml_dtypes.bfloat16)
    con["eye_f32"] = np.eye(128, dtype=np.float32)
    con["i64_f32"] = np.eye(64, dtype=np.float32)
    con["i64rep_bf"] = np.tile(np.eye(64), (1, 16)).astype(ml_dtypes.bfloat16)
    idx = np.arange(128)
    m = np.where(idx[:, None] > idx[None, :], -1.0, 0.0).astype(np.float32)
    con["neglmask_bf"] = m
    con["umask_bf"] = (idx[:, None] <= idx[None, :]).astype(
        np.float32)
    ho = np.zeros((128, 2), np.float32)
    ho[:64, 0] = 1.0
    ho[64:, 1] = 1.0
    con["headones_bf"] = ho.astype(ml_dtypes.bfloat16)
    s2 = np.zeros((2, 128), np.float32)
    s2[0, :64] = 1.0
    s2[1, 64:] = 1.0
    con["sel2t_bf"] = s2.astype(ml_dtypes.bfloat16)
    selh = np.zeros((16, 16 * 64), np.float32)
    for h in range(16):
        selh[h, h * 64:(h + 1) * 64] = 1.0
    con["selh_f32"] = selh
    con["selh_bf"] = selh.astype(ml_dtypes.bfloat16)
    selj = np.zeros((16, 8 * 128), np.float32)
    for j in range(8):
        selj[2 * j, j * 128:j * 128 + 64] = 1.0
        selj[2 * j + 1, j * 128 + 64:(j + 1) * 128] = 1.0
    con["selj_bf"] = selj.astype(ml_dtypes.bfloat16)
    return con


STAGE = 99
SUB = 999
DEBUG_SCRATCH = False


def build_program(TOK):
    NCH = TOK // C
    PT_ = 256
    NPT = TOK // PT_
    CPT = PT_ // C
    G1K = min(1024, TOK)
    NL = 7

    nc = bass.Bass("TRN2", target_bir_lowering=False, debug=False)
    P = Prog()
    CON = build_consts()

    def din(name, shape, dtype):
        return nc.dram_tensor(name, list(shape), dtype, kind="ExternalInput").ap()

    d_x = din("x", (TOK, D), F32)
    d_w = {n: din(n, (D, D), F32) for n in ("wq", "wk", "wv", "wgt", "wo")}
    d_wab = din("wab", (D, 64), F32)
    d_bab = din("bab", (64, 1), F32)
    d_gnorm = din("gnorm", (1, D), F32)
    d_fg = din("ffn_g", (HID, D), I32)
    d_fu = din("ffn_u", (HID, D), I32)
    d_fd = din("ffn_d", (D, HID), I32)
    d_selv = din("selv", (1, 4), F32)
    d_con = {k: din("c_" + k, v.shape,
                    F32 if v.dtype == np.float32 else BF16)
             for k, v in CON.items()}
    d_out = nc.dram_tensor("out", [TOK, D], F32, kind="ExternalOutput").ap()

    def scr(name, shape, dtype):
        kind = "ExternalOutput" if DEBUG_SCRATCH else "Internal"
        if name in ("s_agi", "s_ago"):
            kind = "Internal"
        return nc.dram_tensor(name, list(shape), dtype, kind=kind).ap()

    s_xn = scr("s_xn", (TOK, D), BF16)
    s_kfm = scr("s_kfm", (D, TOK), BF16)
    s_gate = scr("s_gate", (TOK, D), BF16)
    s_ab = scr("s_ab", (TOK // 256, 64, 256), F32)
    s_z = scr("s_z", (NCH, 128, 2048), BF16)
    s_ft = scr("s_ft", (NCH, 64, 2048), BF16)
    s_kd = scr("s_kd", (NCH, 128, 1024), BF16)
    s_pt = scr("s_pt", (NCH, 128, 2048), BF16)
    s_qt = scr("s_qt", (NCH, 8, 128, 128), BF16)
    s_cath = scr("s_cath", (NCH, 2048), BF16)
    s_catl = scr("s_catl", (NCH, 2048), BF16)
    s_agi = scr("s_agi", (2048, 64), F32)
    s_ago = scr("s_ago", (4 * 2048, 64), F32)
    s_xmid = scr("s_xmid", (TOK, D), F32)
    s_x2p = scr("s_x2p", (128, 8 * TOK), BF16)
    s_wgbf = scr("s_wgbf", (HID, D), BF16)
    s_wubf = scr("s_wubf", (HID, D), BF16)
    s_wdbf = scr("s_wdbf", (D, HID), BF16)
    s_fdown = scr("s_fdown", (TOK, D), BF16)
    s_xdbg = scr("s_xdbg", (NCH, 128, 2048), BF16)
    s_emdbg = scr("s_emdbg", (NCH, 4, 128, 512), F32)
    s_tmdbg = scr("s_tmdbg", (NCH, 128, 80), F32)
    s_kkdbg = scr("s_kkdbg", (NCH, 4, 128, 512), F32)
    s_kkdbg2 = scr("s_kkdbg2", (NCH, 4, 128, 512), F32)
    s_ddbg = scr("s_ddbg", (NCH, 4, 128, 512), F32)

    ctx = ExitStack()
    ps_t = ctx.enter_context(nc.psum_tensor("ps", [128, 3584], F32))
    psb_t = ctx.enter_context(nc.psum_tensor("psb", [128, 1024], BF16))

    def psf(b, p0, p1, c0, n):
        return ps_t[p0:p1, 512 * b + c0:512 * b + c0 + n]

    bank_ev = [()] * 8
    bank_hold = [False] * 8
    rr = [0]
    psb_free = [()]

    def bank(n=1, hold=False):
        start = rr[0]
        for off in range(8):
            b0 = (start + off) % 7
            if b0 + n > 7:
                continue
            if any(bank_hold[b] for b in range(b0, b0 + n)):
                continue
            rr[0] = (b0 + n) % 7
            deps = sum((bank_ev[b] for b in range(b0, b0 + n)), ())
            if hold:
                for b in range(b0, b0 + n):
                    bank_hold[b] = True
            return b0, list(deps)
        raise RuntimeError("no banks")

    def bank_done(b0, n, evs):
        for b in range(b0, b0 + n):
            bank_ev[b] = tuple(evs)
            bank_hold[b] = False

    sems = {}
    for name in SEMKEYS:
        sems[name] = ctx.enter_context(nc.semaphore(name))
    block = ctx.enter_context(nc.Block())

    sb = nc.alloc_sbuf_tensor
    t_con = {k: sb("t_" + k, list(v.shape),
                   F32 if v.dtype == np.float32 else BF16)
             for k, v in CON.items()}
    t_gnb = sb("t_gnb", [128, D], F32)
    t_gcols = sb("t_gcols", [64, 16 * NCH], F32)
    t_accF = sb("t_accF", [64, 1024], F32)
    t_uccF = sb("t_uccF", [64, 1024], F32)
    t_bab = sb("t_bab", [64, 1], F32)

    t_eps = sb("t_eps", [128, 1], F32)
    ev_con = [P.dma("sync", t_con[k][:, :], d_con[k]) for k in CON]
    ev_eps = P.step("vector", lambda e: e.memset(t_eps[:, :], 1e-5))
    ev_con.append(ev_eps)
    evc0 = ev_con[-1]
    evb = P.dma("sync", t_bab[:, :], d_bab)
    ev_bab = P.step("scalar", lambda e: e.mul(t_bab[:, :], t_bab[:, :], 0.5),
                    deps=[evb])

    a_ctx = ExitStack()

    def asb(name, shape, dtype):
        return a_ctx.enter_context(nc.sbuf_tensor(name, shape, dtype))

    t_wt = {n: asb("t_" + n, [128, 8 * D], BF16)
            for n in ("wq", "wk", "wv")}
    t_xnfm = asb("t_xnfm", [128, 8 * PT_], BF16)
    t_qn = asb("t_qn", [128, 2 * 8 * PT_], BF16)
    t_kn = asb("t_kn", [128, 2 * 8 * PT_], BF16)
    t_vb = asb("t_vb", [128, 2 * CPT * D], BF16)
    t_ktm = asb("t_ktm", [128, 2 * D], BF16)
    t_sqb = asb("t_sqb", [128, 8 * PT_], BF16)
    t_qbf = asb("t_qbf", [128, 8 * PT_], BF16)
    t_rn = asb("t_rn", [2, 1024], F32)
    t_rn2 = asb("t_rn2", [2, 1024], F32)
    t_rnb = asb("t_rnb", [2, 4 * PT_], BF16)
    sub_ctx = ExitStack()

    def ssb(name, shape, dtype):
        return sub_ctx.enter_context(nc.sbuf_tensor(name, shape, dtype))

    t_wt["wgt"] = ssb("t_wgt", [128, 8 * D], BF16)
    t_wab = ssb("t_wab", [128, 8 * 64], BF16)
    t_wld = ssb("t_wld", [128, 2 * D], F32)
    t_ffw = ssb("t_ffw", [128, 2 * D], I32)
    t_ffb = ssb("t_ffb", [128, 2 * D], BF16)
    t_gn1 = ssb("t_gn1", [1, D], F32)

    ev_g1 = P.dma("sync", t_gn1[:, :], d_gnorm)
    b0, bd = bank(2)
    ev_gm = [P.step("tensor", lambda e, i=i, b0=b0: e.matmul(
        psf(b0 + i, 0, 128, 0, 512), t_con["ones1"][:, :],
        t_gn1[:, 512 * i:512 * (i + 1)], start=True, stop=True),
        deps=[ev_g1, evc0] + bd) for i in range(2)]
    ev_gnb = P.step("vector", lambda e, b0=b0: e.tensor_copy(
        t_gnb[:, :], ps_t[0:128, 512 * b0:512 * b0 + 1024]), deps=ev_gm)
    bank_done(b0, 2, [ev_gnb])

    def _cut(dep_evs):
        for cix in range(NCH):
            P.dma("sync", d_out[cix * 128:(cix + 1) * 128, :], t_gnb[:, :],
                  deps=list(dep_evs) + [ev_gnb])
        P.finalize(block, sems)
        ctx.close()
        return nc

    wev = {}
    wl_ev = [None, None]
    for wkey, cols in [("wq", D), ("wk", D), ("wv", D), ("wgt", D), ("wab", 64)]:
        dst = t_wab if wkey == "wab" else t_wt[wkey]
        src = d_wab if wkey == "wab" else d_w[wkey]
        evs = []
        for kc in range(8):
            par = kc % 2
            ev1 = P.dma("sync", t_wld[:, par * D:par * D + cols],
                        src[kc * 128:(kc + 1) * 128, :],
                        deps=[wl_ev[par]] if wl_ev[par] else [])
            evc = P.step("gpsimd", lambda e, kc=kc, par=par, dst=dst,
                         cols=cols:
                         e.tensor_copy(dst[:, kc * cols:(kc + 1) * cols],
                                       t_wld[:, par * D:par * D + cols]),
                         deps=[ev1])
            wl_ev[par] = evc
            evs.append(evc)
        wev[wkey] = evs

    ffn_cast_ev = []
    fc_ev = [None, None]
    fc_i = 0
    for (dsrc, ddst, rows, cols) in [(d_fg, s_wgbf, HID, D), (d_fu, s_wubf, HID, D),
                                     (d_fd, s_wdbf, D, HID)]:
        for r in range(rows // 128):
            for cc0 in range(cols // D):
                par = fc_i % 2
                fc_i += 1
                ev1 = P.dma("gpsimd", t_ffw[:, par * D:(par + 1) * D],
                            dsrc[r * 128:(r + 1) * 128, cc0 * D:(cc0 + 1) * D],
                            deps=[fc_ev[par]] if fc_ev[par] else [])
                ev2 = P.step("gpsimd", lambda e, par=par: e.tensor_copy(
                    t_ffb[:, par * D:(par + 1) * D],
                    t_ffw[:, par * D:(par + 1) * D]), deps=[ev1])
                ev3 = P.dma("gpsimd", ddst[r * 128:(r + 1) * 128,
                                           cc0 * D:(cc0 + 1) * D],
                            t_ffb[:, par * D:(par + 1) * D], deps=[ev2])
                fc_ev[par] = ev3
                ffn_cast_ev.append(ev3)

    t_x0 = ssb("t_x0", [128, 4 * D], F32)
    t_sq0 = ssb("t_sq0", [128, D], F32)
    t_st0 = ssb("t_st0", [128, 32], F32)
    t_xnb = ssb("t_xnb", [128, 4 * D], BF16)
    ev_xn_w = []
    prev_norms = []
    prev_stores = [None] * 4
    prev_e3s = None
    prev_e5 = None
    assert NCH % 4 == 0
    for gx in range(NCH // 4):
        loads = [P.dma("sync", t_x0[:, j * D:(j + 1) * D],
                       d_x[(gx * 4 + j) * 128:(gx * 4 + j + 1) * 128, :],
                       deps=prev_norms) for j in range(4)]
        sq = []
        for j in range(4):
            sq.append(P.step("scalar", lambda e, j=j: e.activation(
                t_sq0[:, :], t_x0[:, j * D:(j + 1) * D], AF.Square,
                accum_out=t_st0[:, j:j + 1]),
                deps=[loads[j]] + ([prev_e3s] if prev_e3s and j == 0 else []) +
                     ([sq[j - 1]] if j > 0 else [])))
        rd = [P.step("vector", lambda e, j=j: e.reduce_sum(
            t_st0[:, 4 + j:5 + j], t_x0[:, j * D:(j + 1) * D], AXX),
            deps=[loads[j]]) for j in range(4)]
        e1 = P.step("vector", lambda e: e.tensor_scalar_mul(
            t_st0[:, 8:12], t_st0[:, 4:8], 1.0 / D), deps=rd)
        e2 = P.step("vector", lambda e: e.tensor_mul(
            t_st0[:, 12:16], t_st0[:, 8:12], t_st0[:, 8:12]), deps=[e1])
        e3s = P.step("vector", lambda e: e.tensor_scalar_mul(
            t_st0[:, 16:20], t_st0[:, 0:4], 1.0 / D), deps=sq)
        e4 = P.step("vector", lambda e: e.tensor_sub(
            t_st0[:, 20:24], t_st0[:, 16:20], t_st0[:, 12:16]),
            deps=[e2, e3s] + ([prev_e5] if prev_e5 else []))
        e5 = P.step("scalar", lambda e: e.activation(
            t_st0[:, 24:28], t_st0[:, 20:24], AF.Ln, bias=t_eps[:, :]),
            deps=[e4])
        e6 = P.step("scalar", lambda e: e.activation(
            t_st0[:, 28:32], t_st0[:, 24:28], AF.Exp, scale=-0.5),
            deps=[e5] + prev_norms)
        norms = []
        for j in range(4):
            evn = P.step("vector", lambda e, j=j: e.tensor_scalar(
                t_xnb[:, j * D:(j + 1) * D], t_x0[:, j * D:(j + 1) * D],
                t_st0[:, 8 + j:9 + j], t_st0[:, 28 + j:29 + j],
                ALU.subtract, ALU.mult),
                deps=[e6] + ([prev_stores[j]] if prev_stores[j] else []))
            norms.append(evn)
            prev_stores[j] = P.dma(
                "sync", s_xn[(gx * 4 + j) * 128:(gx * 4 + j + 1) * 128, :],
                t_xnb[:, j * D:(j + 1) * D], deps=[evn])
            ev_xn_w.append(prev_stores[j])
        prev_norms = norms
        prev_e3s = e3s
        prev_e5 = e5

    if STAGE <= 1:
        return _cut(P.fence_all())
    t_abz = ssb("t_abz", [64, PT_], F32)
    t_ab = ssb("t_ab", [64, PT_], F32)
    t_gth = ssb("t_gth", [128, D], F32)
    t_gsg = ssb("t_gsg", [128, D], F32)
    t_gout = ssb("t_gout", [128, D], BF16)

    xnfm_guard = []
    ev_ab_tile = [None] * NPT
    ev_gate_w = [None] * NCH
    prev_gate_g = None
    prev_abz = None
    for ti in range(NPT):
        deps = [ev_xn_w[ti * CPT + i] for i in range(CPT)] + xnfm_guard
        xn_evs = [P.dma("sync", t_xnfm[:, dc * PT_:(dc + 1) * PT_],
                        s_xn[ti * PT_:(ti + 1) * PT_, dc * 128:(dc + 1) * 128],
                        deps=deps, transpose=True) for dc in range(8)]
        xnfm_guard = []
        b0, bd = bank(1)
        mme = [P.step("tensor", lambda e, kc=kc, b0=b0: e.matmul(
            psf(b0, 0, 64, 0, PT_), t_wab[:, kc * 64:(kc + 1) * 64],
            t_xnfm[:, kc * PT_:(kc + 1) * PT_], start=(kc == 0), stop=(kc == 7)),
            deps=[xn_evs[kc], wev["wab"][kc]] + (bd if kc == 0 else []))
            for kc in range(8)]
        ev_th = P.step("scalar", lambda e, b0=b0: e.activation(
            t_abz[:, :], psf(b0, 0, 64, 0, PT_), AF.Tanh,
            bias=t_bab[:, :], scale=0.5),
            deps=mme + [ev_bab] + ([prev_abz] if prev_abz else []))
        bank_done(b0, 1, [ev_th])
        pga = [ev_ab_tile[ti - 1]] if ti >= 1 else []
        eva_v = P.step("vector", lambda e: e.tensor_scalar(
            t_ab[:, :], t_abz[:, :], 0.5, 0.5,
            ALU.mult, ALU.add), deps=[ev_th] + pga)
        prev_abz = eva_v
        ev_ab_tile[ti] = P.dma("sync", s_ab[ti, :, :], t_ab[:, :],
                               deps=[eva_v])
        xnfm_guard = [mme[-1]]
        for ci in range(CPT):
            cix = ti * CPT + ci
            b0, bd = bank(2)
            mme = []
            for kc in range(8):
                for nh in range(2):
                    mme.append(P.step(
                        "tensor",
                        lambda e, kc=kc, nh=nh, b0=b0, ci=ci: e.matmul(
                            psf(b0 + nh, 0, 128, 0, 512),
                            t_xnfm[:, kc * PT_ + ci * 128:kc * PT_ + (ci + 1) * 128],
                            t_wt["wgt"][:, kc * D + nh * 512:kc * D + (nh + 1) * 512],
                            start=(kc == 0), stop=(kc == 7)),
                        deps=[xn_evs[kc], wev["wgt"][kc]] + (bd if kc == 0 else [])))
            gps = ps_t[0:128, 512 * b0:512 * b0 + 1024]
            pg = [prev_gate_g] if prev_gate_g else []
            pg2 = [ev_gate_w[cix - 1]] if cix >= 1 else []
            ev_t = P.step("scalar", lambda e, gps=gps: e.activation(
                t_gth[:, :], gps, AF.Tanh, scale=0.5), deps=mme + pg)
            ev_sg = P.step("vector", lambda e: e.tensor_scalar(
                t_gsg[:, :], t_gth[:, :], 0.5, 0.5, ALU.mult, ALU.add),
                deps=[ev_t] + pg)
            ev_g = P.step("vector", lambda e, gps=gps: e.tensor_mul(
                t_gout[:, :], gps, t_gsg[:, :]), deps=[ev_sg] + pg2)
            prev_gate_g = ev_g
            bank_done(b0, 2, [ev_g])
            ev_gate_w[cix] = P.dma("sync", s_gate[cix * 128:(cix + 1) * 128, :],
                                   t_gout[:, :], deps=[ev_g])
            xnfm_guard = [mme[-1]]

    if STAGE <= 2:
        return _cut(P.fence_all())
    a1_fence = P.fence_all()
    sub_ctx.close()
    t_abl = asb("t_abl", [64, 128], F32)
    t_kkdbg = (asb("t_kkdbg", [128, 512], F32) if DEBUG_SCRATCH else None)
    t_la = asb("t_la", [16, 128], F32)
    t_la2 = asb("t_la2", [16, 128], F32)
    t_z16 = asb("t_z16", [16, 128], F32)
    t_c = asb("t_c", [16, 128], F32)
    t_chb = asb("t_chb", [16, 128], BF16)
    t_chf = asb("t_chf", [16, 128], F32)
    t_clb = asb("t_clb", [16, 128], BF16)
    t_cprev = asb("t_cprev", [16, 128], F32)
    t_dend = asb("t_dend", [16, 128], F32)
    t_ecp = asb("t_ecp", [16, 128], F32)
    t_ecbf = asb("t_ecbf", [16, 128], BF16)
    t_tm1 = asb("t_tm1", [128, 80], F32)
    t_becptm = asb("t_becptm", [128, 16], F32)
    t_cath = asb("t_cath", [1, 512], BF16)
    t_catl = asb("t_catl", [1, 512], BF16)
    t_dbuf = asb("t_dbuf", [128, 1024], BF16)
    t_ebuf = asb("t_ebuf", [128, 512], F32)
    t_e1m = asb("t_e1m", [128, 512], F32)
    t_e2m = asb("t_e2m", [128, 512], F32)
    t_xb = asb("t_xb", [128, 2048], BF16)
    t_xp = [asb(f"t_xp{i}", [128, 2048], BF16) for i in range(2)]
    t_xpt = [asb(f"t_xpt{i}", [128, 2048], BF16) for i in range(2)]
    t_xtb = t_xpt[1]
    t_zb = [asb(f"t_zb{i}", [128, 2048], BF16) for i in range(2)]
    t_ftb = asb("t_ftb", [64, 2048], BF16)
    t_kdb = asb("t_kdb", [128, 1024], BF16)
    t_rb = asb("t_rb", [128, 2048], BF16)
    t_ptb = asb("t_ptb", [128, 2048], BF16)
    t_qtb = asb("t_qtb", [128, 1024], BF16)
    t_gib = asb("t_gib", [64, 1024], F32)
    t_atb = asb("t_atb", [64, 1024], BF16)
    t_ub = asb("t_ub", [64, 1024], BF16)
    t_acc = [asb(f"t_acc{i}", [64, 1024], BF16) for i in range(2)]
    t_acct = [asb(f"t_acct{i}", [64, 1024], BF16) for i in range(2)]
    t_ucc = [asb(f"t_ucc{i}", [64, 1024], BF16) for i in range(2)]

    ev_acc0 = P.step("vector", lambda e: e.tensor_copy(
        t_acc[1][:, :], t_con["i64rep_bf"][:, :]),
        deps=[evc0] + list(a1_fence))
    ev_acct0 = P.step("vector", lambda e: e.tensor_copy(
        t_acct[1][:, :], t_con["i64rep_bf"][:, :]), deps=[evc0])
    ev_ucc0 = P.step("vector", lambda e: e.memset(t_ucc[1][:, :], 0.0))
    ev_z16 = P.step("vector", lambda e: e.memset(t_z16[:, :], 0.0))
    compose_prev = (ev_acc0, ev_acct0, ev_ucc0)

    prev_rn = None
    tile_fence = [()] * NPT
    chunk_fence = a1_fence
    ev_chunk_done = [None] * NCH
    ev_gamma = [None] * NCH
    ev_kfm_w = [None] * NPT
    ktm_guard = [()] * 2
    eacctF = euccF = None

    for ti in range(NPT):
        par = ti % 2
        qof = par * 8 * PT_
        deps = [ev_xn_w[ti * CPT + i] for i in range(CPT)] + xnfm_guard
        xn_evs = [P.dma("sync", t_xnfm[:, dc * PT_:(dc + 1) * PT_],
                        s_xn[ti * PT_:(ti + 1) * PT_, dc * 128:(dc + 1) * 128],
                        deps=deps, transpose=True) for dc in range(8)]
        guard_prev = list(tile_fence[ti - 2]) if ti >= 2 else []
        tile_readers = []

        qn_apply = kn_apply = None
        for tname in ("q", "k"):
            wtile = t_wt["wq"] if tname == "q" else t_wt["wk"]
            wevs = wev["wq"] if tname == "q" else wev["wk"]
            dstb = t_qn if tname == "q" else t_kn
            sq_evs = []
            qbf_evs = []
            for oc in range(8):
                b0, bd = bank(1)
                mme = [P.step("tensor",
                              lambda e, oc=oc, kc=kc, b0=b0, wtile=wtile: e.matmul(
                    psf(b0, 0, 128, 0, PT_),
                    wtile[:, kc * D + oc * 128:kc * D + (oc + 1) * 128],
                    t_xnfm[:, kc * PT_:(kc + 1) * PT_],
                    start=(kc == 0), stop=(kc == 7)),
                    deps=[xn_evs[kc], wevs[kc]] + (bd if kc == 0 else []))
                    for kc in range(8)]
                e1 = P.step("vector", lambda e, oc=oc, b0=b0: e.tensor_copy(
                    t_qbf[:, oc * PT_:(oc + 1) * PT_], psf(b0, 0, 128, 0, PT_)),
                    deps=mme)
                e2 = P.step("scalar", lambda e, oc=oc: e.activation(
                    t_sqb[:, oc * PT_:(oc + 1) * PT_],
                    t_qbf[:, oc * PT_:(oc + 1) * PT_],
                    AF.Square), deps=[e1])
                bank_done(b0, 1, [e1])
                sq_evs.append(e2)
                qbf_evs.append(e1)
            nss = 8 * PT_ // 512
            b0, bd = bank(nss, hold=True)
            ssm = [P.step("tensor", lambda e, i4=i4, b0=b0: e.matmul(
                psf(b0 + i4, 0, 2, 0, 512), t_con["headones_bf"][:, :],
                t_sqb[:, 512 * i4:512 * (i4 + 1)], start=True, stop=True),
                deps=sq_evs + (bd if i4 == 0 else []) + [evc0])
                for i4 in range(nss)]
            eh4s = []
            applies = []
            hlen = 4 * PT_
            for hf in range(2):
                ssps = ps_t[0:2, 512 * b0 + hf * hlen:512 * b0 + (hf + 1) * hlen]
                eh1 = P.step("vector", lambda e, ssps=ssps, hf=hf:
                             e.tensor_scalar_max(
                    t_rn[:, 0:hlen], ssps, 1e-24),
                    deps=ssm + (eh4s if hf else []) +
                         ([prev_rn] if prev_rn else []))
                eh2 = P.step("scalar", lambda e: e.activation(
                    t_rn2[:, 0:hlen], t_rn[:, 0:hlen], AF.Ln), deps=[eh1])
                eh3 = P.step("scalar", lambda e: e.activation(
                    t_rn[:, 0:hlen], t_rn2[:, 0:hlen], AF.Exp, scale=-0.5),
                    deps=[eh2, eh1])
                eh4 = P.step("vector", lambda e, hf=hf: e.tensor_copy(
                    t_rnb[:, 0:hlen], t_rn[:, 0:hlen]),
                    deps=[eh3] + (applies[-2:] if hf else []))
                eh4s.append(eh4)
                prev_rn = eh3
                for oc in range(hf * 4, hf * 4 + 4):
                    b2, bd2 = bank(1)
                    em = P.step("tensor", lambda e, oc=oc, b2=b2, hf=hf: e.matmul(
                        psf(b2, 0, 128, 0, PT_), t_con["sel2t_bf"][:, :],
                        t_rnb[:, (oc - hf * 4) * PT_:(oc - hf * 4 + 1) * PT_],
                        start=True, stop=True),
                        deps=[eh4] + bd2)
                    ea = P.step("vector",
                                lambda e, oc=oc, b2=b2, dstb=dstb, qof=qof:
                                e.tensor_mul(
                        dstb[:, qof + oc * PT_:qof + (oc + 1) * PT_],
                        t_qbf[:, oc * PT_:(oc + 1) * PT_],
                        psf(b2, 0, 128, 0, PT_)),
                        deps=[em, qbf_evs[oc]] + (guard_prev if oc == 0 else []))
                    bank_done(b2, 1, [ea])
                    applies.append(ea)
            bank_done(b0, nss, eh4s)
            if tname == "q":
                qn_apply = applies
            else:
                kn_apply = applies

        ev_kfm_w[ti] = P.dma(
            "sync",
            s_kfm.rearrange("(dc p) t -> p dc t", p=128)[
                :, :, ti * PT_:(ti + 1) * PT_],
            t_kn[:, qof:qof + 8 * PT_].rearrange("p (dc t) -> p dc t", dc=8),
            deps=kn_apply)

        v_evs = []
        for ci in range(CPT):
            b0, bd = bank(2)
            mme = []
            for kc in range(8):
                for nh in range(2):
                    mme.append(P.step(
                        "tensor",
                        lambda e, kc=kc, nh=nh, b0=b0, ci=ci: e.matmul(
                            psf(b0 + nh, 0, 128, 0, 512),
                            t_xnfm[:, kc * PT_ + ci * 128:kc * PT_ + (ci + 1) * 128],
                            t_wt["wv"][:, kc * D + nh * 512:kc * D + (nh + 1) * 512],
                            start=(kc == 0), stop=(kc == 7)),
                        deps=[xn_evs[kc], wev["wv"][kc]] + (bd if kc == 0 else [])))
            ev = P.step("vector", lambda e, b0=b0, par=par, ci=ci: e.tensor_copy(
                t_vb[:, (par * CPT + ci) * D:(par * CPT + ci + 1) * D],
                ps_t[0:128, 512 * b0:512 * b0 + 1024]),
                deps=mme + (guard_prev if ci == 0 else []))
            bank_done(b0, 2, [ev])
            v_evs.append(ev)
        xnfm_guard = [mme[-1]]

        for ci in range(CPT if SUB >= 2 else 0):
            cix = ti * CPT + ci
            ks = cix % 2
            P.begin_fence()
            cf = list(chunk_fence)
            ev_ktm = P.dma("sync", t_ktm[:, ks * D:(ks + 1) * D],
                           s_kfm[:, cix * 128:(cix + 1) * 128],
                           deps=[ev_kfm_w[ti]] + list(ktm_guard[ks]),
                           transpose=True)

            if SUB <= 2:
                chunk_fence = P.end_fence()
                continue
            labl = P.dma("sync", t_abl[:, :],
                         s_ab[ti, :, ci * 128:(ci + 1) * 128],
                         deps=[ev_ab_tile[ti]] + cf)
            e1 = P.step("vector", lambda e: e.tensor_scalar_max(
                t_la[:, :], t_abl[0:16, :], 1e-37), deps=[labl] + cf)
            e2 = P.step("scalar", lambda e: e.activation(
                t_la2[:, :], t_la[:, :], AF.Ln), deps=[e1] + cf)
            e3 = P.step("vector", lambda e: e.tensor_tensor_scan(
                t_c[:, :], t_la2[:, :], t_z16[:, :], 0.0, ALU.add, ALU.add),
                deps=[e2, ev_z16])
            e4 = P.step("vector", lambda e: e.tensor_sub(
                t_cprev[:, :], t_c[:, :], t_la2[:, :]), deps=[e3])
            e6 = P.step("scalar", lambda e: e.activation(
                t_dend[:, :], t_c[:, :], AF.Exp,
                bias=t_c[:, 127:128], scale=-1.0), deps=[e3])
            e7 = P.step("scalar", lambda e: e.activation(
                t_ecp[:, :], t_cprev[:, :], AF.Exp), deps=[e4])
            e9 = P.step("scalar", lambda e: e.activation(
                t_ecbf[:, :], t_c[:, :], AF.Exp), deps=[e3])

            b0, bd = bank(2)
            trs = []
            for gi_, (tin, pb) in enumerate([(t_c[:, :], 0),
                                             (t_abl[32:48, :], 32),
                                             (t_cprev[:, :], 0),
                                             (t_dend[:, :], 0),
                                             (t_ecp[:, :], 0)]):
                ident = t_con["eye_f32"][pb:pb + 16, pb:pb + 16]
                dep_t = {0: [e3], 1: [labl], 2: [e4], 3: [e6],
                         4: [e7]}[gi_]
                bno = 1 if pb == 32 else 0
                trs.append(P.step("tensor",
                                  lambda e, gi_=gi_, tin=tin, ident=ident,
                                  b0=b0, bno=bno:
                                  e.transpose(
                    psf(b0 + bno, 0, 128, gi_ * 16, 16), tin, ident),
                    deps=dep_t + (bd if gi_ == 0 else []) + [evc0] + cf))
            e11a = P.step("vector", lambda e, b0=b0: e.tensor_copy(
                t_tm1[:, 0:16], psf(b0, 0, 128, 0, 16)),
                deps=[trs[0]] + cf)
            e11b = P.step("vector", lambda e, b0=b0: e.tensor_copy(
                t_tm1[:, 16:32], psf(b0 + 1, 0, 128, 16, 16)),
                deps=[trs[1]] + cf)
            e11c = P.step("vector", lambda e, b0=b0: e.tensor_copy(
                t_tm1[:, 32:80], psf(b0, 0, 128, 32, 48)),
                deps=trs + cf)
            e11 = P.step("vector", lambda e: e.engine_nop(),
                         deps=[e11a, e11b, e11c])
            bank_done(b0, 2, [e11a, e11b, e11c])
            e8 = P.step("vector", lambda e: e.tensor_mul(
                t_becptm[:, :], t_tm1[:, 16:32], t_tm1[:, 64:80]),
                deps=[e11] + cf)

            if DEBUG_SCRATCH:
                P.dma("sync", s_tmdbg[cix, :, :], t_tm1[:, :],
                      deps=[e11a, e11b, e11c, e8])
            ech = P.step("vector", lambda e: e.tensor_copy(
                t_chb[:, :], t_c[:, :]), deps=[e3] + cf)
            echf = P.step("vector", lambda e: e.tensor_copy(
                t_chf[:, :], t_chb[:, :]), deps=[ech])
            ecl = P.step("vector", lambda e: e.tensor_sub(
                t_clb[:, :], t_c[:, :], t_chf[:, :]), deps=[echf])
            e14a = P.dma("sync", s_cath[cix, :], t_chb[:, :], deps=[ech])
            e14b = P.dma("sync", s_catl[cix, :], t_clb[:, :], deps=[ecl])

            b0, bd = bank(1)
            gme = []
            for h in range(16):
                gme.append(P.step("tensor", lambda e, h=h, b0=b0: e.matmul(
                    ps_t[0:64, 512 * b0 + h:512 * b0 + h + 1],
                    t_con["selh_bf"][:, h * 64:(h + 1) * 64],
                    t_chb[:, 127:128], start=True, stop=False),
                    deps=[ech, evc0] + (bd if h == 0 else []) + cf))
                gme.append(P.step("tensor", lambda e, h=h, b0=b0: e.matmul(
                    ps_t[0:64, 512 * b0 + h:512 * b0 + h + 1],
                    t_con["selh_bf"][:, h * 64:(h + 1) * 64],
                    t_clb[:, 127:128], start=False, stop=True),
                    deps=[ecl]))
            ev_gamma[cix] = P.step("scalar", lambda e, b0=b0, cix=cix:
                                   e.activation(
                t_gcols[:, cix * 16:(cix + 1) * 16],
                ps_t[0:64, 512 * b0:512 * b0 + 16], AF.Exp), deps=gme)
            bank_done(b0, 1, [ev_gamma[cix]])

            if SUB <= 3:
                chunk_fence = P.end_fence()
                continue
            kd3 = t_kdb[:, :].rearrange("p (h e) -> p h e", h=16)
            ktm3 = t_ktm[:, ks * D:(ks + 1) * D].rearrange(
                "p (h e) -> p h e", h=16)
            dend_b = t_tm1[:, 48:64].unsqueeze(2).broadcast_to([128, 16, 64])
            e16 = P.step("vector", lambda e, kd3=kd3, ktm3=ktm3, dend_b=dend_b:
                         e.tensor_tensor(kd3, ktm3, dend_b, ALU.mult),
                         deps=[e11, ev_ktm] + cf)
            r4 = t_rb[:, :].rearrange("p (h t e) -> p h t e", h=16, t=2)
            v3 = t_vb[:, (par * CPT + ci) * D:(par * CPT + ci + 1) * D].rearrange(
                "p (h e) -> p h e", h=16)
            b_b = t_tm1[:, 16:32].unsqueeze(2).broadcast_to([128, 16, 64])
            becp_b = t_becptm[:, :].unsqueeze(2).broadcast_to([128, 16, 64])
            e17 = P.step("vector", lambda e, r4=r4, v3=v3, b_b=b_b:
                         e.tensor_tensor(r4[:, :, 0, :], v3, b_b, ALU.mult),
                         deps=[e11, v_evs[ci]] + cf)
            e18 = P.step("vector", lambda e, r4=r4, ktm3=ktm3, becp_b=becp_b:
                         e.tensor_tensor(r4[:, :, 1, :], ktm3, becp_b, ALU.mult),
                         deps=[e8, ev_ktm] + cf)
            tile_readers += [e16, e17, e18]

            x_evs = [None] * 16
            pt_evs = []
            prev_ecb = None
            prev_ee3 = None
            prev_ee4 = None
            prev_xg_gp = []
            prev_ptg_gp = []
            for g in range(4):
                e15a = P.dma("sync", t_cath[:, :],
                             s_cath[cix:cix + 1, g * 512:(g + 1) * 512],
                             deps=[e14a] + cf + ([prev_ecb] if prev_ecb else []))
                e15b = P.dma("sync", t_catl[:, :],
                             s_catl[cix:cix + 1, g * 512:(g + 1) * 512],
                             deps=[e14b] + cf + ([prev_ecb] if prev_ecb else []))
                bcb, bdc = bank(1)
                ecb1 = P.step("tensor", lambda e, g=g, bcb=bcb: e.matmul(
                    psf(bcb, 0, 128, 0, 512), t_con["ones1_bf"][:, :],
                    t_cath[:, :], start=True, stop=False),
                    deps=[e15a, evc0] + bdc + cf)
                ecb = P.step("tensor", lambda e, g=g, bcb=bcb: e.matmul(
                    psf(bcb, 0, 128, 0, 512), t_con["ones1_bf"][:, :],
                    t_catl[:, :], start=False, stop=True),
                    deps=[e15b, ecb1])
                prev_ecb = ecb
                d_evs = [P.step("vector", lambda e, hh=hh, h=g * 4 + hh, bcb=bcb:
                                e.tensor_scalar(
                    t_dbuf[:, hh * 128:(hh + 1) * 128],
                    psf(bcb, 0, 128, hh * 128, 128),
                    t_tm1[:, 32 + h:33 + h], 0.0, ALU.subtract, ALU.max),
                    deps=[ecb, e11] + ([prev_ee3] if prev_ee3 else []))
                    for hh in range(4)]
                if SUB <= 32:
                    continue
                ee1 = P.step("scalar", lambda e: e.activation(
                    t_ebuf[:, :], t_dbuf[:, 0:512], AF.Exp, scale=-1.0),
                    deps=d_evs + ([prev_ee4] if prev_ee4 else []))
                nl_b = t_con["neglmask_bf"][:, :].unsqueeze(1).broadcast_to(
                    [128, 4, 128])
                e3d = t_e1m[:, :].rearrange("p (hh t) -> p hh t", hh=4)
                eb3 = t_ebuf[:, :].rearrange("p (hh t) -> p hh t", hh=4)
                ee2 = P.step("vector", lambda e, e3d=e3d, eb3=eb3, nl_b=nl_b:
                             e.tensor_tensor(e3d, eb3, nl_b, ALU.mult),
                             deps=[ee1, evc0] + prev_xg_gp)
                d2_evs = [P.step("vector", lambda e, hh=hh, h=g * 4 + hh, bcb=bcb:
                                 e.tensor_scalar(
                    t_dbuf[:, hh * 128:(hh + 1) * 128],
                    psf(bcb, 0, 128, hh * 128, 128),
                    t_tm1[:, 0 + h:1 + h], 0.0, ALU.subtract, ALU.min),
                    deps=[ecb, e11, ee1]) for hh in range(4)]
                ee3 = P.step("scalar", lambda e: e.activation(
                    t_ebuf[:, :], t_dbuf[:, 0:512], AF.Exp),
                    deps=d2_evs + [ee2])
                bank_done(bcb, 1, [ee3])
                um_b = t_con["umask_bf"][:, :].unsqueeze(1).broadcast_to(
                    [128, 4, 128])
                e4d = t_e2m[:, :].rearrange("p (hh t) -> p hh t", hh=4)
                ee4 = P.step("vector", lambda e, e4d=e4d, eb3=eb3, um_b=um_b:
                             e.tensor_tensor(e4d, eb3, um_b, ALU.mult),
                             deps=[ee3] + prev_ptg_gp)
                prev_ee3 = ee3
                prev_ee4 = ee4

                if SUB <= 33:
                    continue
                if DEBUG_SCRATCH:
                    P.dma("sync", s_ddbg[cix, g, :, :].bitcast(BF16)[:, 0:512],
                          t_dbuf[:, 0:512], deps=d2_evs)
                    P.dma("sync", s_emdbg[cix, g, :, :], t_e1m[:, :],
                          deps=[ee2])
                bkk, bdk = bank(2)
                kk_evs = []
                for hh in range(4):
                    h = g * 4 + hh
                    pb = 64 * (h % 2)
                    dcq = h // 2
                    sl_k = t_kn[pb:pb + 64, qof + dcq * PT_ + ci * 128:
                                qof + dcq * PT_ + (ci + 1) * 128]
                    kk_evs.append(P.step("tensor",
                                         lambda e, hh=hh, bkk=bkk, sl_k=sl_k:
                                         e.matmul(
                        psf(bkk + hh % 2, 0, 128, (hh // 2) * 128, 128),
                        sl_k, sl_k, start=True, stop=True),
                        deps=kn_apply + (bdk if hh == 0 else [])))
                if SUB <= 335:
                    bank_done(bkk, 2, kk_evs)
                    continue
                if DEBUG_SCRATCH:
                    ekkd = P.step("vector", lambda e, bkk=bkk: e.tensor_copy(
                        t_kkdbg[:, :],
                        ps_t[0:128, 512 * bkk:512 * bkk + 512]),
                        deps=kk_evs)
                    edma1 = P.dma("sync", s_kkdbg[cix, g, :, :], t_kkdbg[:, :],
                                  deps=[ekkd])
                xg = []
                dbuf_bf = t_dbuf[:, 512:1024]
                for hh in range(4):
                    h = g * 4 + hh
                    if SUB == 336:
                        exg = P.step("vector", lambda e, hh=hh, h=h, bkk=bkk:
                                     e.tensor_copy(
                            t_xb[:, h * 128:(h + 1) * 128],
                            psf(bkk + hh % 2, 0, 128, (hh // 2) * 128, 128)),
                            deps=kk_evs + [ee2, e11])
                    elif SUB == 337:
                        exg = P.step("vector", lambda e, hh=hh, h=h, bkk=bkk:
                                     e.scalar_tensor_tensor(
                            t_xb[:, h * 128:(h + 1) * 128],
                            psf(bkk + hh % 2, 0, 128, (hh // 2) * 128, 128),
                            1.0, t_e1m[:, hh * 128:(hh + 1) * 128],
                            ALU.mult, ALU.mult),
                            deps=kk_evs + [ee2, e11])
                    elif SUB == 338:
                        exg = P.step("vector", lambda e, hh=hh, h=h, bkk=bkk:
                                     e.tensor_scalar(
                            t_xb[:, h * 128:(h + 1) * 128],
                            psf(bkk + hh % 2, 0, 128, (hh // 2) * 128, 128),
                            t_tm1[:, 16 + h:17 + h], None, ALU.mult),
                            deps=kk_evs + [ee2, e11])
                    else:
                        ex1 = P.step("vector",
                                     lambda e, hh=hh, h=h, bkk=bkk,
                                     dbuf_bf=dbuf_bf: e.tensor_scalar(
                            dbuf_bf[:, hh * 128:(hh + 1) * 128],
                            psf(bkk + hh % 2, 0, 128, (hh // 2) * 128, 128),
                            t_tm1[:, 16 + h:17 + h], None, ALU.mult),
                            deps=kk_evs + [ee3, e11])
                        exg = P.step("gpsimd",
                                     lambda e, hh=hh, h=h, dbuf_bf=dbuf_bf:
                                     e.tensor_tensor(
                            t_xb[:, h * 128:(h + 1) * 128],
                            dbuf_bf[:, hh * 128:(hh + 1) * 128],
                            t_e1m[:, hh * 128:(hh + 1) * 128], ALU.mult),
                            deps=[ex1, ee2])
                    xg.append(exg)
                    x_evs[h] = exg
                if DEBUG_SCRATCH:
                    ekkd2 = P.step("vector", lambda e, bkk=bkk: e.tensor_copy(
                        t_kkdbg[:, :],
                        ps_t[0:128, 512 * bkk:512 * bkk + 512]),
                        deps=xg + [edma1])
                    edma2 = P.dma("sync", s_kkdbg2[cix, g, :, :], t_kkdbg[:, :],
                                  deps=[ekkd2])
                    xg = xg + [ekkd2, edma2]
                bank_done(bkk, 2, xg)
                prev_xg_gp = [xg[-1]]
                tile_readers.append(kk_evs[-1])

                if SUB <= 34:
                    continue
                bkq, bdq = bank(2)
                kq_evs = []
                for hh in range(4):
                    h = g * 4 + hh
                    pb = 64 * (h % 2)
                    dcq = h // 2
                    sl_k = t_kn[pb:pb + 64, qof + dcq * PT_ + ci * 128:
                                qof + dcq * PT_ + (ci + 1) * 128]
                    sl_q = t_qn[pb:pb + 64, qof + dcq * PT_ + ci * 128:
                                qof + dcq * PT_ + (ci + 1) * 128]
                    kq_evs.append(P.step(
                        "tensor",
                        lambda e, hh=hh, bkq=bkq, sl_k=sl_k, sl_q=sl_q: e.matmul(
                            psf(bkq + hh % 2, 0, 128, (hh // 2) * 128, 128),
                            sl_k, sl_q, start=True, stop=True),
                        deps=kn_apply + qn_apply + (bdq if hh == 0 else [])))
                ptg = []
                for hh in range(4):
                    h = g * 4 + hh
                    pp1 = P.step("vector",
                                 lambda e, hh=hh, bkq=bkq, dbuf_bf=dbuf_bf:
                                 e.tensor_copy(
                        dbuf_bf[:, hh * 128:(hh + 1) * 128],
                        psf(bkq + hh % 2, 0, 128, (hh // 2) * 128, 128)),
                        deps=kq_evs + xg)
                    ptg.append(P.step("gpsimd",
                                      lambda e, hh=hh, h=h, dbuf_bf=dbuf_bf:
                                      e.tensor_tensor(
                        t_ptb[:, h * 128:(h + 1) * 128],
                        dbuf_bf[:, hh * 128:(hh + 1) * 128],
                        t_e2m[:, hh * 128:(hh + 1) * 128], ALU.mult),
                        deps=[pp1, ee4]))
                pt_evs += ptg
                prev_ptg_gp = [ptg[-1]]
                bank_done(bkq, 2, ptg)
                tile_readers.append(kq_evs[-1])

            if SUB <= 40:
                chunk_fence = P.end_fence()
                continue
            if DEBUG_SCRATCH:
                P.dma("sync", s_xdbg[cix, :, :], t_xb[:, :],
                      deps=[e for e in x_evs])
            xt_evs = []
            for half in range(2):
                tevs = [P.step("tensor", lambda e, h=half * 8 + hh, hh=hh:
                               e.transpose(
                    psb_t[0:128, hh * 128:(hh + 1) * 128],
                    t_xb[:, h * 128:(h + 1) * 128], t_con["eye_bf"][:, :]),
                    deps=[x_evs[half * 8 + hh], evc0] +
                         (list(psb_free[0]) if hh == 0 else []))
                    for hh in range(8)]
                exv = P.step("vector", lambda e, half=half: e.tensor_copy(
                    t_xtb[:, half * 1024:(half + 1) * 1024],
                    psb_t[0:128, 0:1024]), deps=tevs)
                psb_free[0] = (exv,)
                xt_evs.append(exv)

            zpar = 0
            zev = [None, None]
            xsrc, xtsrc = t_xb, t_xtb
            xsrc_ev = [ev for ev in x_evs]
            xtsrc_ev = xt_evs
            asrc_first = [e17, e18]
            for lev in range(NL):
                asrc = t_rb if lev == 0 else t_zb[1 - zpar]
                asrc_ev = asrc_first if lev == 0 else [zev[1 - zpar]]
                for half in range(2):
                    ba, bda = bank(2)
                    mms = [P.step(
                        "tensor",
                        lambda e, h=half * 8 + hh, hh=hh, ba=ba, asrc=asrc,
                        xtsrc=xtsrc: e.matmul(
                            ps_t[0:128,
                                 512 * ba + hh * 128:512 * ba + (hh + 1) * 128],
                            xtsrc[:, h * 128:(h + 1) * 128],
                            asrc[:, h * 128:(h + 1) * 128],
                            start=True, stop=True),
                        deps=(xtsrc_ev if hh == 0 else []) +
                             (asrc_ev if hh == 0 else []) +
                             (bda if hh == 0 else []))
                        for hh in range(8)]
                    eva = P.step("vector",
                                 lambda e, half=half, ba=ba, asrc=asrc, zp=zpar:
                                 e.scalar_tensor_tensor(
                        t_zb[zp][:, half * 1024:(half + 1) * 1024],
                        ps_t[0:128, 512 * ba:512 * ba + 1024], 1.0,
                        asrc[:, half * 1024:(half + 1) * 1024],
                        ALU.mult, ALU.add), deps=mms + asrc_ev +
                        ([zev[zpar]] if zev[zpar] else []))
                    bank_done(ba, 2, [eva])
                    zev[zpar] = eva
                zpar = 1 - zpar

                if lev < NL - 1:
                    xdst = t_xp[lev % 2]
                    xtdst = t_xpt[lev % 2]
                    new_x_ev = []
                    new_xt_ev = []
                    for which in range(2):
                        lop = xtsrc if which == 0 else xsrc
                        rop = xsrc if which == 0 else xtsrc
                        dst = xdst if which == 0 else xtdst
                        for half in range(2):
                            bs, bds = bank(2)
                            mms = [P.step(
                                "tensor",
                                lambda e, h=half * 8 + hh, hh=hh, bs=bs,
                                lop=lop, rop=rop: e.matmul(
                                    ps_t[0:128, 512 * bs + hh * 128:
                                         512 * bs + (hh + 1) * 128],
                                    lop[:, h * 128:(h + 1) * 128],
                                    rop[:, h * 128:(h + 1) * 128],
                                    start=True, stop=True),
                                deps=((xsrc_ev + xtsrc_ev) if hh == 0 else []) +
                                     (bds if hh == 0 else []))
                                for hh in range(8)]
                            if which == 0:
                                ev2 = P.step("vector",
                                             lambda e, half=half, bs=bs, dst=dst:
                                             e.tensor_copy(
                                    dst[:, half * 1024:(half + 1) * 1024],
                                    ps_t[0:128, 512 * bs:512 * bs + 1024]),
                                    deps=mms)
                                new_x_ev.append(ev2)
                            else:
                                ev2 = P.step("scalar",
                                             lambda e, half=half, bs=bs, dst=dst:
                                             e.copy(
                                    dst[:, half * 1024:(half + 1) * 1024],
                                    ps_t[0:128, 512 * bs:512 * bs + 1024]),
                                    deps=mms)
                                new_xt_ev.append(ev2)
                            bank_done(bs, 2, [ev2])
                    xsrc, xtsrc = xdst, xtdst
                    xsrc_ev, xtsrc_ev = new_x_ev, new_xt_ev
            zfin = t_zb[1 - zpar]
            zfin_ev = zev[1 - zpar]

            if SUB <= 5:
                chunk_fence = P.end_fence()
                continue
            ft_half = []
            for half in range(2):
                ftevs = [P.step("tensor", lambda e, h=half * 8 + hh, hh=hh,
                                zfin=zfin: e.transpose(
                    psb_t[0:64, hh * 128:(hh + 1) * 128],
                    zfin[:, h * 128 + 64:(h + 1) * 128], t_con["eye_bf"][:, :]),
                    deps=[zfin_ev, evc0] +
                         (list(psb_free[0]) if hh == 0 else []))
                    for hh in range(8)]
                efth = P.step("vector", lambda e, half=half: e.tensor_copy(
                    t_ftb[:, half * 1024:(half + 1) * 1024],
                    psb_t[0:64, 0:1024]), deps=ftevs)
                psb_free[0] = (efth,)
                ft_half.append(efth)
            eft = ft_half[-1]

            bq, bdq2 = bank(2)
            sel_evs = [P.step("tensor", lambda e, j=j, bq=bq: e.matmul(
                psf(bq, 0, 128, 0, 1024)[0:128, j * 128:(j + 1) * 128]
                if False else
                ps_t[0:128, 512 * bq + j * 128:512 * bq + (j + 1) * 128],
                t_con["selj_bf"][:, j * 128:(j + 1) * 128],
                t_ecbf[:, :], start=True, stop=True),
                deps=[e9, evc0] + (bdq2 if j == 0 else []))
                for j in range(8)]
            qn3 = t_qn[:, qof:qof + 8 * PT_].rearrange(
                "p (dc t) -> p dc t", dc=8)[:, :, ci * 128:(ci + 1) * 128]
            eqt = P.step("vector", lambda e, bq=bq, qn3=qn3: e.tensor_tensor(
                t_qtb[:, :].rearrange("p (dc t) -> p dc t", dc=8), qn3,
                ps_t[0:128, 512 * bq:512 * bq + 1024].rearrange(
                    "p (dc t) -> p dc t", dc=8), ALU.mult),
                deps=sel_evs + qn_apply)
            bank_done(bq, 2, [eqt])
            tile_readers.append(eqt)

            if SUB <= 6:
                chunk_fence = P.end_fence()
                continue
            bgg, bdg = bank(2)
            g_evs = [P.step("tensor", lambda e, h=h, bgg=bgg, zfin=zfin:
                            e.matmul(
                ps_t[0:64, 512 * bgg + h * 64:512 * bgg + (h + 1) * 64],
                zfin[:, h * 128 + 64:(h + 1) * 128],
                t_kdb[:, h * 64:(h + 1) * 64], start=True, stop=True),
                deps=[zfin_ev, e16] + (bdg if h == 0 else []))
                for h in range(16)]
            i64b = t_con["i64_f32"][:, :].unsqueeze(1).broadcast_to([64, 16, 64])
            gcb = t_gcols[:, cix * 16:(cix + 1) * 16].unsqueeze(2).broadcast_to(
                [64, 16, 64])
            egi = P.step("vector", lambda e, i64b=i64b, gcb=gcb: e.tensor_tensor(
                t_gib[:, :].rearrange("p (h e) -> p h e", h=16), i64b, gcb,
                ALU.mult), deps=[ev_gamma[cix], evc0] + cf)
            eat = P.step("vector", lambda e, bgg=bgg: e.scalar_tensor_tensor(
                t_atb[:, :], ps_t[0:64, 512 * bgg:512 * bgg + 1024], -1.0,
                t_gib[:, :], ALU.mult, ALU.add), deps=g_evs + [egi])
            bank_done(bgg, 2, [eat])
            bgu, bdu = bank(2)
            u_evs = [P.step("tensor", lambda e, h=h, bgu=bgu, zfin=zfin:
                            e.matmul(
                ps_t[0:64, 512 * bgu + h * 64:512 * bgu + (h + 1) * 64],
                t_kdb[:, h * 64:(h + 1) * 64],
                zfin[:, h * 128:h * 128 + 64], start=True, stop=True),
                deps=[zfin_ev, e16] + (bdu if h == 0 else []))
                for h in range(16)]
            eu = P.step("vector", lambda e, bgu=bgu: e.tensor_copy(
                t_ub[:, :], ps_t[0:64, 512 * bgu:512 * bgu + 1024]), deps=u_evs)
            bank_done(bgu, 2, [eu])

            cp = cix % 2
            cq = 1 - cp
            pa, pat, pu = compose_prev
            last = (cix == NCH - 1)
            bc1, bdc1 = bank(2)
            acc_m = [P.step("tensor", lambda e, h=h, bc1=bc1, cq=cq: e.matmul(
                ps_t[0:64, 512 * bc1 + h * 64:512 * bc1 + (h + 1) * 64],
                t_atb[:, h * 64:(h + 1) * 64],
                t_acc[cq][:, h * 64:(h + 1) * 64], start=True, stop=True),
                deps=[eat, pa] + (bdc1 if h == 0 else []))
                for h in range(16)]
            eacc = P.step("vector", lambda e, bc1=bc1, cp=cp: e.tensor_copy(
                t_acc[cp][:, :], ps_t[0:64, 512 * bc1:512 * bc1 + 1024]),
                deps=acc_m + [pa])
            bank_done(bc1, 2, [eacc])
            bc2, bdc2 = bank(2)
            acct_m = [P.step("tensor", lambda e, h=h, bc2=bc2, cq=cq: e.matmul(
                ps_t[0:64, 512 * bc2 + h * 64:512 * bc2 + (h + 1) * 64],
                t_acc[cq][:, h * 64:(h + 1) * 64],
                t_atb[:, h * 64:(h + 1) * 64], start=True, stop=True),
                deps=[eat, pa, pat] + (bdc2 if h == 0 else []))
                for h in range(16)]
            eacct = P.step("vector", lambda e, bc2=bc2, cp=cp: e.tensor_copy(
                t_acct[cp][:, :], ps_t[0:64, 512 * bc2:512 * bc2 + 1024]),
                deps=acct_m + [pat])
            evl2 = [eacct]
            if last:
                eacctF = P.step("vector", lambda e, bc2=bc2: e.tensor_copy(
                    t_accF[:, :], ps_t[0:64, 512 * bc2:512 * bc2 + 1024]),
                    deps=acct_m)
                evl2.append(eacctF)
            bank_done(bc2, 2, evl2)
            bc3, bdc3 = bank(2)
            ucc_m = []
            for h in range(16):
                ucc_m.append(P.step("tensor",
                                    lambda e, h=h, bc3=bc3, cq=cq: e.matmul(
                    ps_t[0:64, 512 * bc3 + h * 64:512 * bc3 + (h + 1) * 64],
                    t_atb[:, h * 64:(h + 1) * 64],
                    t_ucc[cq][:, h * 64:(h + 1) * 64], start=True, stop=False),
                    deps=[eat, pu] + (bdc3 if h == 0 else [])))
                ucc_m.append(P.step("tensor", lambda e, h=h, bc3=bc3: e.matmul(
                    ps_t[0:64, 512 * bc3 + h * 64:512 * bc3 + (h + 1) * 64],
                    t_con["i64rep_bf"][:, 0:64],
                    t_ub[:, h * 64:(h + 1) * 64], start=False, stop=True),
                    deps=[eu]))
            eucc = P.step("vector", lambda e, bc3=bc3, cp=cp: e.tensor_copy(
                t_ucc[cp][:, :], ps_t[0:64, 512 * bc3:512 * bc3 + 1024]),
                deps=ucc_m + [pu])
            evl3 = [eucc]
            if last:
                euccF = P.step("vector", lambda e, bc3=bc3: e.tensor_copy(
                    t_uccF[:, :], ps_t[0:64, 512 * bc3:512 * bc3 + 1024]),
                    deps=ucc_m)
                evl3.append(euccF)
            bank_done(bc3, 2, evl3)
            compose_prev = (eacc, eacct, eucc)

            w1 = P.dma("sync", s_z[cix, :, :], zfin[:, :], deps=[zfin_ev])
            w2 = P.dma("sync", s_ft[cix, :, :], t_ftb[:, :], deps=[eft])
            w3 = P.dma("sync", s_kd[cix, :, :], t_kdb[:, :], deps=[e16])
            w4 = P.dma("sync", s_pt[cix, :, :], t_ptb[:, :], deps=pt_evs)
            w5 = P.dma("sync",
                       s_qt[cix].rearrange("j p t -> p j t"),
                       t_qtb[:, :].rearrange("p (j t) -> p j t", j=8),
                       deps=[eqt])
            ev_chunk_done[cix] = (w1, w2, w3, w4, w5)
            chunk_fence = P.end_fence()
            ktm_guard[ks] = (e16, e18, w3)
        tile_fence[ti] = tuple(tile_readers)

    if STAGE <= 3 or SUB < 99:
        return _cut(P.fence_all())
    a_fence = P.fence_all()
    a_ctx.close()

    gb_ctx = ExitStack()
    t_s0 = gb_ctx.enter_context(nc.sbuf_tensor("t_s0", [64, 1024], F32))
    t_s0b = gb_ctx.enter_context(nc.sbuf_tensor("t_s0b", [64, 1024], BF16))
    g_ctx = ExitStack()
    t_gat = g_ctx.enter_context(nc.sbuf_tensor("t_gat", [64, 4096], F32))
    t_gau = g_ctx.enter_context(nc.sbuf_tensor("t_gau", [64, 4096], F32))
    t_sst = g_ctx.enter_context(nc.sbuf_tensor("t_sst", [64, 16 * 256], F32))
    t_selv = g_ctx.enter_context(nc.sbuf_tensor("t_selv", [1, 4], F32))
    t_sel64 = g_ctx.enter_context(nc.sbuf_tensor("t_sel64", [64, 4], F32))

    agi_at = s_agi.rearrange("(h ty r) e -> h ty r e", h=16, ty=2)
    ew1 = P.dma("sync", agi_at[:, 0, :, :].rearrange("h r e -> r h e"),
                t_accF[:, :].rearrange("r (h e) -> r h e", h=16),
                deps=[eacctF] + list(a_fence))
    ew2 = P.dma("sync", agi_at[:, 1, :, :].rearrange("h r e -> r h e"),
                t_uccF[:, :].rearrange("r (h e) -> r h e", h=16),
                deps=[euccF])
    ecc = P.step("gpsimd", lambda e: e.collective_compute(
        "AllGather", ALU.bypass,
        replica_groups=[[0, 1, 2, 3], [4, 5, 6, 7]],
        ins=[s_agi.opt()], outs=[s_ago.opt()]), deps=[ew1, ew2],
        semkey="scc")
    ago5 = s_ago.rearrange("(q h ty r) e -> q h ty r e", q=4, h=16, ty=2)
    gl_evs = []
    for q in range(4):
        gl_evs.append(P.dma(
            "sync",
            t_gat[:, q * 1024:(q + 1) * 1024].rearrange(
                "r (h e) -> r h e", h=16),
            ago5[q, :, 0, :, :].rearrange("h r e -> r h e"), deps=[ecc]))
        gl_evs.append(P.dma(
            "sync",
            t_gau[:, q * 1024:(q + 1) * 1024].rearrange(
                "r (h e) -> r h e", h=16),
            ago5[q, :, 1, :, :].rearrange("h r e -> r h e"), deps=[ecc]))
    ev_selv = P.dma("sync", t_selv[:, :], d_selv)
    bsl, bdsl = bank(1)
    esm = P.step("tensor", lambda e, bsl=bsl: e.matmul(
        ps_t[0:64, 512 * bsl:512 * bsl + 4], t_con["ones1"][0:1, 0:64],
        t_selv[:, :], start=True, stop=True), deps=[ev_selv, evc0] + bdsl)
    esel = P.step("vector", lambda e, bsl=bsl: e.tensor_copy(
        t_sel64[:, :], ps_t[0:64, 512 * bsl:512 * bsl + 4]), deps=[esm])
    bank_done(bsl, 1, [esel])

    sst5 = t_sst[:, :].rearrange("p (h e q) -> p h e q", h=16, q=4)
    bstage, bdst = bank(4, hold=True)
    ev_s0 = []
    prev_copy = None
    for h in range(16):
        prev_ev = prev_copy
        for q in range(4):
            pslot = ps_t[0:64, 512 * bstage + h * 64:512 * bstage + (h + 1) * 64]
            if q == 0:
                m1 = P.step("tensor", lambda e, h=h, pslot=pslot: e.matmul(
                    pslot, t_con["i64_f32"][:, :],
                    t_gau[:, h * 64:(h + 1) * 64],
                    start=True, stop=True),
                    deps=gl_evs + bdst + [evc0] +
                         ([prev_ev] if prev_ev else []))
            else:
                sst_prev = sst5[:, h, :, q - 1]
                m0 = P.step("tensor",
                            lambda e, h=h, q=q, pslot=pslot, sst_prev=sst_prev:
                            e.matmul(
                    pslot, t_gat[:, q * 1024 + h * 64:q * 1024 + (h + 1) * 64],
                    sst_prev, start=True, stop=False),
                    deps=gl_evs + [prev_ev])
                m1 = P.step("tensor", lambda e, h=h, q=q, pslot=pslot: e.matmul(
                    pslot, t_con["i64_f32"][:, :],
                    t_gau[:, q * 1024 + h * 64:q * 1024 + (h + 1) * 64],
                    start=False, stop=True), deps=[])
            prev_ev = P.step("vector", lambda e, pslot=pslot, h=h, q=q:
                             e.tensor_copy(sst5[:, h, :, q], pslot), deps=[m1])
        prev_copy = prev_ev
        ev_s0.append(prev_ev)
    bank_done(bstage, 4, ev_s0)
    sst4 = t_sst[:, :].rearrange("p (he q) -> p he q", q=4)
    sel_b = t_sel64[:, :].unsqueeze(1).broadcast_to([64, 1024, 4])
    emask = P.step("vector", lambda e, sst4=sst4, sel_b=sel_b: e.tensor_tensor(
        sst4, sst4, sel_b, ALU.mult), deps=ev_s0 + [esel])
    ered = P.step("vector", lambda e, sst4=sst4: e.tensor_reduce(
        t_s0[:, :].unsqueeze(2), sst4, AXX, ALU.add), deps=[emask])
    ev_s0f = P.step("vector", lambda e: e.tensor_copy(t_s0b[:, :], t_s0[:, :]),
                    deps=[ered])
    if STAGE <= 4:
        return _cut(P.fence_all())
    g_fence = P.fence_all()
    g_ctx.close()

    b_ctx = ExitStack()

    def bsb(name, shape, dtype):
        return b_ctx.enter_context(nc.sbuf_tensor(name, shape, dtype))

    t_wo = bsb("t_wo", [128, 8 * D], BF16)
    t_wld2 = bsb("t_wld2", [128, D], F32)
    t_zl = [bsb(f"t_zl{i}", [128, 2048], BF16) for i in range(2)]
    t_ftl = [bsb(f"t_ftl{i}", [64, 2048], BF16) for i in range(2)]
    t_kdl = [bsb(f"t_kdl{i}", [128, 1024], BF16) for i in range(2)]
    t_ptl = [bsb(f"t_ptl{i}", [128, 2048], BF16) for i in range(2)]
    t_qtl = [bsb(f"t_qtl{i}", [64, 2048], BF16) for i in range(2)]
    t_gl = [bsb(f"t_gl{i}", [128, D], BF16) for i in range(2)]
    t_wh = bsb("t_wh", [128, 1024], BF16)
    t_sg = bsb("t_sg", [64, 1024], BF16)
    t_sf = bsb("t_sf", [64, 1024], F32)
    t_sb2 = bsb("t_sb2", [64, 1024], BF16)
    t_ytm = bsb("t_ytm", [128, D], F32)
    t_ysq = bsb("t_ysq", [128, D], F32)
    t_yst = bsb("t_yst", [128, 4], F32)
    t_yout = bsb("t_yout", [128, D], BF16)

    woev = []
    wo_par = None
    for kc in range(8):
        ev1 = P.dma("sync", t_wld2[:, :], d_w["wo"][kc * 128:(kc + 1) * 128, :],
                    deps=[wo_par] if wo_par else list(g_fence))
        evcp = P.step("scalar", lambda e, kc=kc: e.copy(
            t_wo[:, kc * D:(kc + 1) * D], t_wld2[:, :]), deps=[ev1])
        wo_par = evcp
        woev.append(evcp)

    ev_sf = P.step("vector", lambda e: e.tensor_copy(t_sf[:, :], t_s0[:, :]),
                   deps=[ered] + list(g_fence))
    ev_sb = P.step("vector", lambda e: e.tensor_copy(t_sb2[:, :], t_s0b[:, :]),
                   deps=[ev_s0f] + list(g_fence))

    t_yfm = bsb("t_yfm", [128, 8 * G1K], BF16)
    t_xo = bsb("t_xo", [128, 2 * D], F32)
    t_att = bsb("t_att", [128, D], F32)
    t_asq = bsb("t_asq", [128, D], F32)
    t_ast = bsb("t_ast", [128, 32], F32)
    t_xm = bsb("t_xm", [128, D], F32)
    t_xn2b = bsb("t_xn2b", [128, D], BF16)
    t_x2ev = bsb("t_x2ev", [128, D], BF16)

    ev_xmid_w = [None] * NCH
    ev_xn2_w = [None] * NCH
    yslot_guard = [()] * 8
    xo_ev = [None, None]
    c1f_box = [()]

    def emit_outproj(cix):
        par = cix % 2
        cc = cix % (G1K // 128)
        so = par * 16
        P.begin_fence()
        cfl = list(c1f_box[0])
        b0, bd = bank(2)
        mme = []
        for kc in range(8):
            for nh in range(2):
                mme.append(P.step(
                    "tensor", lambda e, kc=kc, nh=nh, b0=b0, cc=cc: e.matmul(
                        psf(b0 + nh, 0, 128, 0, 512),
                        t_yfm[:, kc * G1K + cc * 128:kc * G1K + (cc + 1) * 128],
                        t_wo[:, kc * D + nh * 512:kc * D + (nh + 1) * 512],
                        start=(kc == 0), stop=(kc == 7)),
                    deps=[yfm_ev[cix], woev[kc]] + (bd if kc == 0 else [])))
        yslot_guard[cc] = (mme[-1],)
        aps = ps_t[0:128, 512 * b0:512 * b0 + 1024]
        el1 = P.step("scalar", lambda e, aps=aps, so=so: e.activation(
            t_asq[:, :], aps, AF.Square, accum_out=t_ast[:, so:so + 1]),
            deps=mme + cfl)
        el2 = P.step("vector", lambda e, aps=aps, so=so: e.reduce_sum(
            t_ast[:, so + 1:so + 2], aps, AXX), deps=mme + cfl + [el1])
        el3 = P.step("vector", lambda e, so=so: e.tensor_scalar_mul(
            t_ast[:, so + 2:so + 3], t_ast[:, so + 1:so + 2], 1.0 / D),
            deps=[el2])
        el4 = P.step("vector", lambda e, so=so: e.tensor_mul(
            t_ast[:, so + 3:so + 4], t_ast[:, so + 2:so + 3],
            t_ast[:, so + 2:so + 3]), deps=[el3])
        el5 = P.step("vector", lambda e, so=so: e.tensor_scalar_mul(
            t_ast[:, so + 4:so + 5], t_ast[:, so:so + 1], 1.0 / D),
            deps=[el1])
        el6 = P.step("vector", lambda e, so=so: e.tensor_sub(
            t_ast[:, so + 5:so + 6], t_ast[:, so + 4:so + 5],
            t_ast[:, so + 3:so + 4]), deps=[el4, el5])
        el7 = P.step("scalar", lambda e, so=so: e.activation(
            t_ast[:, so + 6:so + 7], t_ast[:, so + 5:so + 6], AF.Ln,
            bias=t_eps[:, :]), deps=[el6])
        el8 = P.step("scalar", lambda e, so=so: e.activation(
            t_ast[:, so + 7:so + 8], t_ast[:, so + 6:so + 7], AF.Exp,
            scale=-0.5), deps=[el7])
        el9 = P.step("vector", lambda e, aps=aps, so=so: e.tensor_scalar(
            t_att[:, :], aps, t_ast[:, so + 2:so + 3],
            t_ast[:, so + 7:so + 8], ALU.subtract, ALU.mult),
            deps=[el8] + cfl)
        bank_done(b0, 2, [el9])
        exl = P.dma("sync", t_xo[:, par * D:(par + 1) * D],
                    d_x[cix * 128:(cix + 1) * 128, :],
                    deps=[xo_ev[par]] if xo_ev[par] else [])
        exm = P.step("vector", lambda e, par=par: e.tensor_add(
            t_xm[:, :], t_att[:, :], t_xo[:, par * D:(par + 1) * D]),
            deps=[el9, exl] + cfl)
        xo_ev[par] = exm
        ev_xmid_w[cix] = P.dma("sync", s_xmid[cix * 128:(cix + 1) * 128, :],
                               t_xm[:, :], deps=[exm])
        em1 = P.step("scalar", lambda e, so=so: e.activation(
            t_asq[:, :], t_xm[:, :], AF.Square,
            accum_out=t_ast[:, so + 8:so + 9]), deps=[exm, el1])
        em2 = P.step("vector", lambda e, so=so: e.reduce_sum(
            t_ast[:, so + 9:so + 10], t_xm[:, :], AXX), deps=[exm])
        em3 = P.step("vector", lambda e, so=so: e.tensor_scalar_mul(
            t_ast[:, so + 10:so + 11], t_ast[:, so + 9:so + 10], 1.0 / D),
            deps=[em2])
        em4 = P.step("vector", lambda e, so=so: e.tensor_mul(
            t_ast[:, so + 11:so + 12], t_ast[:, so + 10:so + 11],
            t_ast[:, so + 10:so + 11]), deps=[em3])
        em5 = P.step("vector", lambda e, so=so: e.tensor_scalar_mul(
            t_ast[:, so + 12:so + 13], t_ast[:, so + 8:so + 9], 1.0 / D),
            deps=[em1])
        em6 = P.step("vector", lambda e, so=so: e.tensor_sub(
            t_ast[:, so + 13:so + 14], t_ast[:, so + 12:so + 13],
            t_ast[:, so + 11:so + 12]), deps=[em4, em5])
        em7 = P.step("scalar", lambda e, so=so: e.activation(
            t_ast[:, so + 14:so + 15], t_ast[:, so + 13:so + 14], AF.Ln,
            bias=t_eps[:, :]), deps=[em6])
        em8 = P.step("scalar", lambda e, so=so: e.activation(
            t_ast[:, so + 15:so + 16], t_ast[:, so + 14:so + 15], AF.Exp,
            scale=-0.5), deps=[em7])
        em9 = P.step("vector", lambda e, so=so: e.tensor_scalar(
            t_xn2b[:, :], t_xm[:, :], t_ast[:, so + 10:so + 11],
            t_ast[:, so + 15:so + 16], ALU.subtract, ALU.mult),
            deps=[em8] + cfl)
        x2t = [P.step("tensor", lambda e, dc=dc: e.transpose(
            psb_t[0:128, dc * 128:(dc + 1) * 128],
            t_xn2b[:, dc * 128:(dc + 1) * 128], t_con["eye_bf"][:, :]),
            deps=[em9, evc0] + (list(psb_free[0]) if dc == 0 else []))
            for dc in range(8)]
        ex2e = P.step("vector", lambda e: e.tensor_copy(
            t_x2ev[:, :], psb_t[0:128, 0:1024]), deps=x2t + cfl)
        psb_free[0] = (ex2e,)
        gi_, cc_ = cix // (G1K // 128), cix % (G1K // 128)
        ev_xn2_w[cix] = P.dma(
            "sync",
            s_x2p[:, :].rearrange("p (g dc t) -> p g dc t", dc=8, t=G1K)[
                :, gi_, :, cc_ * 128:(cc_ + 1) * 128],
            t_x2ev[:, :].rearrange("p (dc t) -> p dc t", dc=8),
            deps=[ex2e])
        c1f_box[0] = P.end_fence()

    ld_guard = [(), ()]
    bfence = ()
    yfm_ev = [None] * NCH
    for cix in range(NCH):
        par = cix % 2
        P.begin_fence()
        bf = list(bfence)
        dch = list(ev_chunk_done[cix])
        lz = P.dma("sync", t_zl[par][:, :], s_z[cix, :, :],
                   deps=[dch[0]] + list(ld_guard[par]))
        lft = P.dma("sync", t_ftl[par][:, :], s_ft[cix, :, :], deps=[dch[1]])
        lkd = P.dma("sync", t_kdl[par][:, :], s_kd[cix, :, :], deps=[dch[2]])
        lpt = P.dma("sync", t_ptl[par][:, :], s_pt[cix, :, :], deps=[dch[3]])
        lqt = P.dma("sync",
                    t_qtl[par][:, :].rearrange("e (j hh t) -> e j hh t",
                                               j=8, hh=2),
                    s_qt[cix].rearrange("j (hh e) t -> e j hh t", hh=2),
                    deps=[dch[4]])
        lg = P.dma("sync", t_gl[par][:, :], s_gate[cix * 128:(cix + 1) * 128, :],
                   deps=[ev_gate_w[cix]])

        bfs, bdf = bank(2)
        fs_m = [P.step("tensor", lambda e, h=h, bfs=bfs, par=par: e.matmul(
            ps_t[0:128, 512 * bfs + h * 64:512 * bfs + (h + 1) * 64],
            t_ftl[par][:, h * 128:(h + 1) * 128],
            t_sb2[:, h * 64:(h + 1) * 64], start=True, stop=True),
            deps=[lft, ev_sb] + (bdf if h == 0 else []))
            for h in range(16)]
        w0v = t_zl[par][:, :].rearrange("p (h two e) -> p h two e",
                                        h=16, two=2)[:, :, 0, :]
        ewh = P.step("vector", lambda e, bfs=bfs, w0v=w0v: e.scalar_tensor_tensor(
            t_wh[:, :].rearrange("p (h e) -> p h e", h=16),
            ps_t[0:128, 512 * bfs:512 * bfs + 1024].rearrange(
                "p (h e) -> p h e", h=16),
            -1.0, w0v, ALU.mult, ALU.add), deps=fs_m + [lz] + bf)
        bank_done(bfs, 2, [ewh])

        bo, bdo = bank(2)
        o_m = []
        for h in range(16):
            o_m.append(P.step("tensor", lambda e, h=h, bo=bo, par=par: e.matmul(
                ps_t[0:128, 512 * bo + h * 64:512 * bo + (h + 1) * 64],
                t_qtl[par][:, h * 128:(h + 1) * 128],
                t_sb2[:, h * 64:(h + 1) * 64], start=True, stop=False),
                deps=[lqt, ev_sb] + (bdo if h == 0 else [])))
            o_m.append(P.step("tensor", lambda e, h=h, bo=bo, par=par: e.matmul(
                ps_t[0:128, 512 * bo + h * 64:512 * bo + (h + 1) * 64],
                t_ptl[par][:, h * 128:(h + 1) * 128],
                t_wh[:, h * 64:(h + 1) * 64], start=False, stop=True),
                deps=[lpt, ewh]))
        ey = P.step("vector", lambda e, bo=bo: e.tensor_copy(
            t_ytm[:, :], ps_t[0:128, 512 * bo:512 * bo + 1024]),
            deps=o_m + bf)
        bank_done(bo, 2, [ey])

        esg = P.step("vector", lambda e, cix=cix: e.tensor_tensor(
            t_sg[:, :].rearrange("p (h e) -> p h e", h=16),
            t_sb2[:, :].rearrange("p (h e) -> p h e", h=16),
            t_gcols[:, cix * 16:(cix + 1) * 16].unsqueeze(2).broadcast_to(
                [64, 16, 64]), ALU.mult), deps=[ev_sb, ev_gamma[cix]] + bf)
        bs2, bds2 = bank(2)
        s_m = []
        for h in range(16):
            s_m.append(P.step("tensor", lambda e, h=h, bs2=bs2, par=par:
                              e.matmul(
                ps_t[0:64, 512 * bs2 + h * 64:512 * bs2 + (h + 1) * 64],
                t_kdl[par][:, h * 64:(h + 1) * 64],
                t_wh[:, h * 64:(h + 1) * 64], start=True, stop=False),
                deps=[lkd, ewh] + (bds2 if h == 0 else [])))
            s_m.append(P.step("tensor", lambda e, h=h, bs2=bs2: e.matmul(
                ps_t[0:64, 512 * bs2 + h * 64:512 * bs2 + (h + 1) * 64],
                t_con["i64rep_bf"][:, 0:64],
                t_sg[:, h * 64:(h + 1) * 64], start=False, stop=True),
                deps=[esg]))
        ev_sf = P.step("vector", lambda e, bs2=bs2: e.tensor_copy(
            t_sf[:, :], ps_t[0:64, 512 * bs2:512 * bs2 + 1024]),
            deps=s_m + [ev_sf])
        ev_sb = P.step("vector", lambda e, bs2=bs2: e.tensor_copy(
            t_sb2[:, :], ps_t[0:64, 512 * bs2:512 * bs2 + 1024]),
            deps=s_m + [ev_sb] + [esg] + fs_m + o_m[:1])
        bank_done(bs2, 2, [ev_sf, ev_sb])

        er1 = P.step("scalar", lambda e: e.activation(
            t_ysq[:, :], t_ytm[:, :], AF.Square, accum_out=t_yst[:, 0:1]),
            deps=[ey] + bf)
        er2 = P.step("scalar", lambda e: e.activation(
            t_yst[:, 1:2], t_yst[:, 0:1], AF.Ln, scale=1.0 / D, bias=t_eps[:, :]),
            deps=[er1])
        er3 = P.step("scalar", lambda e: e.activation(
            t_yst[:, 2:3], t_yst[:, 1:2], AF.Exp, scale=-0.5), deps=[er2])
        er4 = P.step("vector", lambda e: e.scalar_tensor_tensor(
            t_ysq[:, :], t_ytm[:, :], t_yst[:, 2:3], t_gnb[:, :],
            ALU.mult, ALU.mult), deps=[er3, ev_gnb, er1])
        er5 = P.step("vector", lambda e, par=par: e.tensor_mul(
            t_yout[:, :], t_ysq[:, :], t_gl[par][:, :]), deps=[er4, lg] + bf)
        cc = cix % (G1K // 128)
        yt = [P.step("tensor", lambda e, dc=dc: e.transpose(
            psb_t[0:128, dc * 128:(dc + 1) * 128],
            t_yout[:, dc * 128:(dc + 1) * 128], t_con["eye_bf"][:, :]),
            deps=[er5, evc0] + (list(psb_free[0]) if dc == 0 else []) +
                 (list(yslot_guard[cc]) if dc == 0 else []))
            for dc in range(8)]
        yfm_ev[cix] = P.step("vector", lambda e, cc=cc: e.tensor_copy(
            t_yfm[:, :].rearrange("p (dc t) -> p dc t", dc=8)[
                :, :, cc * 128:(cc + 1) * 128],
            psb_t[0:128, 0:1024].rearrange("p (dc t) -> p dc t", dc=8)),
            deps=yt + bf)
        psb_free[0] = (yfm_ev[cix],)
        bfence = P.end_fence()
        ld_guard[par] = (ewh, ey, er5, s_m[-1], o_m[-1], fs_m[-1], lqt)
        emit_outproj(cix)

    if STAGE <= 5:
        return _cut(P.fence_all())

    if STAGE <= 6:
        return _cut(P.fence_all())
    b_fence = P.fence_all()
    b_ctx.close()
    gb_ctx.close()

    c_ctx = ExitStack()

    def csb(name, shape, dtype):
        return c_ctx.enter_context(nc.sbuf_tensor(name, shape, dtype))

    t_fwg = csb("t_fwg", [128, 8 * HID], BF16)
    t_fwu = csb("t_fwu", [128, 8 * HID], BF16)
    t_fwd = csb("t_fwd", [128, 24 * D], BF16)
    t_x2f = csb("t_x2f", [128, 8 * G1K], BF16)
    t_fth = csb("t_fth", [128, 256], F32)
    t_fsg = csb("t_fsg", [128, 256], BF16)
    t_hbf = csb("t_hbf", [128, 256], BF16)
    t_fo = csb("t_fo", [128, 2 * D], BF16)

    SFF = 1.0 / 127.0
    fw_deps = list(b_fence) + ffn_cast_ev
    fgw = [P.dma("sync", t_fwg[:, dc * HID:(dc + 1) * HID],
                 s_wgbf[:, dc * 128:(dc + 1) * 128], deps=fw_deps,
                 transpose=True) for dc in range(8)]
    fuw = [P.dma("sync", t_fwu[:, dc * HID:(dc + 1) * HID],
                 s_wubf[:, dc * 128:(dc + 1) * 128], deps=fw_deps,
                 transpose=True) for dc in range(8)]
    fdw = [P.dma("sync", t_fwd[:, hc * D:(hc + 1) * D],
                 s_wdbf[:, hc * 128:(hc + 1) * 128], deps=fw_deps,
                 transpose=True) for hc in range(24)]

    ev_fdown_w = [None] * NCH
    t_fl = csb("t_fl", [128, D], BF16)
    t_xml = csb("t_xml", [128, D], F32)
    t_osq = csb("t_osq", [128, D], F32)
    t_ost = csb("t_ost", [128, 6 * NCH], F32)
    t_fdum = csb("t_fdum", [128, D], BF16)
    c2b_prev = {"eo9": None, "eoA": None, "ew": None, "esq": None}
    ST_SQ, ST_SM, ST_MN, ST_VR, ST_M2, ST_RS = (0, NCH, 2 * NCH, 3 * NCH,
                                                4 * NCH, 5 * NCH)

    def emit_c2b(cix, stat_ev):
        prev_eo9, prev_eoA, prev_ew = (c2b_prev["eo9"], c2b_prev["eoA"],
                                       c2b_prev["ew"])
        lf = P.dma("gpsimd", t_fl[:, :],
                   s_fdown[cix * 128:(cix + 1) * 128, :],
                   deps=[ev_fdown_w[cix]] + ([prev_eo9] if prev_eo9 else []))
        lx = P.dma("gpsimd", t_xml[:, :],
                   s_xmid[cix * 128:(cix + 1) * 128, :],
                   deps=[ev_xmid_w[cix]] + ([prev_eoA] if prev_eoA else []))
        eo9 = P.step("gpsimd", lambda e, cix=cix: e.tensor_scalar(
            t_osq[:, :], t_fl[:, :], t_ost[:, ST_MN + cix:ST_MN + cix + 1],
            t_ost[:, ST_RS + cix:ST_RS + cix + 1],
            ALU.subtract, ALU.mult),
            deps=[lf, stat_ev] + ([prev_ew] if prev_ew else []))
        eoA = P.step("gpsimd", lambda e: e.tensor_add(
            t_osq[:, :], t_osq[:, :], t_xml[:, :]),
            deps=[eo9, lx] + ([prev_ew] if prev_ew else []))
        ew = P.dma("gpsimd", d_out[cix * 128:(cix + 1) * 128, :],
                   t_osq[:, :], deps=[eoA])
        c2b_prev["eo9"] = eo9
        c2b_prev["eoA"] = eoA
        c2b_prev["ew"] = ew

    x2f_guard = []
    fo_guard = []
    for gi in range(TOK // G1K):
        xdeps = [ev_xn2_w[gi * (G1K // 128) + i] for i in range(G1K // 128)]
        x2l = P.dma("sync", t_x2f[:, :],
                    s_x2p[:, gi * 8 * G1K:(gi + 1) * 8 * G1K],
                    deps=xdeps + x2f_guard)
        x2evs = [x2l] * 8
        for bi in range(G1K // 256):
            tof = bi * 256
            bdn, bddn = bank(4, hold=True)
            dn_m = []
            h_prev = None
            prev_dn = None
            for hc in range(24):
                bg_, bdg_ = bank(1)
                g_m = [P.step("tensor",
                              lambda e, kc=kc, hc=hc, bg_=bg_, tof=tof: e.matmul(
                    psf(bg_, 0, 128, 0, 256),
                    t_fwg[:, kc * HID + hc * 128:kc * HID + (hc + 1) * 128],
                    t_x2f[:, kc * G1K + tof:kc * G1K + tof + 256],
                    start=(kc == 0), stop=(kc == 7)),
                    deps=[x2evs[kc], fgw[kc]] + (bdg_ if kc == 0 else []))
                    for kc in range(8)]
                bu_, bdu_ = bank(1)
                u_m = [P.step("tensor",
                              lambda e, kc=kc, hc=hc, bu_=bu_, tof=tof: e.matmul(
                    psf(bu_, 0, 128, 0, 256),
                    t_fwu[:, kc * HID + hc * 128:kc * HID + (hc + 1) * 128],
                    t_x2f[:, kc * G1K + tof:kc * G1K + tof + 256],
                    start=(kc == 0), stop=(kc == 7)),
                    deps=[x2evs[kc], fuw[kc]] + (bdu_ if kc == 0 else []))
                    for kc in range(8)]
                et = P.step("scalar", lambda e, bg_=bg_: e.activation(
                    t_fth[:, :], psf(bg_, 0, 128, 0, 256), AF.Tanh,
                    scale=SFF / 2),
                    deps=g_m + ([h_prev] if h_prev else []))
                es = P.step("vector", lambda e: e.tensor_scalar(
                    t_fsg[:, :], t_fth[:, :], 0.5, 0.5, ALU.mult, ALU.add),
                    deps=[et])
                es2 = P.step("vector", lambda e, bg_=bg_: e.scalar_tensor_tensor(
                    t_fsg[:, :], psf(bg_, 0, 128, 0, 256), SFF, t_fsg[:, :],
                    ALU.mult, ALU.mult), deps=[es])
                bank_done(bg_, 1, [es2])
                eh = P.step("vector", lambda e, bu_=bu_: e.scalar_tensor_tensor(
                    t_hbf[:, :], psf(bu_, 0, 128, 0, 256), SFF, t_fsg[:, :],
                    ALU.mult, ALU.mult),
                    deps=[es2] + u_m + ([prev_dn] if prev_dn else []))
                bank_done(bu_, 1, [eh])
                h_prev = eh
                for ts_ in range(2):
                    for nh in range(2):
                        dn_m.append(P.step(
                            "tensor",
                            lambda e, ts_=ts_, nh=nh, hc=hc, bdn=bdn: e.matmul(
                                psf(bdn + 2 * ts_ + nh, 0, 128, 0, 512),
                                t_hbf[:, ts_ * 128:(ts_ + 1) * 128],
                                t_fwd[:, hc * D + nh * 512:hc * D + (nh + 1) * 512],
                                start=(hc == 0), stop=(hc == 23)),
                            deps=[eh, fdw[hc]] + (bddn if hc == 0 and
                                                  ts_ == 0 and nh == 0 else [])))
                prev_dn = dn_m[-1]
            efo = [P.step("scalar", lambda e, ts_=ts_, bdn=bdn: e.mul(
                t_fo[:, ts_ * D:(ts_ + 1) * D],
                ps_t[0:128, 512 * (bdn + 2 * ts_):512 * (bdn + 2 * ts_) + 1024],
                SFF), deps=dn_m + fo_guard) for ts_ in range(2)]
            cix0 = (gi * G1K + tof) // 128
            sq_evs = []
            for ts_ in range(2):
                esq = P.step("scalar", lambda e, ts_=ts_, bdn=bdn, cix0=cix0:
                             e.activation(
                    t_fdum[:, :],
                    ps_t[0:128, 512 * (bdn + 2 * ts_):
                         512 * (bdn + 2 * ts_) + 1024],
                    AF.Square, scale=SFF,
                    accum_out=t_ost[:, ST_SQ + cix0 + ts_:
                                    ST_SQ + cix0 + ts_ + 1]),
                    deps=dn_m + ([c2b_prev["esq"]] if c2b_prev["esq"]
                                 else []))
                c2b_prev["esq"] = esq
                sq_evs.append(esq)
                sq_evs.append(P.step("vector", lambda e, ts_=ts_, bdn=bdn,
                                     cix0=cix0: e.reduce_sum(
                    t_ost[:, ST_SM + cix0 + ts_:ST_SM + cix0 + ts_ + 1],
                    ps_t[0:128, 512 * (bdn + 2 * ts_):
                         512 * (bdn + 2 * ts_) + 1024], AXX),
                    deps=dn_m + [esq]))
            bank_done(bdn, 4, efo + sq_evs)
            ev = P.dma("sync",
                       s_fdown[cix0 * 128:(cix0 + 2) * 128, :].rearrange(
                           "(ts p) d -> p ts d", ts=2),
                       t_fo[:, :].rearrange("p (ts d) -> p ts d", ts=2),
                       deps=efo)
            ev_fdown_w[cix0] = ev
            ev_fdown_w[cix0 + 1] = ev
            fo_guard = [ev]
        x2f_guard = [dn_m[-1], g_m[-1], u_m[-1]]
        g0 = gi * (G1K // 128)
        ng = G1K // 128
        st1 = P.step("vector", lambda e, g0=g0, ng=ng: e.tensor_scalar_mul(
            t_ost[:, ST_MN + g0:ST_MN + g0 + ng],
            t_ost[:, ST_SM + g0:ST_SM + g0 + ng], SFF / D), deps=sq_evs)
        st2 = P.step("vector", lambda e, g0=g0, ng=ng: e.tensor_mul(
            t_ost[:, ST_M2 + g0:ST_M2 + g0 + ng],
            t_ost[:, ST_MN + g0:ST_MN + g0 + ng],
            t_ost[:, ST_MN + g0:ST_MN + g0 + ng]), deps=[st1])
        st3 = P.step("vector", lambda e, g0=g0, ng=ng: e.tensor_scalar_mul(
            t_ost[:, ST_VR + g0:ST_VR + g0 + ng],
            t_ost[:, ST_SQ + g0:ST_SQ + g0 + ng], 1.0 / D), deps=sq_evs)
        st4 = P.step("vector", lambda e, g0=g0, ng=ng: e.tensor_sub(
            t_ost[:, ST_VR + g0:ST_VR + g0 + ng],
            t_ost[:, ST_VR + g0:ST_VR + g0 + ng],
            t_ost[:, ST_M2 + g0:ST_M2 + g0 + ng]), deps=[st2, st3])
        st5 = P.step("scalar", lambda e, g0=g0, ng=ng: e.activation(
            t_ost[:, ST_M2 + g0:ST_M2 + g0 + ng],
            t_ost[:, ST_VR + g0:ST_VR + g0 + ng], AF.Ln,
            bias=t_eps[:, :]), deps=[st4, c2b_prev["esq"]])
        st6 = P.step("scalar", lambda e, g0=g0, ng=ng: e.activation(
            t_ost[:, ST_RS + g0:ST_RS + g0 + ng],
            t_ost[:, ST_M2 + g0:ST_M2 + g0 + ng], AF.Exp,
            scale=-0.5), deps=[st5])
        for cix in range(g0, g0 + ng):
            emit_c2b(cix, st6)

    if STAGE <= 7:
        return _cut(P.fence_all())
    c_ctx.close()
    P.finalize(block, sems)
    ctx.close()
    return nc


def kernel(L0_attn_x, L0_attn_Wq, L0_attn_Wk, L0_attn_Wv, L0_attn_Wa,
           L0_attn_ba, L0_attn_Wb, L0_attn_bb, L0_attn_Wg, L0_attn_Wo,
           L0_attn_g_norm, L0_ffn_w_gate, L0_ffn_w_up, L0_ffn_w_down):
    B, S, _ = L0_attn_x.shape
    TOK = B * S // NCORE
    nc = build_program(TOK)
    CON = build_consts()

    wab = np.zeros((D, 64), np.float32)
    wab[:, 0:16] = np.asarray(L0_attn_Wa, np.float32)
    wab[:, 32:48] = np.asarray(L0_attn_Wb, np.float32)
    bab = np.zeros((64, 1), np.float32)
    bab[0:16, 0] = np.asarray(L0_attn_ba, np.float32)
    bab[32:48, 0] = np.asarray(L0_attn_bb, np.float32)

    x = np.ascontiguousarray(
        np.asarray(L0_attn_x, np.float32).reshape(B, 4, S // 4, D))
    base = {
        "wq": np.ascontiguousarray(L0_attn_Wq, np.float32),
        "wk": np.ascontiguousarray(L0_attn_Wk, np.float32),
        "wv": np.ascontiguousarray(L0_attn_Wv, np.float32),
        "wgt": np.ascontiguousarray(L0_attn_Wg, np.float32),
        "wo": np.ascontiguousarray(L0_attn_Wo, np.float32),
        "wab": wab, "bab": bab,
        "gnorm": np.ascontiguousarray(
            np.asarray(L0_attn_g_norm, np.float32).reshape(1, D)),
        "ffn_g": np.ascontiguousarray(L0_ffn_w_gate, np.int32),
        "ffn_u": np.ascontiguousarray(L0_ffn_w_up, np.int32),
        "ffn_d": np.ascontiguousarray(L0_ffn_w_down, np.int32),
    }
    for k, v in CON.items():
        base["c_" + k] = v
    in_maps = []
    for core in range(NCORE):
        m = dict(base)
        m["x"] = np.ascontiguousarray(x[core // 4, core % 4])
        selv = np.zeros((1, 4), np.float32)
        if core % 4 > 0:
            selv[0, core % 4 - 1] = 1.0
        m["selv"] = selv
        in_maps.append(m)

    res = run_bass_kernel_spmd(nc, in_maps, core_ids=list(range(NCORE)))
    out = np.zeros((B, S, D), np.float32)
    for core in range(NCORE):
        out[core // 4, (core % 4) * TOK:(core % 4 + 1) * TOK] = \
            np.asarray(res.results[core]["out"], np.float32)
    return out


# version 09
# speedup vs baseline: 1.0156x; 1.0156x over previous
import numpy as np
import ml_dtypes
import concourse.bass as bass
import concourse.mybir as mybir
from concourse.bass_utils import run_bass_kernel_spmd
from contextlib import ExitStack

F32 = mybir.dt.float32
BF16 = mybir.dt.bfloat16
I32 = mybir.dt.int32
AF = mybir.ActivationFunctionType
ALU = mybir.AluOpType
AXX = mybir.AxisListType.X

D = 1024
HID = 3072
C = 128
NCORE = 8

ENG2SEM = {"tensor": "st", "vector": "sv", "scalar": "ss", "gpsimd": "sp"}
NLANE = 8
LANES = tuple(f"ld{i}" for i in range(NLANE))
NGLANE = 2
GLANES = tuple(f"gd{i}" for i in range(NGLANE))
SEMKEYS = ("st", "sv", "ss", "sp", "scc") + LANES + GLANES


class Prog:

    def __init__(self):
        self.sched = []
        self.cnt = {k: 0 for k in SEMKEYS}
        self.collect = None
        self.lane_rr = 0
        self.glane_rr = 0

    def step(self, engine, fn, deps=(), inc=1, semkey=None):
        key = semkey or ENG2SEM[engine]
        deps = tuple(d for d in deps if d is not None)
        self.sched.append((engine, fn, deps, inc, key))
        self.cnt[key] += inc
        ev = (key, self.cnt[key])
        if self.collect is not None:
            self.collect[key] = ev
        return ev

    def dma(self, engine, out, in_, deps=(), **kw):
        if engine == "gpsimd":
            lane = GLANES[self.glane_rr]
            self.glane_rr = (self.glane_rr + 1) % NGLANE
        else:
            lane = LANES[self.lane_rr]
            self.lane_rr = (self.lane_rr + 1) % NLANE
        deps = tuple(d for d in deps if d is not None)
        if self.cnt[lane] > 0:
            deps = deps + ((lane, self.cnt[lane]),)
        return self.step(
            engine,
            lambda e, out=out, in_=in_, kw=kw: e.dma_start(out=out, in_=in_, **kw),
            deps=deps, inc=16, semkey=lane)

    def begin_fence(self):
        self.collect = {}

    def end_fence(self):
        f = tuple(self.collect.values())
        self.collect = None
        return f

    def fence_all(self):
        return tuple((k, v) for k, v in self.cnt.items() if v > 0)

    def finalize(self, block, sems):
        water = {}

        def emit(this_engine, eng):
            for engine, fn, deps, inc, key in self.sched:
                if engine != this_engine:
                    continue
                for (k, v) in deps:
                    if water.get((this_engine, k), 0) < v:
                        eng.wait_ge(sems[k], v)
                        water[(this_engine, k)] = v
                inst = fn(eng)
                if inst is not None:
                    inst.then_inc(sems[key], inc)
                else:
                    eng.nop().then_inc(sems[key], inc)

        @block.sync
        def _(eng):
            emit("sync", eng)
            for k in LANES + GLANES + ("scc",):
                if self.cnt[k] > 0:
                    eng.wait_ge(sems[k], self.cnt[k])

        @block.tensor
        def _(eng):
            emit("tensor", eng)

        @block.vector
        def _(eng):
            emit("vector", eng)

        @block.scalar
        def _(eng):
            emit("scalar", eng)

        @block.gpsimd
        def _(eng):
            emit("gpsimd", eng)


def build_consts():
    con = {}
    con["ones1"] = np.ones((1, 128), np.float32)
    con["ones1_bf"] = np.ones((1, 128), ml_dtypes.bfloat16)
    con["eye_bf"] = np.eye(128, dtype=ml_dtypes.bfloat16)
    con["eye_f32"] = np.eye(128, dtype=np.float32)
    con["i64_f32"] = np.eye(64, dtype=np.float32)
    con["i64rep_bf"] = np.tile(np.eye(64), (1, 16)).astype(ml_dtypes.bfloat16)
    idx = np.arange(128)
    m = np.where(idx[:, None] > idx[None, :], -1.0, 0.0).astype(np.float32)
    con["neglmask_bf"] = m
    con["umask_bf"] = (idx[:, None] <= idx[None, :]).astype(
        np.float32)
    ho = np.zeros((128, 2), np.float32)
    ho[:64, 0] = 1.0
    ho[64:, 1] = 1.0
    con["headones_bf"] = ho.astype(ml_dtypes.bfloat16)
    s2 = np.zeros((2, 128), np.float32)
    s2[0, :64] = 1.0
    s2[1, 64:] = 1.0
    con["sel2t_bf"] = s2.astype(ml_dtypes.bfloat16)
    selh = np.zeros((16, 16 * 64), np.float32)
    for h in range(16):
        selh[h, h * 64:(h + 1) * 64] = 1.0
    con["selh_f32"] = selh
    con["selh_bf"] = selh.astype(ml_dtypes.bfloat16)
    selj = np.zeros((16, 8 * 128), np.float32)
    for j in range(8):
        selj[2 * j, j * 128:j * 128 + 64] = 1.0
        selj[2 * j + 1, j * 128 + 64:(j + 1) * 128] = 1.0
    con["selj_bf"] = selj.astype(ml_dtypes.bfloat16)
    return con


STAGE = 99
SUB = 999
DEBUG_SCRATCH = False


def build_program(TOK):
    NCH = TOK // C
    PT_ = 256
    NPT = TOK // PT_
    CPT = PT_ // C
    G1K = min(1024, TOK)
    NL = 7

    nc = bass.Bass("TRN2", target_bir_lowering=False, debug=False)
    P = Prog()
    CON = build_consts()

    def din(name, shape, dtype):
        return nc.dram_tensor(name, list(shape), dtype, kind="ExternalInput").ap()

    d_x = din("x", (TOK, D), F32)
    d_w = {n: din(n, (D, D), F32) for n in ("wq", "wk", "wv", "wgt", "wo")}
    d_wab = din("wab", (D, 64), F32)
    d_bab = din("bab", (64, 1), F32)
    d_gnorm = din("gnorm", (1, D), F32)
    d_fg = din("ffn_g", (HID, D), I32)
    d_fu = din("ffn_u", (HID, D), I32)
    d_fd = din("ffn_d", (D, HID), I32)
    d_selv = din("selv", (1, 4), F32)
    d_con = {k: din("c_" + k, v.shape,
                    F32 if v.dtype == np.float32 else BF16)
             for k, v in CON.items()}
    d_out = nc.dram_tensor("out", [TOK, D], F32, kind="ExternalOutput").ap()

    def scr(name, shape, dtype):
        kind = "ExternalOutput" if DEBUG_SCRATCH else "Internal"
        if name in ("s_agi", "s_ago"):
            kind = "Internal"
        return nc.dram_tensor(name, list(shape), dtype, kind=kind).ap()

    s_xn = scr("s_xn", (TOK, D), BF16)
    s_kfm = scr("s_kfm", (D, TOK), BF16)
    s_gate = scr("s_gate", (TOK, D), BF16)
    s_ab = scr("s_ab", (TOK // 256, 64, 256), F32)
    s_z = scr("s_z", (NCH, 128, 2048), BF16)
    s_ft = scr("s_ft", (NCH, 64, 2048), BF16)
    s_kd = scr("s_kd", (NCH, 128, 1024), BF16)
    s_pt = scr("s_pt", (NCH, 128, 2048), BF16)
    s_qt = scr("s_qt", (NCH, 8, 128, 128), BF16)
    s_cath = scr("s_cath", (NCH, 2048), BF16)
    s_catl = scr("s_catl", (NCH, 2048), BF16)
    s_agi = scr("s_agi", (2048, 64), F32)
    s_ago = scr("s_ago", (4 * 2048, 64), F32)
    s_y = scr("s_y", (TOK, D), BF16)
    s_xmid = scr("s_xmid", (TOK, D), F32)
    s_xn2 = scr("s_xn2", (TOK, D), BF16)
    s_wgbf = scr("s_wgbf", (HID, D), BF16)
    s_wubf = scr("s_wubf", (HID, D), BF16)
    s_wdbf = scr("s_wdbf", (D, HID), BF16)
    s_fdown = scr("s_fdown", (TOK, D), BF16)
    s_xdbg = scr("s_xdbg", (NCH, 128, 2048), BF16)
    s_emdbg = scr("s_emdbg", (NCH, 4, 128, 512), F32)
    s_tmdbg = scr("s_tmdbg", (NCH, 128, 80), F32)
    s_kkdbg = scr("s_kkdbg", (NCH, 4, 128, 512), F32)
    s_kkdbg2 = scr("s_kkdbg2", (NCH, 4, 128, 512), F32)
    s_ddbg = scr("s_ddbg", (NCH, 4, 128, 512), F32)

    ctx = ExitStack()
    ps_t = ctx.enter_context(nc.psum_tensor("ps", [128, 3584], F32))
    psb_t = ctx.enter_context(nc.psum_tensor("psb", [128, 1024], BF16))

    def psf(b, p0, p1, c0, n):
        return ps_t[p0:p1, 512 * b + c0:512 * b + c0 + n]

    bank_ev = [()] * 8
    bank_hold = [False] * 8
    rr = [0]
    psb_free = [()]

    def bank(n=1, hold=False):
        start = rr[0]
        for off in range(8):
            b0 = (start + off) % 7
            if b0 + n > 7:
                continue
            if any(bank_hold[b] for b in range(b0, b0 + n)):
                continue
            rr[0] = (b0 + n) % 7
            deps = sum((bank_ev[b] for b in range(b0, b0 + n)), ())
            if hold:
                for b in range(b0, b0 + n):
                    bank_hold[b] = True
            return b0, list(deps)
        raise RuntimeError("no banks")

    def bank_done(b0, n, evs):
        for b in range(b0, b0 + n):
            bank_ev[b] = tuple(evs)
            bank_hold[b] = False

    sems = {}
    for name in SEMKEYS:
        sems[name] = ctx.enter_context(nc.semaphore(name))
    block = ctx.enter_context(nc.Block())

    sb = nc.alloc_sbuf_tensor
    t_con = {k: sb("t_" + k, list(v.shape),
                   F32 if v.dtype == np.float32 else BF16)
             for k, v in CON.items()}
    t_gnb = sb("t_gnb", [128, D], F32)
    t_gcols = sb("t_gcols", [64, 16 * NCH], F32)
    t_accF = sb("t_accF", [64, 1024], F32)
    t_uccF = sb("t_uccF", [64, 1024], F32)
    t_bab = sb("t_bab", [64, 1], F32)

    t_eps = sb("t_eps", [128, 1], F32)
    ev_con = [P.dma("sync", t_con[k][:, :], d_con[k]) for k in CON]
    ev_eps = P.step("vector", lambda e: e.memset(t_eps[:, :], 1e-5))
    ev_con.append(ev_eps)
    evc0 = ev_con[-1]
    evb = P.dma("sync", t_bab[:, :], d_bab)
    ev_bab = P.step("scalar", lambda e: e.mul(t_bab[:, :], t_bab[:, :], 0.5),
                    deps=[evb])

    a_ctx = ExitStack()

    def asb(name, shape, dtype):
        return a_ctx.enter_context(nc.sbuf_tensor(name, shape, dtype))

    t_wt = {n: asb("t_" + n, [128, 8 * D], BF16)
            for n in ("wq", "wk", "wv")}
    t_xnfm = asb("t_xnfm", [128, 8 * PT_], BF16)
    t_qn = asb("t_qn", [128, 2 * 8 * PT_], BF16)
    t_kn = asb("t_kn", [128, 2 * 8 * PT_], BF16)
    t_vb = asb("t_vb", [128, 2 * CPT * D], BF16)
    t_ktm = asb("t_ktm", [128, 2 * D], BF16)
    t_sqb = asb("t_sqb", [128, 8 * PT_], BF16)
    t_qbf = asb("t_qbf", [128, 8 * PT_], BF16)
    t_rn = asb("t_rn", [2, 1024], F32)
    t_rn2 = asb("t_rn2", [2, 1024], F32)
    t_rnb = asb("t_rnb", [2, 4 * PT_], BF16)
    sub_ctx = ExitStack()

    def ssb(name, shape, dtype):
        return sub_ctx.enter_context(nc.sbuf_tensor(name, shape, dtype))

    t_wt["wgt"] = ssb("t_wgt", [128, 8 * D], BF16)
    t_wab = ssb("t_wab", [128, 8 * 64], BF16)
    t_wld = ssb("t_wld", [128, 2 * D], F32)
    t_ffw = ssb("t_ffw", [128, 2 * D], I32)
    t_ffb = ssb("t_ffb", [128, 2 * D], BF16)
    t_gn1 = ssb("t_gn1", [1, D], F32)

    ev_g1 = P.dma("sync", t_gn1[:, :], d_gnorm)
    b0, bd = bank(2)
    ev_gm = [P.step("tensor", lambda e, i=i, b0=b0: e.matmul(
        psf(b0 + i, 0, 128, 0, 512), t_con["ones1"][:, :],
        t_gn1[:, 512 * i:512 * (i + 1)], start=True, stop=True),
        deps=[ev_g1, evc0] + bd) for i in range(2)]
    ev_gnb = P.step("vector", lambda e, b0=b0: e.tensor_copy(
        t_gnb[:, :], ps_t[0:128, 512 * b0:512 * b0 + 1024]), deps=ev_gm)
    bank_done(b0, 2, [ev_gnb])

    def _cut(dep_evs):
        for cix in range(NCH):
            P.dma("sync", d_out[cix * 128:(cix + 1) * 128, :], t_gnb[:, :],
                  deps=list(dep_evs) + [ev_gnb])
        P.finalize(block, sems)
        ctx.close()
        return nc

    wev = {}
    wl_ev = [None, None]
    for wkey, cols in [("wq", D), ("wk", D), ("wv", D), ("wgt", D), ("wab", 64)]:
        dst = t_wab if wkey == "wab" else t_wt[wkey]
        src = d_wab if wkey == "wab" else d_w[wkey]
        evs = []
        for kc in range(8):
            par = kc % 2
            ev1 = P.dma("gpsimd", t_wld[:, par * D:par * D + cols],
                        src[kc * 128:(kc + 1) * 128, :],
                        deps=[wl_ev[par]] if wl_ev[par] else [])
            evc = P.step("gpsimd", lambda e, kc=kc, par=par, dst=dst,
                         cols=cols:
                         e.tensor_copy(dst[:, kc * cols:(kc + 1) * cols],
                                       t_wld[:, par * D:par * D + cols]),
                         deps=[ev1])
            wl_ev[par] = evc
            evs.append(evc)
        wev[wkey] = evs

    ffn_cast_ev = []
    fc_ev = [None, None]
    fc_i = 0
    for (dsrc, ddst, rows, cols) in [(d_fg, s_wgbf, HID, D), (d_fu, s_wubf, HID, D),
                                     (d_fd, s_wdbf, D, HID)]:
        for r in range(rows // 128):
            for cc0 in range(cols // D):
                par = fc_i % 2
                fc_i += 1
                ev1 = P.dma("gpsimd", t_ffw[:, par * D:(par + 1) * D],
                            dsrc[r * 128:(r + 1) * 128, cc0 * D:(cc0 + 1) * D],
                            deps=[fc_ev[par]] if fc_ev[par] else [])
                ev2 = P.step("gpsimd", lambda e, par=par: e.tensor_copy(
                    t_ffb[:, par * D:(par + 1) * D],
                    t_ffw[:, par * D:(par + 1) * D]), deps=[ev1])
                ev3 = P.dma("gpsimd", ddst[r * 128:(r + 1) * 128,
                                           cc0 * D:(cc0 + 1) * D],
                            t_ffb[:, par * D:(par + 1) * D], deps=[ev2])
                fc_ev[par] = ev3
                ffn_cast_ev.append(ev3)

    t_x0 = ssb("t_x0", [128, 4 * D], F32)
    t_sq0 = ssb("t_sq0", [128, D], F32)
    t_st0 = ssb("t_st0", [128, 32], F32)
    t_xnb = ssb("t_xnb", [128, 4 * D], BF16)
    ev_xn_w = []
    prev_norms = []
    prev_stores = [None] * 4
    prev_e3s = None
    prev_e5 = None
    assert NCH % 4 == 0
    for gx in range(NCH // 4):
        loads = [P.dma("sync", t_x0[:, j * D:(j + 1) * D],
                       d_x[(gx * 4 + j) * 128:(gx * 4 + j + 1) * 128, :],
                       deps=prev_norms) for j in range(4)]
        sq = []
        for j in range(4):
            sq.append(P.step("scalar", lambda e, j=j: e.activation(
                t_sq0[:, :], t_x0[:, j * D:(j + 1) * D], AF.Square,
                accum_out=t_st0[:, j:j + 1]),
                deps=[loads[j]] + ([prev_e3s] if prev_e3s and j == 0 else []) +
                     ([sq[j - 1]] if j > 0 else [])))
        rd = [P.step("vector", lambda e, j=j: e.reduce_sum(
            t_st0[:, 4 + j:5 + j], t_x0[:, j * D:(j + 1) * D], AXX),
            deps=[loads[j]]) for j in range(4)]
        e1 = P.step("vector", lambda e: e.tensor_scalar_mul(
            t_st0[:, 8:12], t_st0[:, 4:8], 1.0 / D), deps=rd)
        e2 = P.step("vector", lambda e: e.tensor_mul(
            t_st0[:, 12:16], t_st0[:, 8:12], t_st0[:, 8:12]), deps=[e1])
        e3s = P.step("vector", lambda e: e.tensor_scalar_mul(
            t_st0[:, 16:20], t_st0[:, 0:4], 1.0 / D), deps=sq)
        e4 = P.step("vector", lambda e: e.tensor_sub(
            t_st0[:, 20:24], t_st0[:, 16:20], t_st0[:, 12:16]),
            deps=[e2, e3s] + ([prev_e5] if prev_e5 else []))
        e5 = P.step("scalar", lambda e: e.activation(
            t_st0[:, 24:28], t_st0[:, 20:24], AF.Ln, bias=t_eps[:, :]),
            deps=[e4])
        e6 = P.step("scalar", lambda e: e.activation(
            t_st0[:, 28:32], t_st0[:, 24:28], AF.Exp, scale=-0.5),
            deps=[e5] + prev_norms)
        norms = []
        for j in range(4):
            evn = P.step("vector", lambda e, j=j: e.tensor_scalar(
                t_xnb[:, j * D:(j + 1) * D], t_x0[:, j * D:(j + 1) * D],
                t_st0[:, 8 + j:9 + j], t_st0[:, 28 + j:29 + j],
                ALU.subtract, ALU.mult),
                deps=[e6] + ([prev_stores[j]] if prev_stores[j] else []))
            norms.append(evn)
            prev_stores[j] = P.dma(
                "sync", s_xn[(gx * 4 + j) * 128:(gx * 4 + j + 1) * 128, :],
                t_xnb[:, j * D:(j + 1) * D], deps=[evn])
            ev_xn_w.append(prev_stores[j])
        prev_norms = norms
        prev_e3s = e3s
        prev_e5 = e5

    if STAGE <= 1:
        return _cut(P.fence_all())
    t_abz = ssb("t_abz", [64, PT_], F32)
    t_ab = ssb("t_ab", [64, PT_], F32)
    t_gth = ssb("t_gth", [128, D], F32)
    t_gsg = ssb("t_gsg", [128, D], F32)
    t_gout = ssb("t_gout", [128, D], BF16)

    xnfm_guard = []
    ev_ab_tile = [None] * NPT
    ev_gate_w = [None] * NCH
    prev_gate_g = None
    prev_abz = None
    for ti in range(NPT):
        deps = [ev_xn_w[ti * CPT + i] for i in range(CPT)] + xnfm_guard
        xn_evs = [P.dma("sync", t_xnfm[:, dc * PT_:(dc + 1) * PT_],
                        s_xn[ti * PT_:(ti + 1) * PT_, dc * 128:(dc + 1) * 128],
                        deps=deps, transpose=True) for dc in range(8)]
        xnfm_guard = []
        b0, bd = bank(1)
        mme = [P.step("tensor", lambda e, kc=kc, b0=b0: e.matmul(
            psf(b0, 0, 64, 0, PT_), t_wab[:, kc * 64:(kc + 1) * 64],
            t_xnfm[:, kc * PT_:(kc + 1) * PT_], start=(kc == 0), stop=(kc == 7)),
            deps=[xn_evs[kc], wev["wab"][kc]] + (bd if kc == 0 else []))
            for kc in range(8)]
        ev_th = P.step("scalar", lambda e, b0=b0: e.activation(
            t_abz[:, :], psf(b0, 0, 64, 0, PT_), AF.Tanh,
            bias=t_bab[:, :], scale=0.5),
            deps=mme + [ev_bab] + ([prev_abz] if prev_abz else []))
        bank_done(b0, 1, [ev_th])
        pga = [ev_ab_tile[ti - 1]] if ti >= 1 else []
        eva_v = P.step("vector", lambda e: e.tensor_scalar(
            t_ab[:, :], t_abz[:, :], 0.5, 0.5,
            ALU.mult, ALU.add), deps=[ev_th] + pga)
        prev_abz = eva_v
        ev_ab_tile[ti] = P.dma("sync", s_ab[ti, :, :], t_ab[:, :],
                               deps=[eva_v])
        xnfm_guard = [mme[-1]]
        for ci in range(CPT):
            cix = ti * CPT + ci
            b0, bd = bank(2)
            mme = []
            for kc in range(8):
                for nh in range(2):
                    mme.append(P.step(
                        "tensor",
                        lambda e, kc=kc, nh=nh, b0=b0, ci=ci: e.matmul(
                            psf(b0 + nh, 0, 128, 0, 512),
                            t_xnfm[:, kc * PT_ + ci * 128:kc * PT_ + (ci + 1) * 128],
                            t_wt["wgt"][:, kc * D + nh * 512:kc * D + (nh + 1) * 512],
                            start=(kc == 0), stop=(kc == 7)),
                        deps=[xn_evs[kc], wev["wgt"][kc]] + (bd if kc == 0 else [])))
            gps = ps_t[0:128, 512 * b0:512 * b0 + 1024]
            pg = [prev_gate_g] if prev_gate_g else []
            pg2 = [ev_gate_w[cix - 1]] if cix >= 1 else []
            ev_t = P.step("scalar", lambda e, gps=gps: e.activation(
                t_gth[:, :], gps, AF.Tanh, scale=0.5), deps=mme + pg)
            ev_sg = P.step("vector", lambda e: e.tensor_scalar(
                t_gsg[:, :], t_gth[:, :], 0.5, 0.5, ALU.mult, ALU.add),
                deps=[ev_t] + pg)
            ev_g = P.step("vector", lambda e, gps=gps: e.tensor_mul(
                t_gout[:, :], gps, t_gsg[:, :]), deps=[ev_sg] + pg2)
            prev_gate_g = ev_g
            bank_done(b0, 2, [ev_g])
            ev_gate_w[cix] = P.dma("sync", s_gate[cix * 128:(cix + 1) * 128, :],
                                   t_gout[:, :], deps=[ev_g])
            xnfm_guard = [mme[-1]]

    if STAGE <= 2:
        return _cut(P.fence_all())
    a1_fence = P.fence_all()
    sub_ctx.close()
    t_abl = asb("t_abl", [64, 128], F32)
    t_kkdbg = (asb("t_kkdbg", [128, 512], F32) if DEBUG_SCRATCH else None)
    t_la = asb("t_la", [16, 128], F32)
    t_la2 = asb("t_la2", [16, 128], F32)
    t_z16 = asb("t_z16", [16, 128], F32)
    t_c = asb("t_c", [16, 128], F32)
    t_chb = asb("t_chb", [16, 128], BF16)
    t_chf = asb("t_chf", [16, 128], F32)
    t_clb = asb("t_clb", [16, 128], BF16)
    t_cprev = asb("t_cprev", [16, 128], F32)
    t_dend = asb("t_dend", [16, 128], F32)
    t_ecp = asb("t_ecp", [16, 128], F32)
    t_ecbf = asb("t_ecbf", [16, 128], BF16)
    t_tm1 = asb("t_tm1", [128, 80], F32)
    t_becptm = asb("t_becptm", [128, 16], F32)
    t_cath = asb("t_cath", [1, 512], BF16)
    t_catl = asb("t_catl", [1, 512], BF16)
    t_dbuf = asb("t_dbuf", [128, 1024], BF16)
    t_ebuf = asb("t_ebuf", [128, 512], F32)
    t_e1m = asb("t_e1m", [128, 512], F32)
    t_e2m = asb("t_e2m", [128, 512], F32)
    t_xb = asb("t_xb", [128, 2048], BF16)
    t_xp = [asb(f"t_xp{i}", [128, 2048], BF16) for i in range(2)]
    t_xpt = [asb(f"t_xpt{i}", [128, 2048], BF16) for i in range(2)]
    t_xtb = t_xpt[1]
    t_zb = [asb(f"t_zb{i}", [128, 2048], BF16) for i in range(2)]
    t_ftb = asb("t_ftb", [64, 2048], BF16)
    t_kdb = asb("t_kdb", [128, 1024], BF16)
    t_rb = asb("t_rb", [128, 2048], BF16)
    t_ptb = asb("t_ptb", [128, 2048], BF16)
    t_qtb = asb("t_qtb", [128, 1024], BF16)
    t_gib = asb("t_gib", [64, 1024], F32)
    t_atb = asb("t_atb", [64, 1024], BF16)
    t_ub = asb("t_ub", [64, 1024], BF16)
    t_acc = [asb(f"t_acc{i}", [64, 1024], BF16) for i in range(2)]
    t_acct = [asb(f"t_acct{i}", [64, 1024], BF16) for i in range(2)]
    t_ucc = [asb(f"t_ucc{i}", [64, 1024], BF16) for i in range(2)]

    ev_acc0 = P.step("vector", lambda e: e.tensor_copy(
        t_acc[1][:, :], t_con["i64rep_bf"][:, :]),
        deps=[evc0] + list(a1_fence))
    ev_acct0 = P.step("vector", lambda e: e.tensor_copy(
        t_acct[1][:, :], t_con["i64rep_bf"][:, :]), deps=[evc0])
    ev_ucc0 = P.step("vector", lambda e: e.memset(t_ucc[1][:, :], 0.0))
    ev_z16 = P.step("vector", lambda e: e.memset(t_z16[:, :], 0.0))
    compose_prev = (ev_acc0, ev_acct0, ev_ucc0)

    prev_rn = None
    tile_fence = [()] * NPT
    chunk_fence = a1_fence
    ev_chunk_done = [None] * NCH
    ev_gamma = [None] * NCH
    ev_kfm_w = [None] * NPT
    ktm_guard = [()] * 2
    eacctF = euccF = None

    for ti in range(NPT):
        par = ti % 2
        qof = par * 8 * PT_
        deps = [ev_xn_w[ti * CPT + i] for i in range(CPT)] + xnfm_guard
        xn_evs = [P.dma("sync", t_xnfm[:, dc * PT_:(dc + 1) * PT_],
                        s_xn[ti * PT_:(ti + 1) * PT_, dc * 128:(dc + 1) * 128],
                        deps=deps, transpose=True) for dc in range(8)]
        guard_prev = list(tile_fence[ti - 2]) if ti >= 2 else []
        tile_readers = []

        qn_apply = kn_apply = None
        for tname in ("q", "k"):
            wtile = t_wt["wq"] if tname == "q" else t_wt["wk"]
            wevs = wev["wq"] if tname == "q" else wev["wk"]
            dstb = t_qn if tname == "q" else t_kn
            sq_evs = []
            qbf_evs = []
            for oc in range(8):
                b0, bd = bank(1)
                mme = [P.step("tensor",
                              lambda e, oc=oc, kc=kc, b0=b0, wtile=wtile: e.matmul(
                    psf(b0, 0, 128, 0, PT_),
                    wtile[:, kc * D + oc * 128:kc * D + (oc + 1) * 128],
                    t_xnfm[:, kc * PT_:(kc + 1) * PT_],
                    start=(kc == 0), stop=(kc == 7)),
                    deps=[xn_evs[kc], wevs[kc]] + (bd if kc == 0 else []))
                    for kc in range(8)]
                e1 = P.step("vector", lambda e, oc=oc, b0=b0: e.tensor_copy(
                    t_qbf[:, oc * PT_:(oc + 1) * PT_], psf(b0, 0, 128, 0, PT_)),
                    deps=mme)
                e2 = P.step("scalar", lambda e, oc=oc: e.activation(
                    t_sqb[:, oc * PT_:(oc + 1) * PT_],
                    t_qbf[:, oc * PT_:(oc + 1) * PT_],
                    AF.Square), deps=[e1])
                bank_done(b0, 1, [e1])
                sq_evs.append(e2)
                qbf_evs.append(e1)
            nss = 8 * PT_ // 512
            b0, bd = bank(nss, hold=True)
            ssm = [P.step("tensor", lambda e, i4=i4, b0=b0: e.matmul(
                psf(b0 + i4, 0, 2, 0, 512), t_con["headones_bf"][:, :],
                t_sqb[:, 512 * i4:512 * (i4 + 1)], start=True, stop=True),
                deps=sq_evs + (bd if i4 == 0 else []) + [evc0])
                for i4 in range(nss)]
            eh4s = []
            applies = []
            hlen = 4 * PT_
            for hf in range(2):
                ssps = ps_t[0:2, 512 * b0 + hf * hlen:512 * b0 + (hf + 1) * hlen]
                eh1 = P.step("vector", lambda e, ssps=ssps, hf=hf:
                             e.tensor_scalar_max(
                    t_rn[:, 0:hlen], ssps, 1e-24),
                    deps=ssm + (eh4s if hf else []) +
                         ([prev_rn] if prev_rn else []))
                eh2 = P.step("scalar", lambda e: e.activation(
                    t_rn2[:, 0:hlen], t_rn[:, 0:hlen], AF.Ln), deps=[eh1])
                eh3 = P.step("scalar", lambda e: e.activation(
                    t_rn[:, 0:hlen], t_rn2[:, 0:hlen], AF.Exp, scale=-0.5),
                    deps=[eh2, eh1])
                eh4 = P.step("vector", lambda e, hf=hf: e.tensor_copy(
                    t_rnb[:, 0:hlen], t_rn[:, 0:hlen]),
                    deps=[eh3] + (applies[-2:] if hf else []))
                eh4s.append(eh4)
                prev_rn = eh3
                for oc in range(hf * 4, hf * 4 + 4):
                    b2, bd2 = bank(1)
                    em = P.step("tensor", lambda e, oc=oc, b2=b2, hf=hf: e.matmul(
                        psf(b2, 0, 128, 0, PT_), t_con["sel2t_bf"][:, :],
                        t_rnb[:, (oc - hf * 4) * PT_:(oc - hf * 4 + 1) * PT_],
                        start=True, stop=True),
                        deps=[eh4] + bd2)
                    ea = P.step("vector",
                                lambda e, oc=oc, b2=b2, dstb=dstb, qof=qof:
                                e.tensor_mul(
                        dstb[:, qof + oc * PT_:qof + (oc + 1) * PT_],
                        t_qbf[:, oc * PT_:(oc + 1) * PT_],
                        psf(b2, 0, 128, 0, PT_)),
                        deps=[em, qbf_evs[oc]] + (guard_prev if oc == 0 else []))
                    bank_done(b2, 1, [ea])
                    applies.append(ea)
            bank_done(b0, nss, eh4s)
            if tname == "q":
                qn_apply = applies
            else:
                kn_apply = applies

        ev_kfm_w[ti] = P.dma(
            "sync",
            s_kfm.rearrange("(dc p) t -> p dc t", p=128)[
                :, :, ti * PT_:(ti + 1) * PT_],
            t_kn[:, qof:qof + 8 * PT_].rearrange("p (dc t) -> p dc t", dc=8),
            deps=kn_apply)

        v_evs = []
        for ci in range(CPT):
            b0, bd = bank(2)
            mme = []
            for kc in range(8):
                for nh in range(2):
                    mme.append(P.step(
                        "tensor",
                        lambda e, kc=kc, nh=nh, b0=b0, ci=ci: e.matmul(
                            psf(b0 + nh, 0, 128, 0, 512),
                            t_xnfm[:, kc * PT_ + ci * 128:kc * PT_ + (ci + 1) * 128],
                            t_wt["wv"][:, kc * D + nh * 512:kc * D + (nh + 1) * 512],
                            start=(kc == 0), stop=(kc == 7)),
                        deps=[xn_evs[kc], wev["wv"][kc]] + (bd if kc == 0 else [])))
            ev = P.step("vector", lambda e, b0=b0, par=par, ci=ci: e.tensor_copy(
                t_vb[:, (par * CPT + ci) * D:(par * CPT + ci + 1) * D],
                ps_t[0:128, 512 * b0:512 * b0 + 1024]),
                deps=mme + (guard_prev if ci == 0 else []))
            bank_done(b0, 2, [ev])
            v_evs.append(ev)
        xnfm_guard = [mme[-1]]

        for ci in range(CPT if SUB >= 2 else 0):
            cix = ti * CPT + ci
            ks = cix % 2
            P.begin_fence()
            cf = list(chunk_fence)
            ev_ktm = P.dma("sync", t_ktm[:, ks * D:(ks + 1) * D],
                           s_kfm[:, cix * 128:(cix + 1) * 128],
                           deps=[ev_kfm_w[ti]] + list(ktm_guard[ks]),
                           transpose=True)

            if SUB <= 2:
                chunk_fence = P.end_fence()
                continue
            labl = P.dma("sync", t_abl[:, :],
                         s_ab[ti, :, ci * 128:(ci + 1) * 128],
                         deps=[ev_ab_tile[ti]] + cf)
            e1 = P.step("vector", lambda e: e.tensor_scalar_max(
                t_la[:, :], t_abl[0:16, :], 1e-37), deps=[labl] + cf)
            e2 = P.step("scalar", lambda e: e.activation(
                t_la2[:, :], t_la[:, :], AF.Ln), deps=[e1] + cf)
            e3 = P.step("vector", lambda e: e.tensor_tensor_scan(
                t_c[:, :], t_la2[:, :], t_z16[:, :], 0.0, ALU.add, ALU.add),
                deps=[e2, ev_z16])
            e4 = P.step("vector", lambda e: e.tensor_sub(
                t_cprev[:, :], t_c[:, :], t_la2[:, :]), deps=[e3])
            e6 = P.step("scalar", lambda e: e.activation(
                t_dend[:, :], t_c[:, :], AF.Exp,
                bias=t_c[:, 127:128], scale=-1.0), deps=[e3])
            e7 = P.step("scalar", lambda e: e.activation(
                t_ecp[:, :], t_cprev[:, :], AF.Exp), deps=[e4])
            e9 = P.step("scalar", lambda e: e.activation(
                t_ecbf[:, :], t_c[:, :], AF.Exp), deps=[e3])

            b0, bd = bank(2)
            trs = []
            for gi_, (tin, pb) in enumerate([(t_c[:, :], 0),
                                             (t_abl[32:48, :], 32),
                                             (t_cprev[:, :], 0),
                                             (t_dend[:, :], 0),
                                             (t_ecp[:, :], 0)]):
                ident = t_con["eye_f32"][pb:pb + 16, pb:pb + 16]
                dep_t = {0: [e3], 1: [labl], 2: [e4], 3: [e6],
                         4: [e7]}[gi_]
                bno = 1 if pb == 32 else 0
                trs.append(P.step("tensor",
                                  lambda e, gi_=gi_, tin=tin, ident=ident,
                                  b0=b0, bno=bno:
                                  e.transpose(
                    psf(b0 + bno, 0, 128, gi_ * 16, 16), tin, ident),
                    deps=dep_t + (bd if gi_ == 0 else []) + [evc0] + cf))
            e11a = P.step("vector", lambda e, b0=b0: e.tensor_copy(
                t_tm1[:, 0:16], psf(b0, 0, 128, 0, 16)),
                deps=[trs[0]] + cf)
            e11b = P.step("vector", lambda e, b0=b0: e.tensor_copy(
                t_tm1[:, 16:32], psf(b0 + 1, 0, 128, 16, 16)),
                deps=[trs[1]] + cf)
            e11c = P.step("vector", lambda e, b0=b0: e.tensor_copy(
                t_tm1[:, 32:80], psf(b0, 0, 128, 32, 48)),
                deps=trs + cf)
            e11 = P.step("vector", lambda e: e.engine_nop(),
                         deps=[e11a, e11b, e11c])
            bank_done(b0, 2, [e11a, e11b, e11c])
            e8 = P.step("vector", lambda e: e.tensor_mul(
                t_becptm[:, :], t_tm1[:, 16:32], t_tm1[:, 64:80]),
                deps=[e11] + cf)

            if DEBUG_SCRATCH:
                P.dma("sync", s_tmdbg[cix, :, :], t_tm1[:, :],
                      deps=[e11a, e11b, e11c, e8])
            ech = P.step("vector", lambda e: e.tensor_copy(
                t_chb[:, :], t_c[:, :]), deps=[e3] + cf)
            echf = P.step("vector", lambda e: e.tensor_copy(
                t_chf[:, :], t_chb[:, :]), deps=[ech])
            ecl = P.step("vector", lambda e: e.tensor_sub(
                t_clb[:, :], t_c[:, :], t_chf[:, :]), deps=[echf])
            e14a = P.dma("sync", s_cath[cix, :], t_chb[:, :], deps=[ech])
            e14b = P.dma("sync", s_catl[cix, :], t_clb[:, :], deps=[ecl])

            b0, bd = bank(1)
            gme = []
            for h in range(16):
                gme.append(P.step("tensor", lambda e, h=h, b0=b0: e.matmul(
                    ps_t[0:64, 512 * b0 + h:512 * b0 + h + 1],
                    t_con["selh_bf"][:, h * 64:(h + 1) * 64],
                    t_chb[:, 127:128], start=True, stop=False),
                    deps=[ech, evc0] + (bd if h == 0 else []) + cf))
                gme.append(P.step("tensor", lambda e, h=h, b0=b0: e.matmul(
                    ps_t[0:64, 512 * b0 + h:512 * b0 + h + 1],
                    t_con["selh_bf"][:, h * 64:(h + 1) * 64],
                    t_clb[:, 127:128], start=False, stop=True),
                    deps=[ecl]))
            ev_gamma[cix] = P.step("scalar", lambda e, b0=b0, cix=cix:
                                   e.activation(
                t_gcols[:, cix * 16:(cix + 1) * 16],
                ps_t[0:64, 512 * b0:512 * b0 + 16], AF.Exp), deps=gme)
            bank_done(b0, 1, [ev_gamma[cix]])

            if SUB <= 3:
                chunk_fence = P.end_fence()
                continue
            kd3 = t_kdb[:, :].rearrange("p (h e) -> p h e", h=16)
            ktm3 = t_ktm[:, ks * D:(ks + 1) * D].rearrange(
                "p (h e) -> p h e", h=16)
            dend_b = t_tm1[:, 48:64].unsqueeze(2).broadcast_to([128, 16, 64])
            e16 = P.step("vector", lambda e, kd3=kd3, ktm3=ktm3, dend_b=dend_b:
                         e.tensor_tensor(kd3, ktm3, dend_b, ALU.mult),
                         deps=[e11, ev_ktm] + cf)
            r4 = t_rb[:, :].rearrange("p (h t e) -> p h t e", h=16, t=2)
            v3 = t_vb[:, (par * CPT + ci) * D:(par * CPT + ci + 1) * D].rearrange(
                "p (h e) -> p h e", h=16)
            b_b = t_tm1[:, 16:32].unsqueeze(2).broadcast_to([128, 16, 64])
            becp_b = t_becptm[:, :].unsqueeze(2).broadcast_to([128, 16, 64])
            e17 = P.step("vector", lambda e, r4=r4, v3=v3, b_b=b_b:
                         e.tensor_tensor(r4[:, :, 0, :], v3, b_b, ALU.mult),
                         deps=[e11, v_evs[ci]] + cf)
            e18 = P.step("vector", lambda e, r4=r4, ktm3=ktm3, becp_b=becp_b:
                         e.tensor_tensor(r4[:, :, 1, :], ktm3, becp_b, ALU.mult),
                         deps=[e8, ev_ktm] + cf)
            tile_readers += [e16, e17, e18]

            x_evs = [None] * 16
            pt_evs = []
            prev_ecb = None
            prev_ee3 = None
            prev_ee4 = None
            prev_xg_gp = []
            prev_ptg_gp = []
            for g in range(4):
                e15a = P.dma("sync", t_cath[:, :],
                             s_cath[cix:cix + 1, g * 512:(g + 1) * 512],
                             deps=[e14a] + cf + ([prev_ecb] if prev_ecb else []))
                e15b = P.dma("sync", t_catl[:, :],
                             s_catl[cix:cix + 1, g * 512:(g + 1) * 512],
                             deps=[e14b] + cf + ([prev_ecb] if prev_ecb else []))
                bcb, bdc = bank(1)
                ecb1 = P.step("tensor", lambda e, g=g, bcb=bcb: e.matmul(
                    psf(bcb, 0, 128, 0, 512), t_con["ones1_bf"][:, :],
                    t_cath[:, :], start=True, stop=False),
                    deps=[e15a, evc0] + bdc + cf)
                ecb = P.step("tensor", lambda e, g=g, bcb=bcb: e.matmul(
                    psf(bcb, 0, 128, 0, 512), t_con["ones1_bf"][:, :],
                    t_catl[:, :], start=False, stop=True),
                    deps=[e15b, ecb1])
                prev_ecb = ecb
                d_evs = [P.step("vector", lambda e, hh=hh, h=g * 4 + hh, bcb=bcb:
                                e.tensor_scalar(
                    t_dbuf[:, hh * 128:(hh + 1) * 128],
                    psf(bcb, 0, 128, hh * 128, 128),
                    t_tm1[:, 32 + h:33 + h], 0.0, ALU.subtract, ALU.max),
                    deps=[ecb, e11] + ([prev_ee3] if prev_ee3 else []))
                    for hh in range(4)]
                if SUB <= 32:
                    continue
                ee1 = P.step("scalar", lambda e: e.activation(
                    t_ebuf[:, :], t_dbuf[:, 0:512], AF.Exp, scale=-1.0),
                    deps=d_evs + ([prev_ee4] if prev_ee4 else []))
                nl_b = t_con["neglmask_bf"][:, :].unsqueeze(1).broadcast_to(
                    [128, 4, 128])
                e3d = t_e1m[:, :].rearrange("p (hh t) -> p hh t", hh=4)
                eb3 = t_ebuf[:, :].rearrange("p (hh t) -> p hh t", hh=4)
                ee2 = P.step("vector", lambda e, e3d=e3d, eb3=eb3, nl_b=nl_b:
                             e.tensor_tensor(e3d, eb3, nl_b, ALU.mult),
                             deps=[ee1, evc0] + prev_xg_gp)
                d2_evs = [P.step("vector", lambda e, hh=hh, h=g * 4 + hh, bcb=bcb:
                                 e.tensor_scalar(
                    t_dbuf[:, hh * 128:(hh + 1) * 128],
                    psf(bcb, 0, 128, hh * 128, 128),
                    t_tm1[:, 0 + h:1 + h], 0.0, ALU.subtract, ALU.min),
                    deps=[ecb, e11, ee1]) for hh in range(4)]
                ee3 = P.step("scalar", lambda e: e.activation(
                    t_ebuf[:, :], t_dbuf[:, 0:512], AF.Exp),
                    deps=d2_evs + [ee2])
                bank_done(bcb, 1, [ee3])
                um_b = t_con["umask_bf"][:, :].unsqueeze(1).broadcast_to(
                    [128, 4, 128])
                e4d = t_e2m[:, :].rearrange("p (hh t) -> p hh t", hh=4)
                ee4 = P.step("vector", lambda e, e4d=e4d, eb3=eb3, um_b=um_b:
                             e.tensor_tensor(e4d, eb3, um_b, ALU.mult),
                             deps=[ee3] + prev_ptg_gp)
                prev_ee3 = ee3
                prev_ee4 = ee4

                if SUB <= 33:
                    continue
                if DEBUG_SCRATCH:
                    P.dma("sync", s_ddbg[cix, g, :, :].bitcast(BF16)[:, 0:512],
                          t_dbuf[:, 0:512], deps=d2_evs)
                    P.dma("sync", s_emdbg[cix, g, :, :], t_e1m[:, :],
                          deps=[ee2])
                bkk, bdk = bank(2)
                kk_evs = []
                for hh in range(4):
                    h = g * 4 + hh
                    pb = 64 * (h % 2)
                    dcq = h // 2
                    sl_k = t_kn[pb:pb + 64, qof + dcq * PT_ + ci * 128:
                                qof + dcq * PT_ + (ci + 1) * 128]
                    kk_evs.append(P.step("tensor",
                                         lambda e, hh=hh, bkk=bkk, sl_k=sl_k:
                                         e.matmul(
                        psf(bkk + hh % 2, 0, 128, (hh // 2) * 128, 128),
                        sl_k, sl_k, start=True, stop=True),
                        deps=kn_apply + (bdk if hh == 0 else [])))
                if SUB <= 335:
                    bank_done(bkk, 2, kk_evs)
                    continue
                if DEBUG_SCRATCH:
                    ekkd = P.step("vector", lambda e, bkk=bkk: e.tensor_copy(
                        t_kkdbg[:, :],
                        ps_t[0:128, 512 * bkk:512 * bkk + 512]),
                        deps=kk_evs)
                    edma1 = P.dma("sync", s_kkdbg[cix, g, :, :], t_kkdbg[:, :],
                                  deps=[ekkd])
                xg = []
                dbuf_bf = t_dbuf[:, 512:1024]
                for hh in range(4):
                    h = g * 4 + hh
                    if SUB == 336:
                        exg = P.step("vector", lambda e, hh=hh, h=h, bkk=bkk:
                                     e.tensor_copy(
                            t_xb[:, h * 128:(h + 1) * 128],
                            psf(bkk + hh % 2, 0, 128, (hh // 2) * 128, 128)),
                            deps=kk_evs + [ee2, e11])
                    elif SUB == 337:
                        exg = P.step("vector", lambda e, hh=hh, h=h, bkk=bkk:
                                     e.scalar_tensor_tensor(
                            t_xb[:, h * 128:(h + 1) * 128],
                            psf(bkk + hh % 2, 0, 128, (hh // 2) * 128, 128),
                            1.0, t_e1m[:, hh * 128:(hh + 1) * 128],
                            ALU.mult, ALU.mult),
                            deps=kk_evs + [ee2, e11])
                    elif SUB == 338:
                        exg = P.step("vector", lambda e, hh=hh, h=h, bkk=bkk:
                                     e.tensor_scalar(
                            t_xb[:, h * 128:(h + 1) * 128],
                            psf(bkk + hh % 2, 0, 128, (hh // 2) * 128, 128),
                            t_tm1[:, 16 + h:17 + h], None, ALU.mult),
                            deps=kk_evs + [ee2, e11])
                    else:
                        ex1 = P.step("vector",
                                     lambda e, hh=hh, h=h, bkk=bkk,
                                     dbuf_bf=dbuf_bf: e.tensor_scalar(
                            dbuf_bf[:, hh * 128:(hh + 1) * 128],
                            psf(bkk + hh % 2, 0, 128, (hh // 2) * 128, 128),
                            t_tm1[:, 16 + h:17 + h], None, ALU.mult),
                            deps=kk_evs + [ee3, e11])
                        exg = P.step("gpsimd",
                                     lambda e, hh=hh, h=h, dbuf_bf=dbuf_bf:
                                     e.tensor_tensor(
                            t_xb[:, h * 128:(h + 1) * 128],
                            dbuf_bf[:, hh * 128:(hh + 1) * 128],
                            t_e1m[:, hh * 128:(hh + 1) * 128], ALU.mult),
                            deps=[ex1, ee2])
                    xg.append(exg)
                    x_evs[h] = exg
                if DEBUG_SCRATCH:
                    ekkd2 = P.step("vector", lambda e, bkk=bkk: e.tensor_copy(
                        t_kkdbg[:, :],
                        ps_t[0:128, 512 * bkk:512 * bkk + 512]),
                        deps=xg + [edma1])
                    edma2 = P.dma("sync", s_kkdbg2[cix, g, :, :], t_kkdbg[:, :],
                                  deps=[ekkd2])
                    xg = xg + [ekkd2, edma2]
                bank_done(bkk, 2, xg)
                prev_xg_gp = [xg[-1]]
                tile_readers.append(kk_evs[-1])

                if SUB <= 34:
                    continue
                bkq, bdq = bank(2)
                kq_evs = []
                for hh in range(4):
                    h = g * 4 + hh
                    pb = 64 * (h % 2)
                    dcq = h // 2
                    sl_k = t_kn[pb:pb + 64, qof + dcq * PT_ + ci * 128:
                                qof + dcq * PT_ + (ci + 1) * 128]
                    sl_q = t_qn[pb:pb + 64, qof + dcq * PT_ + ci * 128:
                                qof + dcq * PT_ + (ci + 1) * 128]
                    kq_evs.append(P.step(
                        "tensor",
                        lambda e, hh=hh, bkq=bkq, sl_k=sl_k, sl_q=sl_q: e.matmul(
                            psf(bkq + hh % 2, 0, 128, (hh // 2) * 128, 128),
                            sl_k, sl_q, start=True, stop=True),
                        deps=kn_apply + qn_apply + (bdq if hh == 0 else [])))
                ptg = []
                for hh in range(4):
                    h = g * 4 + hh
                    pp1 = P.step("vector",
                                 lambda e, hh=hh, bkq=bkq, dbuf_bf=dbuf_bf:
                                 e.tensor_copy(
                        dbuf_bf[:, hh * 128:(hh + 1) * 128],
                        psf(bkq + hh % 2, 0, 128, (hh // 2) * 128, 128)),
                        deps=kq_evs + xg)
                    ptg.append(P.step("gpsimd",
                                      lambda e, hh=hh, h=h, dbuf_bf=dbuf_bf:
                                      e.tensor_tensor(
                        t_ptb[:, h * 128:(h + 1) * 128],
                        dbuf_bf[:, hh * 128:(hh + 1) * 128],
                        t_e2m[:, hh * 128:(hh + 1) * 128], ALU.mult),
                        deps=[pp1, ee4]))
                pt_evs += ptg
                prev_ptg_gp = [ptg[-1]]
                bank_done(bkq, 2, ptg)
                tile_readers.append(kq_evs[-1])

            if SUB <= 40:
                chunk_fence = P.end_fence()
                continue
            if DEBUG_SCRATCH:
                P.dma("sync", s_xdbg[cix, :, :], t_xb[:, :],
                      deps=[e for e in x_evs])
            xt_evs = []
            for half in range(2):
                tevs = [P.step("tensor", lambda e, h=half * 8 + hh, hh=hh:
                               e.transpose(
                    psb_t[0:128, hh * 128:(hh + 1) * 128],
                    t_xb[:, h * 128:(h + 1) * 128], t_con["eye_bf"][:, :]),
                    deps=[x_evs[half * 8 + hh], evc0] +
                         (list(psb_free[0]) if hh == 0 else []))
                    for hh in range(8)]
                exv = P.step("vector", lambda e, half=half: e.tensor_copy(
                    t_xtb[:, half * 1024:(half + 1) * 1024],
                    psb_t[0:128, 0:1024]), deps=tevs)
                psb_free[0] = (exv,)
                xt_evs.append(exv)

            zpar = 0
            zev = [None, None]
            xsrc, xtsrc = t_xb, t_xtb
            xsrc_ev = [ev for ev in x_evs]
            xtsrc_ev = xt_evs
            asrc_first = [e17, e18]
            for lev in range(NL):
                asrc = t_rb if lev == 0 else t_zb[1 - zpar]
                asrc_ev = asrc_first if lev == 0 else [zev[1 - zpar]]
                for half in range(2):
                    ba, bda = bank(2)
                    mms = [P.step(
                        "tensor",
                        lambda e, h=half * 8 + hh, hh=hh, ba=ba, asrc=asrc,
                        xtsrc=xtsrc: e.matmul(
                            ps_t[0:128,
                                 512 * ba + hh * 128:512 * ba + (hh + 1) * 128],
                            xtsrc[:, h * 128:(h + 1) * 128],
                            asrc[:, h * 128:(h + 1) * 128],
                            start=True, stop=True),
                        deps=(xtsrc_ev if hh == 0 else []) +
                             (asrc_ev if hh == 0 else []) +
                             (bda if hh == 0 else []))
                        for hh in range(8)]
                    eva = P.step("vector",
                                 lambda e, half=half, ba=ba, asrc=asrc, zp=zpar:
                                 e.scalar_tensor_tensor(
                        t_zb[zp][:, half * 1024:(half + 1) * 1024],
                        ps_t[0:128, 512 * ba:512 * ba + 1024], 1.0,
                        asrc[:, half * 1024:(half + 1) * 1024],
                        ALU.mult, ALU.add), deps=mms + asrc_ev +
                        ([zev[zpar]] if zev[zpar] else []))
                    bank_done(ba, 2, [eva])
                    zev[zpar] = eva
                zpar = 1 - zpar

                if lev < NL - 1:
                    xdst = t_xp[lev % 2]
                    xtdst = t_xpt[lev % 2]
                    new_x_ev = []
                    new_xt_ev = []
                    for which in range(2):
                        lop = xtsrc if which == 0 else xsrc
                        rop = xsrc if which == 0 else xtsrc
                        dst = xdst if which == 0 else xtdst
                        for half in range(2):
                            bs, bds = bank(2)
                            mms = [P.step(
                                "tensor",
                                lambda e, h=half * 8 + hh, hh=hh, bs=bs,
                                lop=lop, rop=rop: e.matmul(
                                    ps_t[0:128, 512 * bs + hh * 128:
                                         512 * bs + (hh + 1) * 128],
                                    lop[:, h * 128:(h + 1) * 128],
                                    rop[:, h * 128:(h + 1) * 128],
                                    start=True, stop=True),
                                deps=((xsrc_ev + xtsrc_ev) if hh == 0 else []) +
                                     (bds if hh == 0 else []))
                                for hh in range(8)]
                            if which == 0:
                                ev2 = P.step("vector",
                                             lambda e, half=half, bs=bs, dst=dst:
                                             e.tensor_copy(
                                    dst[:, half * 1024:(half + 1) * 1024],
                                    ps_t[0:128, 512 * bs:512 * bs + 1024]),
                                    deps=mms)
                                new_x_ev.append(ev2)
                            else:
                                ev2 = P.step("scalar",
                                             lambda e, half=half, bs=bs, dst=dst:
                                             e.copy(
                                    dst[:, half * 1024:(half + 1) * 1024],
                                    ps_t[0:128, 512 * bs:512 * bs + 1024]),
                                    deps=mms)
                                new_xt_ev.append(ev2)
                            bank_done(bs, 2, [ev2])
                    xsrc, xtsrc = xdst, xtdst
                    xsrc_ev, xtsrc_ev = new_x_ev, new_xt_ev
            zfin = t_zb[1 - zpar]
            zfin_ev = zev[1 - zpar]

            if SUB <= 5:
                chunk_fence = P.end_fence()
                continue
            ft_half = []
            for half in range(2):
                ftevs = [P.step("tensor", lambda e, h=half * 8 + hh, hh=hh,
                                zfin=zfin: e.transpose(
                    psb_t[0:64, hh * 128:(hh + 1) * 128],
                    zfin[:, h * 128 + 64:(h + 1) * 128], t_con["eye_bf"][:, :]),
                    deps=[zfin_ev, evc0] +
                         (list(psb_free[0]) if hh == 0 else []))
                    for hh in range(8)]
                efth = P.step("vector", lambda e, half=half: e.tensor_copy(
                    t_ftb[:, half * 1024:(half + 1) * 1024],
                    psb_t[0:64, 0:1024]), deps=ftevs)
                psb_free[0] = (efth,)
                ft_half.append(efth)
            eft = ft_half[-1]

            bq, bdq2 = bank(2)
            sel_evs = [P.step("tensor", lambda e, j=j, bq=bq: e.matmul(
                psf(bq, 0, 128, 0, 1024)[0:128, j * 128:(j + 1) * 128]
                if False else
                ps_t[0:128, 512 * bq + j * 128:512 * bq + (j + 1) * 128],
                t_con["selj_bf"][:, j * 128:(j + 1) * 128],
                t_ecbf[:, :], start=True, stop=True),
                deps=[e9, evc0] + (bdq2 if j == 0 else []))
                for j in range(8)]
            qn3 = t_qn[:, qof:qof + 8 * PT_].rearrange(
                "p (dc t) -> p dc t", dc=8)[:, :, ci * 128:(ci + 1) * 128]
            eqt = P.step("vector", lambda e, bq=bq, qn3=qn3: e.tensor_tensor(
                t_qtb[:, :].rearrange("p (dc t) -> p dc t", dc=8), qn3,
                ps_t[0:128, 512 * bq:512 * bq + 1024].rearrange(
                    "p (dc t) -> p dc t", dc=8), ALU.mult),
                deps=sel_evs + qn_apply)
            bank_done(bq, 2, [eqt])
            tile_readers.append(eqt)

            if SUB <= 6:
                chunk_fence = P.end_fence()
                continue
            bgg, bdg = bank(2)
            g_evs = [P.step("tensor", lambda e, h=h, bgg=bgg, zfin=zfin:
                            e.matmul(
                ps_t[0:64, 512 * bgg + h * 64:512 * bgg + (h + 1) * 64],
                zfin[:, h * 128 + 64:(h + 1) * 128],
                t_kdb[:, h * 64:(h + 1) * 64], start=True, stop=True),
                deps=[zfin_ev, e16] + (bdg if h == 0 else []))
                for h in range(16)]
            i64b = t_con["i64_f32"][:, :].unsqueeze(1).broadcast_to([64, 16, 64])
            gcb = t_gcols[:, cix * 16:(cix + 1) * 16].unsqueeze(2).broadcast_to(
                [64, 16, 64])
            egi = P.step("vector", lambda e, i64b=i64b, gcb=gcb: e.tensor_tensor(
                t_gib[:, :].rearrange("p (h e) -> p h e", h=16), i64b, gcb,
                ALU.mult), deps=[ev_gamma[cix], evc0] + cf)
            eat = P.step("vector", lambda e, bgg=bgg: e.scalar_tensor_tensor(
                t_atb[:, :], ps_t[0:64, 512 * bgg:512 * bgg + 1024], -1.0,
                t_gib[:, :], ALU.mult, ALU.add), deps=g_evs + [egi])
            bank_done(bgg, 2, [eat])
            bgu, bdu = bank(2)
            u_evs = [P.step("tensor", lambda e, h=h, bgu=bgu, zfin=zfin:
                            e.matmul(
                ps_t[0:64, 512 * bgu + h * 64:512 * bgu + (h + 1) * 64],
                t_kdb[:, h * 64:(h + 1) * 64],
                zfin[:, h * 128:h * 128 + 64], start=True, stop=True),
                deps=[zfin_ev, e16] + (bdu if h == 0 else []))
                for h in range(16)]
            eu = P.step("vector", lambda e, bgu=bgu: e.tensor_copy(
                t_ub[:, :], ps_t[0:64, 512 * bgu:512 * bgu + 1024]), deps=u_evs)
            bank_done(bgu, 2, [eu])

            cp = cix % 2
            cq = 1 - cp
            pa, pat, pu = compose_prev
            last = (cix == NCH - 1)
            bc1, bdc1 = bank(2)
            acc_m = [P.step("tensor", lambda e, h=h, bc1=bc1, cq=cq: e.matmul(
                ps_t[0:64, 512 * bc1 + h * 64:512 * bc1 + (h + 1) * 64],
                t_atb[:, h * 64:(h + 1) * 64],
                t_acc[cq][:, h * 64:(h + 1) * 64], start=True, stop=True),
                deps=[eat, pa] + (bdc1 if h == 0 else []))
                for h in range(16)]
            eacc = P.step("vector", lambda e, bc1=bc1, cp=cp: e.tensor_copy(
                t_acc[cp][:, :], ps_t[0:64, 512 * bc1:512 * bc1 + 1024]),
                deps=acc_m + [pa])
            bank_done(bc1, 2, [eacc])
            bc2, bdc2 = bank(2)
            acct_m = [P.step("tensor", lambda e, h=h, bc2=bc2, cq=cq: e.matmul(
                ps_t[0:64, 512 * bc2 + h * 64:512 * bc2 + (h + 1) * 64],
                t_acc[cq][:, h * 64:(h + 1) * 64],
                t_atb[:, h * 64:(h + 1) * 64], start=True, stop=True),
                deps=[eat, pa, pat] + (bdc2 if h == 0 else []))
                for h in range(16)]
            eacct = P.step("vector", lambda e, bc2=bc2, cp=cp: e.tensor_copy(
                t_acct[cp][:, :], ps_t[0:64, 512 * bc2:512 * bc2 + 1024]),
                deps=acct_m + [pat])
            evl2 = [eacct]
            if last:
                eacctF = P.step("vector", lambda e, bc2=bc2: e.tensor_copy(
                    t_accF[:, :], ps_t[0:64, 512 * bc2:512 * bc2 + 1024]),
                    deps=acct_m)
                evl2.append(eacctF)
            bank_done(bc2, 2, evl2)
            bc3, bdc3 = bank(2)
            ucc_m = []
            for h in range(16):
                ucc_m.append(P.step("tensor",
                                    lambda e, h=h, bc3=bc3, cq=cq: e.matmul(
                    ps_t[0:64, 512 * bc3 + h * 64:512 * bc3 + (h + 1) * 64],
                    t_atb[:, h * 64:(h + 1) * 64],
                    t_ucc[cq][:, h * 64:(h + 1) * 64], start=True, stop=False),
                    deps=[eat, pu] + (bdc3 if h == 0 else [])))
                ucc_m.append(P.step("tensor", lambda e, h=h, bc3=bc3: e.matmul(
                    ps_t[0:64, 512 * bc3 + h * 64:512 * bc3 + (h + 1) * 64],
                    t_con["i64rep_bf"][:, 0:64],
                    t_ub[:, h * 64:(h + 1) * 64], start=False, stop=True),
                    deps=[eu]))
            eucc = P.step("vector", lambda e, bc3=bc3, cp=cp: e.tensor_copy(
                t_ucc[cp][:, :], ps_t[0:64, 512 * bc3:512 * bc3 + 1024]),
                deps=ucc_m + [pu])
            evl3 = [eucc]
            if last:
                euccF = P.step("vector", lambda e, bc3=bc3: e.tensor_copy(
                    t_uccF[:, :], ps_t[0:64, 512 * bc3:512 * bc3 + 1024]),
                    deps=ucc_m)
                evl3.append(euccF)
            bank_done(bc3, 2, evl3)
            compose_prev = (eacc, eacct, eucc)

            w1 = P.dma("sync", s_z[cix, :, :], zfin[:, :], deps=[zfin_ev])
            w2 = P.dma("sync", s_ft[cix, :, :], t_ftb[:, :], deps=[eft])
            w3 = P.dma("sync", s_kd[cix, :, :], t_kdb[:, :], deps=[e16])
            w4 = P.dma("sync", s_pt[cix, :, :], t_ptb[:, :], deps=pt_evs)
            w5 = P.dma("sync",
                       s_qt[cix].rearrange("j p t -> p j t"),
                       t_qtb[:, :].rearrange("p (j t) -> p j t", j=8),
                       deps=[eqt])
            ev_chunk_done[cix] = (w1, w2, w3, w4, w5)
            chunk_fence = P.end_fence()
            ktm_guard[ks] = (e16, e18, w3)
        tile_fence[ti] = tuple(tile_readers)

    if STAGE <= 3 or SUB < 99:
        return _cut(P.fence_all())
    a_fence = P.fence_all()
    a_ctx.close()

    gb_ctx = ExitStack()
    t_s0 = gb_ctx.enter_context(nc.sbuf_tensor("t_s0", [64, 1024], F32))
    t_s0b = gb_ctx.enter_context(nc.sbuf_tensor("t_s0b", [64, 1024], BF16))
    g_ctx = ExitStack()
    t_gat = g_ctx.enter_context(nc.sbuf_tensor("t_gat", [64, 4096], F32))
    t_gau = g_ctx.enter_context(nc.sbuf_tensor("t_gau", [64, 4096], F32))
    t_sst = g_ctx.enter_context(nc.sbuf_tensor("t_sst", [64, 16 * 256], F32))
    t_selv = g_ctx.enter_context(nc.sbuf_tensor("t_selv", [1, 4], F32))
    t_sel64 = g_ctx.enter_context(nc.sbuf_tensor("t_sel64", [64, 4], F32))

    agi_at = s_agi.rearrange("(h ty r) e -> h ty r e", h=16, ty=2)
    ew1 = P.dma("sync", agi_at[:, 0, :, :].rearrange("h r e -> r h e"),
                t_accF[:, :].rearrange("r (h e) -> r h e", h=16),
                deps=[eacctF] + list(a_fence))
    ew2 = P.dma("sync", agi_at[:, 1, :, :].rearrange("h r e -> r h e"),
                t_uccF[:, :].rearrange("r (h e) -> r h e", h=16),
                deps=[euccF])
    ecc = P.step("gpsimd", lambda e: e.collective_compute(
        "AllGather", ALU.bypass,
        replica_groups=[[0, 1, 2, 3], [4, 5, 6, 7]],
        ins=[s_agi.opt()], outs=[s_ago.opt()]), deps=[ew1, ew2],
        semkey="scc")
    ago5 = s_ago.rearrange("(q h ty r) e -> q h ty r e", q=4, h=16, ty=2)
    gl_evs = []
    for q in range(4):
        gl_evs.append(P.dma(
            "sync",
            t_gat[:, q * 1024:(q + 1) * 1024].rearrange(
                "r (h e) -> r h e", h=16),
            ago5[q, :, 0, :, :].rearrange("h r e -> r h e"), deps=[ecc]))
        gl_evs.append(P.dma(
            "sync",
            t_gau[:, q * 1024:(q + 1) * 1024].rearrange(
                "r (h e) -> r h e", h=16),
            ago5[q, :, 1, :, :].rearrange("h r e -> r h e"), deps=[ecc]))
    ev_selv = P.dma("sync", t_selv[:, :], d_selv)
    bsl, bdsl = bank(1)
    esm = P.step("tensor", lambda e, bsl=bsl: e.matmul(
        ps_t[0:64, 512 * bsl:512 * bsl + 4], t_con["ones1"][0:1, 0:64],
        t_selv[:, :], start=True, stop=True), deps=[ev_selv, evc0] + bdsl)
    esel = P.step("vector", lambda e, bsl=bsl: e.tensor_copy(
        t_sel64[:, :], ps_t[0:64, 512 * bsl:512 * bsl + 4]), deps=[esm])
    bank_done(bsl, 1, [esel])

    sst5 = t_sst[:, :].rearrange("p (h e q) -> p h e q", h=16, q=4)
    bstage, bdst = bank(4, hold=True)
    ev_s0 = []
    prev_copy = None
    for h in range(16):
        prev_ev = prev_copy
        for q in range(4):
            pslot = ps_t[0:64, 512 * bstage + h * 64:512 * bstage + (h + 1) * 64]
            if q == 0:
                m1 = P.step("tensor", lambda e, h=h, pslot=pslot: e.matmul(
                    pslot, t_con["i64_f32"][:, :],
                    t_gau[:, h * 64:(h + 1) * 64],
                    start=True, stop=True),
                    deps=gl_evs + bdst + [evc0] +
                         ([prev_ev] if prev_ev else []))
            else:
                sst_prev = sst5[:, h, :, q - 1]
                m0 = P.step("tensor",
                            lambda e, h=h, q=q, pslot=pslot, sst_prev=sst_prev:
                            e.matmul(
                    pslot, t_gat[:, q * 1024 + h * 64:q * 1024 + (h + 1) * 64],
                    sst_prev, start=True, stop=False),
                    deps=gl_evs + [prev_ev])
                m1 = P.step("tensor", lambda e, h=h, q=q, pslot=pslot: e.matmul(
                    pslot, t_con["i64_f32"][:, :],
                    t_gau[:, q * 1024 + h * 64:q * 1024 + (h + 1) * 64],
                    start=False, stop=True), deps=[])
            prev_ev = P.step("vector", lambda e, pslot=pslot, h=h, q=q:
                             e.tensor_copy(sst5[:, h, :, q], pslot), deps=[m1])
        prev_copy = prev_ev
        ev_s0.append(prev_ev)
    bank_done(bstage, 4, ev_s0)
    sst4 = t_sst[:, :].rearrange("p (he q) -> p he q", q=4)
    sel_b = t_sel64[:, :].unsqueeze(1).broadcast_to([64, 1024, 4])
    emask = P.step("vector", lambda e, sst4=sst4, sel_b=sel_b: e.tensor_tensor(
        sst4, sst4, sel_b, ALU.mult), deps=ev_s0 + [esel])
    ered = P.step("vector", lambda e, sst4=sst4: e.tensor_reduce(
        t_s0[:, :].unsqueeze(2), sst4, AXX, ALU.add), deps=[emask])
    ev_s0f = P.step("vector", lambda e: e.tensor_copy(t_s0b[:, :], t_s0[:, :]),
                    deps=[ered])
    if STAGE <= 4:
        return _cut(P.fence_all())
    g_fence = P.fence_all()
    g_ctx.close()

    b_ctx = ExitStack()

    def bsb(name, shape, dtype):
        return b_ctx.enter_context(nc.sbuf_tensor(name, shape, dtype))

    t_wo = bsb("t_wo", [128, 8 * D], BF16)
    t_wld2 = bsb("t_wld2", [128, D], F32)
    t_zl = [bsb(f"t_zl{i}", [128, 2048], BF16) for i in range(2)]
    t_ftl = [bsb(f"t_ftl{i}", [64, 2048], BF16) for i in range(2)]
    t_kdl = [bsb(f"t_kdl{i}", [128, 1024], BF16) for i in range(2)]
    t_ptl = [bsb(f"t_ptl{i}", [128, 2048], BF16) for i in range(2)]
    t_qtl = [bsb(f"t_qtl{i}", [64, 2048], BF16) for i in range(2)]
    t_gl = [bsb(f"t_gl{i}", [128, D], BF16) for i in range(2)]
    t_wh = bsb("t_wh", [128, 1024], BF16)
    t_sg = bsb("t_sg", [64, 1024], BF16)
    t_sf = bsb("t_sf", [64, 1024], F32)
    t_sb2 = bsb("t_sb2", [64, 1024], BF16)
    t_ytm = bsb("t_ytm", [128, D], F32)
    t_ysq = bsb("t_ysq", [128, D], F32)
    t_yst = bsb("t_yst", [128, 4], F32)
    t_yout = bsb("t_yout", [128, D], BF16)

    woev = []
    wo_par = None
    for kc in range(8):
        ev1 = P.dma("sync", t_wld2[:, :], d_w["wo"][kc * 128:(kc + 1) * 128, :],
                    deps=[wo_par] if wo_par else list(g_fence))
        evcp = P.step("scalar", lambda e, kc=kc: e.copy(
            t_wo[:, kc * D:(kc + 1) * D], t_wld2[:, :]), deps=[ev1])
        wo_par = evcp
        woev.append(evcp)

    ev_sf = P.step("vector", lambda e: e.tensor_copy(t_sf[:, :], t_s0[:, :]),
                   deps=[ered] + list(g_fence))
    ev_sb = P.step("vector", lambda e: e.tensor_copy(t_sb2[:, :], t_s0b[:, :]),
                   deps=[ev_s0f] + list(g_fence))

    ld_guard = [(), ()]
    bfence = ()
    ev_y_w = [None] * NCH
    for cix in range(NCH):
        par = cix % 2
        P.begin_fence()
        bf = list(bfence)
        dch = list(ev_chunk_done[cix])
        lz = P.dma("sync", t_zl[par][:, :], s_z[cix, :, :],
                   deps=[dch[0]] + list(ld_guard[par]))
        lft = P.dma("sync", t_ftl[par][:, :], s_ft[cix, :, :], deps=[dch[1]])
        lkd = P.dma("sync", t_kdl[par][:, :], s_kd[cix, :, :], deps=[dch[2]])
        lpt = P.dma("sync", t_ptl[par][:, :], s_pt[cix, :, :], deps=[dch[3]])
        lqt = P.dma("sync",
                    t_qtl[par][:, :].rearrange("e (j hh t) -> e j hh t",
                                               j=8, hh=2),
                    s_qt[cix].rearrange("j (hh e) t -> e j hh t", hh=2),
                    deps=[dch[4]])
        lg = P.dma("sync", t_gl[par][:, :], s_gate[cix * 128:(cix + 1) * 128, :],
                   deps=[ev_gate_w[cix]])

        bfs, bdf = bank(2)
        fs_m = [P.step("tensor", lambda e, h=h, bfs=bfs, par=par: e.matmul(
            ps_t[0:128, 512 * bfs + h * 64:512 * bfs + (h + 1) * 64],
            t_ftl[par][:, h * 128:(h + 1) * 128],
            t_sb2[:, h * 64:(h + 1) * 64], start=True, stop=True),
            deps=[lft, ev_sb] + (bdf if h == 0 else []))
            for h in range(16)]
        w0v = t_zl[par][:, :].rearrange("p (h two e) -> p h two e",
                                        h=16, two=2)[:, :, 0, :]
        ewh = P.step("vector", lambda e, bfs=bfs, w0v=w0v: e.scalar_tensor_tensor(
            t_wh[:, :].rearrange("p (h e) -> p h e", h=16),
            ps_t[0:128, 512 * bfs:512 * bfs + 1024].rearrange(
                "p (h e) -> p h e", h=16),
            -1.0, w0v, ALU.mult, ALU.add), deps=fs_m + [lz] + bf)
        bank_done(bfs, 2, [ewh])

        bo, bdo = bank(2)
        o_m = []
        for h in range(16):
            o_m.append(P.step("tensor", lambda e, h=h, bo=bo, par=par: e.matmul(
                ps_t[0:128, 512 * bo + h * 64:512 * bo + (h + 1) * 64],
                t_qtl[par][:, h * 128:(h + 1) * 128],
                t_sb2[:, h * 64:(h + 1) * 64], start=True, stop=False),
                deps=[lqt, ev_sb] + (bdo if h == 0 else [])))
            o_m.append(P.step("tensor", lambda e, h=h, bo=bo, par=par: e.matmul(
                ps_t[0:128, 512 * bo + h * 64:512 * bo + (h + 1) * 64],
                t_ptl[par][:, h * 128:(h + 1) * 128],
                t_wh[:, h * 64:(h + 1) * 64], start=False, stop=True),
                deps=[lpt, ewh]))
        ey = P.step("vector", lambda e, bo=bo: e.tensor_copy(
            t_ytm[:, :], ps_t[0:128, 512 * bo:512 * bo + 1024]),
            deps=o_m + bf)
        bank_done(bo, 2, [ey])

        esg = P.step("vector", lambda e, cix=cix: e.tensor_tensor(
            t_sg[:, :].rearrange("p (h e) -> p h e", h=16),
            t_sb2[:, :].rearrange("p (h e) -> p h e", h=16),
            t_gcols[:, cix * 16:(cix + 1) * 16].unsqueeze(2).broadcast_to(
                [64, 16, 64]), ALU.mult), deps=[ev_sb, ev_gamma[cix]] + bf)
        bs2, bds2 = bank(2)
        s_m = []
        for h in range(16):
            s_m.append(P.step("tensor", lambda e, h=h, bs2=bs2, par=par:
                              e.matmul(
                ps_t[0:64, 512 * bs2 + h * 64:512 * bs2 + (h + 1) * 64],
                t_kdl[par][:, h * 64:(h + 1) * 64],
                t_wh[:, h * 64:(h + 1) * 64], start=True, stop=False),
                deps=[lkd, ewh] + (bds2 if h == 0 else [])))
            s_m.append(P.step("tensor", lambda e, h=h, bs2=bs2: e.matmul(
                ps_t[0:64, 512 * bs2 + h * 64:512 * bs2 + (h + 1) * 64],
                t_con["i64rep_bf"][:, 0:64],
                t_sg[:, h * 64:(h + 1) * 64], start=False, stop=True),
                deps=[esg]))
        ev_sf = P.step("vector", lambda e, bs2=bs2: e.tensor_copy(
            t_sf[:, :], ps_t[0:64, 512 * bs2:512 * bs2 + 1024]),
            deps=s_m + [ev_sf])
        ev_sb = P.step("vector", lambda e, bs2=bs2: e.tensor_copy(
            t_sb2[:, :], ps_t[0:64, 512 * bs2:512 * bs2 + 1024]),
            deps=s_m + [ev_sb] + [esg] + fs_m + o_m[:1])
        bank_done(bs2, 2, [ev_sf, ev_sb])

        er1 = P.step("scalar", lambda e: e.activation(
            t_ysq[:, :], t_ytm[:, :], AF.Square, accum_out=t_yst[:, 0:1]),
            deps=[ey] + bf)
        er2 = P.step("scalar", lambda e: e.activation(
            t_yst[:, 1:2], t_yst[:, 0:1], AF.Ln, scale=1.0 / D, bias=t_eps[:, :]),
            deps=[er1])
        er3 = P.step("scalar", lambda e: e.activation(
            t_yst[:, 2:3], t_yst[:, 1:2], AF.Exp, scale=-0.5), deps=[er2])
        er4 = P.step("vector", lambda e: e.scalar_tensor_tensor(
            t_ysq[:, :], t_ytm[:, :], t_yst[:, 2:3], t_gnb[:, :],
            ALU.mult, ALU.mult), deps=[er3, ev_gnb, er1])
        er5 = P.step("vector", lambda e, par=par: e.tensor_mul(
            t_yout[:, :], t_ysq[:, :], t_gl[par][:, :]), deps=[er4, lg] + bf)
        ev_y_w[cix] = P.dma("sync", s_y[cix * 128:(cix + 1) * 128, :],
                            t_yout[:, :], deps=[er5])
        bfence = P.end_fence()
        ld_guard[par] = (ewh, ey, er5, s_m[-1], o_m[-1], fs_m[-1], lqt)

    if STAGE <= 5:
        return _cut(P.fence_all())
    t_yfm = bsb("t_yfm", [128, 8 * G1K], BF16)
    t_xo = bsb("t_xo", [128, 2 * D], F32)
    t_att = bsb("t_att", [128, D], F32)
    t_asq = bsb("t_asq", [128, D], F32)
    t_ast = bsb("t_ast", [128, 32], F32)
    t_xm = bsb("t_xm", [128, D], F32)
    t_xn2b = bsb("t_xn2b", [128, D], BF16)

    ev_xmid_w = [None] * NCH
    ev_xn2_w = [None] * NCH
    yfm_guard = []
    xo_ev = [None, None]
    c1f = ()
    for gi in range(TOK // G1K):
        ydeps = [ev_y_w[gi * (G1K // 128) + i] for i in range(G1K // 128)]
        yevs = [P.dma("sync", t_yfm[:, dc * G1K:(dc + 1) * G1K],
                      s_y[gi * G1K:(gi + 1) * G1K, dc * 128:(dc + 1) * 128],
                      deps=ydeps + yfm_guard, transpose=True)
                for dc in range(8)]
        for cc in range(G1K // 128):
            cix = gi * (G1K // 128) + cc
            par = cix % 2
            so = par * 16
            P.begin_fence()
            cfl = list(c1f)
            b0, bd = bank(2)
            mme = []
            for kc in range(8):
                for nh in range(2):
                    mme.append(P.step(
                        "tensor", lambda e, kc=kc, nh=nh, b0=b0, cc=cc: e.matmul(
                            psf(b0 + nh, 0, 128, 0, 512),
                            t_yfm[:, kc * G1K + cc * 128:kc * G1K + (cc + 1) * 128],
                            t_wo[:, kc * D + nh * 512:kc * D + (nh + 1) * 512],
                            start=(kc == 0), stop=(kc == 7)),
                        deps=[yevs[kc], woev[kc]] + (bd if kc == 0 else [])))
            aps = ps_t[0:128, 512 * b0:512 * b0 + 1024]
            el1 = P.step("scalar", lambda e, aps=aps, so=so: e.activation(
                t_asq[:, :], aps, AF.Square, accum_out=t_ast[:, so:so + 1]),
                deps=mme + cfl)
            el2 = P.step("vector", lambda e, aps=aps, so=so: e.reduce_sum(
                t_ast[:, so + 1:so + 2], aps, AXX), deps=mme + cfl + [el1])
            el3 = P.step("vector", lambda e, so=so: e.tensor_scalar_mul(
                t_ast[:, so + 2:so + 3], t_ast[:, so + 1:so + 2], 1.0 / D),
                deps=[el2])
            el4 = P.step("vector", lambda e, so=so: e.tensor_mul(
                t_ast[:, so + 3:so + 4], t_ast[:, so + 2:so + 3],
                t_ast[:, so + 2:so + 3]), deps=[el3])
            el5 = P.step("vector", lambda e, so=so: e.tensor_scalar_mul(
                t_ast[:, so + 4:so + 5], t_ast[:, so:so + 1], 1.0 / D),
                deps=[el1])
            el6 = P.step("vector", lambda e, so=so: e.tensor_sub(
                t_ast[:, so + 5:so + 6], t_ast[:, so + 4:so + 5],
                t_ast[:, so + 3:so + 4]), deps=[el4, el5])
            el7 = P.step("scalar", lambda e, so=so: e.activation(
                t_ast[:, so + 6:so + 7], t_ast[:, so + 5:so + 6], AF.Ln,
                bias=t_eps[:, :]), deps=[el6])
            el8 = P.step("scalar", lambda e, so=so: e.activation(
                t_ast[:, so + 7:so + 8], t_ast[:, so + 6:so + 7], AF.Exp,
                scale=-0.5), deps=[el7])
            el9 = P.step("vector", lambda e, aps=aps, so=so: e.tensor_scalar(
                t_att[:, :], aps, t_ast[:, so + 2:so + 3],
                t_ast[:, so + 7:so + 8], ALU.subtract, ALU.mult),
                deps=[el8] + cfl)
            bank_done(b0, 2, [el9])
            exl = P.dma("sync", t_xo[:, par * D:(par + 1) * D],
                        d_x[cix * 128:(cix + 1) * 128, :],
                        deps=[xo_ev[par]] if xo_ev[par] else [])
            exm = P.step("vector", lambda e, par=par: e.tensor_add(
                t_xm[:, :], t_att[:, :], t_xo[:, par * D:(par + 1) * D]),
                deps=[el9, exl] + cfl)
            xo_ev[par] = exm
            ev_xmid_w[cix] = P.dma("sync", s_xmid[cix * 128:(cix + 1) * 128, :],
                                   t_xm[:, :], deps=[exm])
            em1 = P.step("scalar", lambda e, so=so: e.activation(
                t_asq[:, :], t_xm[:, :], AF.Square,
                accum_out=t_ast[:, so + 8:so + 9]), deps=[exm, el1])
            em2 = P.step("vector", lambda e, so=so: e.reduce_sum(
                t_ast[:, so + 9:so + 10], t_xm[:, :], AXX), deps=[exm])
            em3 = P.step("vector", lambda e, so=so: e.tensor_scalar_mul(
                t_ast[:, so + 10:so + 11], t_ast[:, so + 9:so + 10], 1.0 / D),
                deps=[em2])
            em4 = P.step("vector", lambda e, so=so: e.tensor_mul(
                t_ast[:, so + 11:so + 12], t_ast[:, so + 10:so + 11],
                t_ast[:, so + 10:so + 11]), deps=[em3])
            em5 = P.step("vector", lambda e, so=so: e.tensor_scalar_mul(
                t_ast[:, so + 12:so + 13], t_ast[:, so + 8:so + 9], 1.0 / D),
                deps=[em1])
            em6 = P.step("vector", lambda e, so=so: e.tensor_sub(
                t_ast[:, so + 13:so + 14], t_ast[:, so + 12:so + 13],
                t_ast[:, so + 11:so + 12]), deps=[em4, em5])
            em7 = P.step("scalar", lambda e, so=so: e.activation(
                t_ast[:, so + 14:so + 15], t_ast[:, so + 13:so + 14], AF.Ln,
                bias=t_eps[:, :]), deps=[em6])
            em8 = P.step("scalar", lambda e, so=so: e.activation(
                t_ast[:, so + 15:so + 16], t_ast[:, so + 14:so + 15], AF.Exp,
                scale=-0.5), deps=[em7])
            em9 = P.step("vector", lambda e, so=so: e.tensor_scalar(
                t_xn2b[:, :], t_xm[:, :], t_ast[:, so + 10:so + 11],
                t_ast[:, so + 15:so + 16], ALU.subtract, ALU.mult),
                deps=[em8] + cfl)
            ev_xn2_w[cix] = P.dma("sync", s_xn2[cix * 128:(cix + 1) * 128, :],
                                  t_xn2b[:, :], deps=[em9])
            c1f = P.end_fence()
        yfm_guard = [mme[-1]]

    if STAGE <= 6:
        return _cut(P.fence_all())
    b_fence = P.fence_all()
    b_ctx.close()
    gb_ctx.close()

    c_ctx = ExitStack()

    def csb(name, shape, dtype):
        return c_ctx.enter_context(nc.sbuf_tensor(name, shape, dtype))

    t_fwg = csb("t_fwg", [128, 8 * HID], BF16)
    t_fwu = csb("t_fwu", [128, 8 * HID], BF16)
    t_fwd = csb("t_fwd", [128, 24 * D], BF16)
    t_x2f = csb("t_x2f", [128, 8 * G1K], BF16)
    t_fth = csb("t_fth", [128, 256], F32)
    t_fsg = csb("t_fsg", [128, 256], BF16)
    t_hbf = csb("t_hbf", [128, 256], BF16)
    t_fo = csb("t_fo", [128, 2 * D], BF16)

    SFF = 1.0 / 127.0
    fw_deps = list(b_fence) + ffn_cast_ev
    fgw = [P.dma("sync", t_fwg[:, dc * HID:(dc + 1) * HID],
                 s_wgbf[:, dc * 128:(dc + 1) * 128], deps=fw_deps,
                 transpose=True) for dc in range(8)]
    fuw = [P.dma("sync", t_fwu[:, dc * HID:(dc + 1) * HID],
                 s_wubf[:, dc * 128:(dc + 1) * 128], deps=fw_deps,
                 transpose=True) for dc in range(8)]
    fdw = [P.dma("sync", t_fwd[:, hc * D:(hc + 1) * D],
                 s_wdbf[:, hc * 128:(hc + 1) * 128], deps=fw_deps,
                 transpose=True) for hc in range(24)]

    ev_fdown_w = [None] * NCH
    t_fl = csb("t_fl", [128, D], BF16)
    t_xml = csb("t_xml", [128, D], F32)
    t_osq = csb("t_osq", [128, D], F32)
    t_ost = csb("t_ost", [128, 6 * NCH], F32)
    t_fdum = csb("t_fdum", [128, D], BF16)
    c2b_prev = {"eo9": None, "eoA": None, "ew": None, "esq": None}
    ST_SQ, ST_SM, ST_MN, ST_VR, ST_M2, ST_RS = (0, NCH, 2 * NCH, 3 * NCH,
                                                4 * NCH, 5 * NCH)

    def emit_c2b(cix, stat_ev):
        prev_eo9, prev_eoA, prev_ew = (c2b_prev["eo9"], c2b_prev["eoA"],
                                       c2b_prev["ew"])
        lf = P.dma("gpsimd", t_fl[:, :],
                   s_fdown[cix * 128:(cix + 1) * 128, :],
                   deps=[ev_fdown_w[cix]] + ([prev_eo9] if prev_eo9 else []))
        lx = P.dma("gpsimd", t_xml[:, :],
                   s_xmid[cix * 128:(cix + 1) * 128, :],
                   deps=[ev_xmid_w[cix]] + ([prev_eoA] if prev_eoA else []))
        eo9 = P.step("vector", lambda e, cix=cix: e.tensor_scalar(
            t_osq[:, :], t_fl[:, :], t_ost[:, ST_MN + cix:ST_MN + cix + 1],
            t_ost[:, ST_RS + cix:ST_RS + cix + 1],
            ALU.subtract, ALU.mult),
            deps=[lf, stat_ev] + ([prev_ew] if prev_ew else []))
        eoA = P.step("vector", lambda e: e.tensor_add(
            t_osq[:, :], t_osq[:, :], t_xml[:, :]),
            deps=[eo9, lx] + ([prev_ew] if prev_ew else []))
        ew = P.dma("gpsimd", d_out[cix * 128:(cix + 1) * 128, :],
                   t_osq[:, :], deps=[eoA])
        c2b_prev["eo9"] = eo9
        c2b_prev["eoA"] = eoA
        c2b_prev["ew"] = ew

    x2f_guard = []
    fo_guard = []
    for gi in range(TOK // G1K):
        xdeps = [ev_xn2_w[gi * (G1K // 128) + i] for i in range(G1K // 128)]
        x2evs = [P.dma("sync", t_x2f[:, dc * G1K:(dc + 1) * G1K],
                       s_xn2[gi * G1K:(gi + 1) * G1K, dc * 128:(dc + 1) * 128],
                       deps=xdeps + x2f_guard, transpose=True)
                 for dc in range(8)]
        for bi in range(G1K // 256):
            tof = bi * 256
            bdn, bddn = bank(4, hold=True)
            dn_m = []
            h_prev = None
            prev_dn = None
            for hc in range(24):
                bg_, bdg_ = bank(1)
                g_m = [P.step("tensor",
                              lambda e, kc=kc, hc=hc, bg_=bg_, tof=tof: e.matmul(
                    psf(bg_, 0, 128, 0, 256),
                    t_fwg[:, kc * HID + hc * 128:kc * HID + (hc + 1) * 128],
                    t_x2f[:, kc * G1K + tof:kc * G1K + tof + 256],
                    start=(kc == 0), stop=(kc == 7)),
                    deps=[x2evs[kc], fgw[kc]] + (bdg_ if kc == 0 else []))
                    for kc in range(8)]
                bu_, bdu_ = bank(1)
                u_m = [P.step("tensor",
                              lambda e, kc=kc, hc=hc, bu_=bu_, tof=tof: e.matmul(
                    psf(bu_, 0, 128, 0, 256),
                    t_fwu[:, kc * HID + hc * 128:kc * HID + (hc + 1) * 128],
                    t_x2f[:, kc * G1K + tof:kc * G1K + tof + 256],
                    start=(kc == 0), stop=(kc == 7)),
                    deps=[x2evs[kc], fuw[kc]] + (bdu_ if kc == 0 else []))
                    for kc in range(8)]
                et = P.step("scalar", lambda e, bg_=bg_: e.activation(
                    t_fth[:, :], psf(bg_, 0, 128, 0, 256), AF.Tanh,
                    scale=SFF / 2),
                    deps=g_m + ([h_prev] if h_prev else []))
                es = P.step("vector", lambda e: e.tensor_scalar(
                    t_fsg[:, :], t_fth[:, :], 0.5, 0.5, ALU.mult, ALU.add),
                    deps=[et])
                es2 = P.step("vector", lambda e, bg_=bg_: e.scalar_tensor_tensor(
                    t_fsg[:, :], psf(bg_, 0, 128, 0, 256), SFF, t_fsg[:, :],
                    ALU.mult, ALU.mult), deps=[es])
                bank_done(bg_, 1, [es2])
                eh = P.step("vector", lambda e, bu_=bu_: e.scalar_tensor_tensor(
                    t_hbf[:, :], psf(bu_, 0, 128, 0, 256), SFF, t_fsg[:, :],
                    ALU.mult, ALU.mult),
                    deps=[es2] + u_m + ([prev_dn] if prev_dn else []))
                bank_done(bu_, 1, [eh])
                h_prev = eh
                for ts_ in range(2):
                    for nh in range(2):
                        dn_m.append(P.step(
                            "tensor",
                            lambda e, ts_=ts_, nh=nh, hc=hc, bdn=bdn: e.matmul(
                                psf(bdn + 2 * ts_ + nh, 0, 128, 0, 512),
                                t_hbf[:, ts_ * 128:(ts_ + 1) * 128],
                                t_fwd[:, hc * D + nh * 512:hc * D + (nh + 1) * 512],
                                start=(hc == 0), stop=(hc == 23)),
                            deps=[eh, fdw[hc]] + (bddn if hc == 0 and
                                                  ts_ == 0 and nh == 0 else [])))
                prev_dn = dn_m[-1]
            efo = [P.step("scalar", lambda e, ts_=ts_, bdn=bdn: e.mul(
                t_fo[:, ts_ * D:(ts_ + 1) * D],
                ps_t[0:128, 512 * (bdn + 2 * ts_):512 * (bdn + 2 * ts_) + 1024],
                SFF), deps=dn_m + fo_guard) for ts_ in range(2)]
            cix0 = (gi * G1K + tof) // 128
            sq_evs = []
            for ts_ in range(2):
                esq = P.step("scalar", lambda e, ts_=ts_, bdn=bdn, cix0=cix0:
                             e.activation(
                    t_fdum[:, :],
                    ps_t[0:128, 512 * (bdn + 2 * ts_):
                         512 * (bdn + 2 * ts_) + 1024],
                    AF.Square, scale=SFF,
                    accum_out=t_ost[:, ST_SQ + cix0 + ts_:
                                    ST_SQ + cix0 + ts_ + 1]),
                    deps=dn_m + ([c2b_prev["esq"]] if c2b_prev["esq"]
                                 else []))
                c2b_prev["esq"] = esq
                sq_evs.append(esq)
                sq_evs.append(P.step("vector", lambda e, ts_=ts_, bdn=bdn,
                                     cix0=cix0: e.reduce_sum(
                    t_ost[:, ST_SM + cix0 + ts_:ST_SM + cix0 + ts_ + 1],
                    ps_t[0:128, 512 * (bdn + 2 * ts_):
                         512 * (bdn + 2 * ts_) + 1024], AXX),
                    deps=dn_m + [esq]))
            bank_done(bdn, 4, efo + sq_evs)
            ev = P.dma("sync",
                       s_fdown[cix0 * 128:(cix0 + 2) * 128, :].rearrange(
                           "(ts p) d -> p ts d", ts=2),
                       t_fo[:, :].rearrange("p (ts d) -> p ts d", ts=2),
                       deps=efo)
            ev_fdown_w[cix0] = ev
            ev_fdown_w[cix0 + 1] = ev
            fo_guard = [ev]
        x2f_guard = [dn_m[-1], g_m[-1], u_m[-1]]
        g0 = gi * (G1K // 128)
        ng = G1K // 128
        st1 = P.step("vector", lambda e, g0=g0, ng=ng: e.tensor_scalar_mul(
            t_ost[:, ST_MN + g0:ST_MN + g0 + ng],
            t_ost[:, ST_SM + g0:ST_SM + g0 + ng], SFF / D), deps=sq_evs)
        st2 = P.step("vector", lambda e, g0=g0, ng=ng: e.tensor_mul(
            t_ost[:, ST_M2 + g0:ST_M2 + g0 + ng],
            t_ost[:, ST_MN + g0:ST_MN + g0 + ng],
            t_ost[:, ST_MN + g0:ST_MN + g0 + ng]), deps=[st1])
        st3 = P.step("vector", lambda e, g0=g0, ng=ng: e.tensor_scalar_mul(
            t_ost[:, ST_VR + g0:ST_VR + g0 + ng],
            t_ost[:, ST_SQ + g0:ST_SQ + g0 + ng], 1.0 / D), deps=sq_evs)
        st4 = P.step("vector", lambda e, g0=g0, ng=ng: e.tensor_sub(
            t_ost[:, ST_VR + g0:ST_VR + g0 + ng],
            t_ost[:, ST_VR + g0:ST_VR + g0 + ng],
            t_ost[:, ST_M2 + g0:ST_M2 + g0 + ng]), deps=[st2, st3])
        st5 = P.step("scalar", lambda e, g0=g0, ng=ng: e.activation(
            t_ost[:, ST_M2 + g0:ST_M2 + g0 + ng],
            t_ost[:, ST_VR + g0:ST_VR + g0 + ng], AF.Ln,
            bias=t_eps[:, :]), deps=[st4, c2b_prev["esq"]])
        st6 = P.step("scalar", lambda e, g0=g0, ng=ng: e.activation(
            t_ost[:, ST_RS + g0:ST_RS + g0 + ng],
            t_ost[:, ST_M2 + g0:ST_M2 + g0 + ng], AF.Exp,
            scale=-0.5), deps=[st5])
        for cix in range(g0, g0 + ng):
            emit_c2b(cix, st6)

    if STAGE <= 7:
        return _cut(P.fence_all())
    c_ctx.close()
    P.finalize(block, sems)
    ctx.close()
    return nc


def kernel(L0_attn_x, L0_attn_Wq, L0_attn_Wk, L0_attn_Wv, L0_attn_Wa,
           L0_attn_ba, L0_attn_Wb, L0_attn_bb, L0_attn_Wg, L0_attn_Wo,
           L0_attn_g_norm, L0_ffn_w_gate, L0_ffn_w_up, L0_ffn_w_down):
    B, S, _ = L0_attn_x.shape
    TOK = B * S // NCORE
    nc = build_program(TOK)
    CON = build_consts()

    wab = np.zeros((D, 64), np.float32)
    wab[:, 0:16] = np.asarray(L0_attn_Wa, np.float32)
    wab[:, 32:48] = np.asarray(L0_attn_Wb, np.float32)
    bab = np.zeros((64, 1), np.float32)
    bab[0:16, 0] = np.asarray(L0_attn_ba, np.float32)
    bab[32:48, 0] = np.asarray(L0_attn_bb, np.float32)

    x = np.ascontiguousarray(
        np.asarray(L0_attn_x, np.float32).reshape(B, 4, S // 4, D))
    base = {
        "wq": np.ascontiguousarray(L0_attn_Wq, np.float32),
        "wk": np.ascontiguousarray(L0_attn_Wk, np.float32),
        "wv": np.ascontiguousarray(L0_attn_Wv, np.float32),
        "wgt": np.ascontiguousarray(L0_attn_Wg, np.float32),
        "wo": np.ascontiguousarray(L0_attn_Wo, np.float32),
        "wab": wab, "bab": bab,
        "gnorm": np.ascontiguousarray(
            np.asarray(L0_attn_g_norm, np.float32).reshape(1, D)),
        "ffn_g": np.ascontiguousarray(L0_ffn_w_gate, np.int32),
        "ffn_u": np.ascontiguousarray(L0_ffn_w_up, np.int32),
        "ffn_d": np.ascontiguousarray(L0_ffn_w_down, np.int32),
    }
    for k, v in CON.items():
        base["c_" + k] = v
    in_maps = []
    for core in range(NCORE):
        m = dict(base)
        m["x"] = np.ascontiguousarray(x[core // 4, core % 4])
        selv = np.zeros((1, 4), np.float32)
        if core % 4 > 0:
            selv[0, core % 4 - 1] = 1.0
        m["selv"] = selv
        in_maps.append(m)

    res = run_bass_kernel_spmd(nc, in_maps, core_ids=list(range(NCORE)))
    out = np.zeros((B, S, D), np.float32)
    for core in range(NCORE):
        out[core // 4, (core % 4) * TOK:(core % 4 + 1) * TOK] = \
            np.asarray(res.results[core]["out"], np.float32)
    return out


# version 10
# speedup vs baseline: 1.0214x; 1.0057x over previous
import numpy as np
import ml_dtypes
import concourse.bass as bass
import concourse.mybir as mybir
from concourse.bass_utils import run_bass_kernel_spmd
from contextlib import ExitStack

F32 = mybir.dt.float32
BF16 = mybir.dt.bfloat16
I32 = mybir.dt.int32
AF = mybir.ActivationFunctionType
ALU = mybir.AluOpType
AXX = mybir.AxisListType.X

D = 1024
HID = 3072
C = 128
NCORE = 8

ENG2SEM = {"tensor": "st", "vector": "sv", "scalar": "ss", "gpsimd": "sp"}
NLANE = 8
LANES = tuple(f"ld{i}" for i in range(NLANE))
NGLANE = 2
GLANES = tuple(f"gd{i}" for i in range(NGLANE))
SEMKEYS = ("st", "sv", "ss", "sp", "scc") + LANES + GLANES


class Prog:

    def __init__(self):
        self.sched = []
        self.cnt = {k: 0 for k in SEMKEYS}
        self.collect = None
        self.lane_rr = 0
        self.glane_rr = 0

    def step(self, engine, fn, deps=(), inc=1, semkey=None):
        key = semkey or ENG2SEM[engine]
        deps = tuple(d for d in deps if d is not None)
        self.sched.append((engine, fn, deps, inc, key))
        self.cnt[key] += inc
        ev = (key, self.cnt[key])
        if self.collect is not None:
            self.collect[key] = ev
        return ev

    def dma(self, engine, out, in_, deps=(), **kw):
        if engine == "gpsimd":
            lane = GLANES[self.glane_rr]
            self.glane_rr = (self.glane_rr + 1) % NGLANE
        else:
            lane = LANES[self.lane_rr]
            self.lane_rr = (self.lane_rr + 1) % NLANE
        deps = tuple(d for d in deps if d is not None)
        if self.cnt[lane] > 0:
            deps = deps + ((lane, self.cnt[lane]),)
        return self.step(
            engine,
            lambda e, out=out, in_=in_, kw=kw: e.dma_start(out=out, in_=in_, **kw),
            deps=deps, inc=16, semkey=lane)

    def begin_fence(self):
        self.collect = {}

    def end_fence(self):
        f = tuple(self.collect.values())
        self.collect = None
        return f

    def fence_all(self):
        return tuple((k, v) for k, v in self.cnt.items() if v > 0)

    def finalize(self, block, sems):
        water = {}

        def emit(this_engine, eng):
            for engine, fn, deps, inc, key in self.sched:
                if engine != this_engine:
                    continue
                for (k, v) in deps:
                    if water.get((this_engine, k), 0) < v:
                        eng.wait_ge(sems[k], v)
                        water[(this_engine, k)] = v
                inst = fn(eng)
                if inst is not None:
                    inst.then_inc(sems[key], inc)
                else:
                    eng.nop().then_inc(sems[key], inc)

        @block.sync
        def _(eng):
            emit("sync", eng)
            for k in LANES + GLANES + ("scc",):
                if self.cnt[k] > 0:
                    eng.wait_ge(sems[k], self.cnt[k])

        @block.tensor
        def _(eng):
            emit("tensor", eng)

        @block.vector
        def _(eng):
            emit("vector", eng)

        @block.scalar
        def _(eng):
            emit("scalar", eng)

        @block.gpsimd
        def _(eng):
            emit("gpsimd", eng)


def build_consts():
    con = {}
    con["ones1"] = np.ones((1, 128), np.float32)
    con["ones1_bf"] = np.ones((1, 128), ml_dtypes.bfloat16)
    con["eye_bf"] = np.eye(128, dtype=ml_dtypes.bfloat16)
    con["eye_f32"] = np.eye(128, dtype=np.float32)
    con["i64_f32"] = np.eye(64, dtype=np.float32)
    con["i64rep_bf"] = np.tile(np.eye(64), (1, 16)).astype(ml_dtypes.bfloat16)
    idx = np.arange(128)
    m = np.where(idx[:, None] > idx[None, :], -1.0, 0.0).astype(np.float32)
    con["neglmask_bf"] = m
    con["umask_bf"] = (idx[:, None] <= idx[None, :]).astype(
        np.float32)
    ho = np.zeros((128, 2), np.float32)
    ho[:64, 0] = 1.0
    ho[64:, 1] = 1.0
    con["headones_bf"] = ho.astype(ml_dtypes.bfloat16)
    s2 = np.zeros((2, 128), np.float32)
    s2[0, :64] = 1.0
    s2[1, 64:] = 1.0
    con["sel2t_bf"] = s2.astype(ml_dtypes.bfloat16)
    selh = np.zeros((16, 16 * 64), np.float32)
    for h in range(16):
        selh[h, h * 64:(h + 1) * 64] = 1.0
    con["selh_f32"] = selh
    con["selh_bf"] = selh.astype(ml_dtypes.bfloat16)
    selj = np.zeros((16, 8 * 128), np.float32)
    for j in range(8):
        selj[2 * j, j * 128:j * 128 + 64] = 1.0
        selj[2 * j + 1, j * 128 + 64:(j + 1) * 128] = 1.0
    con["selj_bf"] = selj.astype(ml_dtypes.bfloat16)
    return con


STAGE = 99
SUB = 999
DEBUG_SCRATCH = False


def build_program(TOK):
    NCH = TOK // C
    PT_ = 256
    NPT = TOK // PT_
    CPT = PT_ // C
    G1K = min(1024, TOK)
    NL = 7

    nc = bass.Bass("TRN2", target_bir_lowering=False, debug=False)
    P = Prog()
    CON = build_consts()

    def din(name, shape, dtype):
        return nc.dram_tensor(name, list(shape), dtype, kind="ExternalInput").ap()

    d_x = din("x", (TOK, D), F32)
    d_w = {n: din(n, (D, D), F32) for n in ("wq", "wk", "wv", "wgt", "wo")}
    d_wab = din("wab", (D, 64), F32)
    d_bab = din("bab", (64, 1), F32)
    d_gnorm = din("gnorm", (1, D), F32)
    d_fg = din("ffn_g", (HID, D), I32)
    d_fu = din("ffn_u", (HID, D), I32)
    d_fd = din("ffn_d", (D, HID), I32)
    d_selv = din("selv", (1, 4), F32)
    d_con = {k: din("c_" + k, v.shape,
                    F32 if v.dtype == np.float32 else BF16)
             for k, v in CON.items()}
    d_out = nc.dram_tensor("out", [TOK, D], F32, kind="ExternalOutput").ap()

    def scr(name, shape, dtype):
        kind = "ExternalOutput" if DEBUG_SCRATCH else "Internal"
        if name in ("s_agi", "s_ago"):
            kind = "Internal"
        return nc.dram_tensor(name, list(shape), dtype, kind=kind).ap()

    s_xn = scr("s_xn", (TOK, D), BF16)
    s_kfm = scr("s_kfm", (D, TOK), BF16)
    s_gate = scr("s_gate", (TOK, D), BF16)
    s_ab = scr("s_ab", (TOK // 256, 64, 256), F32)
    s_z = scr("s_z", (NCH, 128, 2048), BF16)
    s_ft = scr("s_ft", (NCH, 64, 2048), BF16)
    s_kd = scr("s_kd", (NCH, 128, 1024), BF16)
    s_pt = scr("s_pt", (NCH, 128, 2048), BF16)
    s_qt = scr("s_qt", (NCH, 8, 128, 128), BF16)
    s_cath = scr("s_cath", (NCH, 2048), BF16)
    s_catl = scr("s_catl", (NCH, 2048), BF16)
    s_agi = scr("s_agi", (2048, 64), F32)
    s_ago = scr("s_ago", (4 * 2048, 64), F32)
    s_y = scr("s_y", (TOK, D), BF16)
    s_xmid = scr("s_xmid", (TOK, D), F32)
    s_xn2 = scr("s_xn2", (TOK, D), BF16)
    s_wgbf = scr("s_wgbf", (HID, D), BF16)
    s_wubf = scr("s_wubf", (HID, D), BF16)
    s_wdbf = scr("s_wdbf", (D, HID), BF16)
    s_fdown = scr("s_fdown", (TOK, D), BF16)
    s_xdbg = scr("s_xdbg", (NCH, 128, 2048), BF16)
    s_emdbg = scr("s_emdbg", (NCH, 4, 128, 512), F32)
    s_tmdbg = scr("s_tmdbg", (NCH, 128, 80), F32)
    s_kkdbg = scr("s_kkdbg", (NCH, 4, 128, 512), F32)
    s_kkdbg2 = scr("s_kkdbg2", (NCH, 4, 128, 512), F32)
    s_ddbg = scr("s_ddbg", (NCH, 4, 128, 512), F32)

    ctx = ExitStack()
    ps_t = ctx.enter_context(nc.psum_tensor("ps", [128, 3584], F32))
    psb_t = ctx.enter_context(nc.psum_tensor("psb", [128, 1024], BF16))

    def psf(b, p0, p1, c0, n):
        return ps_t[p0:p1, 512 * b + c0:512 * b + c0 + n]

    bank_ev = [()] * 8
    bank_hold = [False] * 8
    rr = [0]
    psb_free = [()]

    def bank(n=1, hold=False):
        start = rr[0]
        for off in range(8):
            b0 = (start + off) % 7
            if b0 + n > 7:
                continue
            if any(bank_hold[b] for b in range(b0, b0 + n)):
                continue
            rr[0] = (b0 + n) % 7
            deps = sum((bank_ev[b] for b in range(b0, b0 + n)), ())
            if hold:
                for b in range(b0, b0 + n):
                    bank_hold[b] = True
            return b0, list(deps)
        raise RuntimeError("no banks")

    def bank_done(b0, n, evs):
        for b in range(b0, b0 + n):
            bank_ev[b] = tuple(evs)
            bank_hold[b] = False

    sems = {}
    for name in SEMKEYS:
        sems[name] = ctx.enter_context(nc.semaphore(name))
    block = ctx.enter_context(nc.Block())

    sb = nc.alloc_sbuf_tensor
    t_con = {k: sb("t_" + k, list(v.shape),
                   F32 if v.dtype == np.float32 else BF16)
             for k, v in CON.items()}
    t_gnb = sb("t_gnb", [128, D], F32)
    t_gcols = sb("t_gcols", [64, 16 * NCH], F32)
    t_accF = sb("t_accF", [64, 1024], F32)
    t_uccF = sb("t_uccF", [64, 1024], F32)
    t_bab = sb("t_bab", [64, 1], F32)

    t_eps = sb("t_eps", [128, 1], F32)
    ev_con = [P.dma("sync", t_con[k][:, :], d_con[k]) for k in CON]
    ev_eps = P.step("vector", lambda e: e.memset(t_eps[:, :], 1e-5))
    ev_con.append(ev_eps)
    evc0 = ev_con[-1]
    evb = P.dma("sync", t_bab[:, :], d_bab)
    ev_bab = P.step("scalar", lambda e: e.mul(t_bab[:, :], t_bab[:, :], 0.5),
                    deps=[evb])

    a_ctx = ExitStack()

    def asb(name, shape, dtype):
        return a_ctx.enter_context(nc.sbuf_tensor(name, shape, dtype))

    t_wt = {n: asb("t_" + n, [128, 8 * D], BF16)
            for n in ("wq", "wk", "wv")}
    t_xnfm = asb("t_xnfm", [128, 8 * PT_], BF16)
    t_qn = asb("t_qn", [128, 2 * 8 * PT_], BF16)
    t_kn = asb("t_kn", [128, 2 * 8 * PT_], BF16)
    t_vb = asb("t_vb", [128, 2 * CPT * D], BF16)
    t_ktm = asb("t_ktm", [128, 2 * D], BF16)
    t_sqb = asb("t_sqb", [128, 8 * PT_], BF16)
    t_qbf = asb("t_qbf", [128, 8 * PT_], BF16)
    t_rn = asb("t_rn", [2, 1024], F32)
    t_rn2 = asb("t_rn2", [2, 1024], F32)
    t_rnb = asb("t_rnb", [2, 4 * PT_], BF16)
    sub_ctx = ExitStack()

    def ssb(name, shape, dtype):
        return sub_ctx.enter_context(nc.sbuf_tensor(name, shape, dtype))

    t_wt["wgt"] = ssb("t_wgt", [128, 8 * D], BF16)
    t_wab = ssb("t_wab", [128, 8 * 64], BF16)
    t_wld = ssb("t_wld", [128, 2 * D], F32)
    t_ffw = ssb("t_ffw", [128, 2 * D], I32)
    t_ffb = ssb("t_ffb", [128, 2 * D], BF16)
    t_gn1 = ssb("t_gn1", [1, D], F32)

    ev_g1 = P.dma("sync", t_gn1[:, :], d_gnorm)
    b0, bd = bank(2)
    ev_gm = [P.step("tensor", lambda e, i=i, b0=b0: e.matmul(
        psf(b0 + i, 0, 128, 0, 512), t_con["ones1"][:, :],
        t_gn1[:, 512 * i:512 * (i + 1)], start=True, stop=True),
        deps=[ev_g1, evc0] + bd) for i in range(2)]
    ev_gnb = P.step("vector", lambda e, b0=b0: e.tensor_copy(
        t_gnb[:, :], ps_t[0:128, 512 * b0:512 * b0 + 1024]), deps=ev_gm)
    bank_done(b0, 2, [ev_gnb])

    def _cut(dep_evs):
        for cix in range(NCH):
            P.dma("sync", d_out[cix * 128:(cix + 1) * 128, :], t_gnb[:, :],
                  deps=list(dep_evs) + [ev_gnb])
        P.finalize(block, sems)
        ctx.close()
        return nc

    wev = {}
    wl_ev = [None, None]
    for wkey, cols in [("wq", D), ("wk", D), ("wv", D), ("wgt", D), ("wab", 64)]:
        dst = t_wab if wkey == "wab" else t_wt[wkey]
        src = d_wab if wkey == "wab" else d_w[wkey]
        evs = []
        for kc in range(8):
            par = kc % 2
            ev1 = P.dma("gpsimd", t_wld[:, par * D:par * D + cols],
                        src[kc * 128:(kc + 1) * 128, :],
                        deps=[wl_ev[par]] if wl_ev[par] else [])
            evc = P.step("gpsimd", lambda e, kc=kc, par=par, dst=dst,
                         cols=cols:
                         e.tensor_copy(dst[:, kc * cols:(kc + 1) * cols],
                                       t_wld[:, par * D:par * D + cols]),
                         deps=[ev1])
            wl_ev[par] = evc
            evs.append(evc)
        wev[wkey] = evs

    ffn_cast_ev = []
    fc_ev = [None, None]
    fc_i = 0
    for (dsrc, ddst, rows, cols) in [(d_fg, s_wgbf, HID, D), (d_fu, s_wubf, HID, D),
                                     (d_fd, s_wdbf, D, HID)]:
        for r in range(rows // 128):
            for cc0 in range(cols // D):
                par = fc_i % 2
                fc_i += 1
                ev1 = P.dma("gpsimd", t_ffw[:, par * D:(par + 1) * D],
                            dsrc[r * 128:(r + 1) * 128, cc0 * D:(cc0 + 1) * D],
                            deps=[fc_ev[par]] if fc_ev[par] else [])
                ev2 = P.step("gpsimd", lambda e, par=par: e.tensor_copy(
                    t_ffb[:, par * D:(par + 1) * D],
                    t_ffw[:, par * D:(par + 1) * D]), deps=[ev1])
                ev3 = P.dma("gpsimd", ddst[r * 128:(r + 1) * 128,
                                           cc0 * D:(cc0 + 1) * D],
                            t_ffb[:, par * D:(par + 1) * D], deps=[ev2])
                fc_ev[par] = ev3
                ffn_cast_ev.append(ev3)

    t_x0 = ssb("t_x0", [128, 4 * D], F32)
    t_sq0 = ssb("t_sq0", [128, D], F32)
    t_st0 = ssb("t_st0", [128, 32], F32)
    t_xnb = ssb("t_xnb", [128, 4 * D], BF16)
    ev_xn_w = []
    prev_norms = []
    prev_stores = [None] * 4
    prev_e3s = None
    prev_e5 = None
    assert NCH % 4 == 0
    for gx in range(NCH // 4):
        loads = [P.dma("sync", t_x0[:, j * D:(j + 1) * D],
                       d_x[(gx * 4 + j) * 128:(gx * 4 + j + 1) * 128, :],
                       deps=prev_norms) for j in range(4)]
        sq = []
        for j in range(4):
            sq.append(P.step("scalar", lambda e, j=j: e.activation(
                t_sq0[:, :], t_x0[:, j * D:(j + 1) * D], AF.Square,
                accum_out=t_st0[:, j:j + 1]),
                deps=[loads[j]] + ([prev_e3s] if prev_e3s and j == 0 else []) +
                     ([sq[j - 1]] if j > 0 else [])))
        rd = [P.step("vector", lambda e, j=j: e.reduce_sum(
            t_st0[:, 4 + j:5 + j], t_x0[:, j * D:(j + 1) * D], AXX),
            deps=[loads[j]]) for j in range(4)]
        e1 = P.step("vector", lambda e: e.tensor_scalar_mul(
            t_st0[:, 8:12], t_st0[:, 4:8], 1.0 / D), deps=rd)
        e2 = P.step("vector", lambda e: e.tensor_mul(
            t_st0[:, 12:16], t_st0[:, 8:12], t_st0[:, 8:12]), deps=[e1])
        e3s = P.step("vector", lambda e: e.tensor_scalar_mul(
            t_st0[:, 16:20], t_st0[:, 0:4], 1.0 / D), deps=sq)
        e4 = P.step("vector", lambda e: e.tensor_sub(
            t_st0[:, 20:24], t_st0[:, 16:20], t_st0[:, 12:16]),
            deps=[e2, e3s] + ([prev_e5] if prev_e5 else []))
        e5 = P.step("scalar", lambda e: e.activation(
            t_st0[:, 24:28], t_st0[:, 20:24], AF.Ln, bias=t_eps[:, :]),
            deps=[e4])
        e6 = P.step("scalar", lambda e: e.activation(
            t_st0[:, 28:32], t_st0[:, 24:28], AF.Exp, scale=-0.5),
            deps=[e5] + prev_norms)
        norms = []
        for j in range(4):
            evn = P.step("vector", lambda e, j=j: e.tensor_scalar(
                t_xnb[:, j * D:(j + 1) * D], t_x0[:, j * D:(j + 1) * D],
                t_st0[:, 8 + j:9 + j], t_st0[:, 28 + j:29 + j],
                ALU.subtract, ALU.mult),
                deps=[e6] + ([prev_stores[j]] if prev_stores[j] else []))
            norms.append(evn)
            prev_stores[j] = P.dma(
                "sync", s_xn[(gx * 4 + j) * 128:(gx * 4 + j + 1) * 128, :],
                t_xnb[:, j * D:(j + 1) * D], deps=[evn])
            ev_xn_w.append(prev_stores[j])
        prev_norms = norms
        prev_e3s = e3s
        prev_e5 = e5

    if STAGE <= 1:
        return _cut(P.fence_all())
    t_abz = ssb("t_abz", [64, PT_], F32)
    t_ab = ssb("t_ab", [64, PT_], F32)
    t_gth = ssb("t_gth", [128, D], F32)
    t_gsg = ssb("t_gsg", [128, D], F32)
    t_gout = ssb("t_gout", [128, D], BF16)

    xnfm_guard = []
    ev_ab_tile = [None] * NPT
    ev_gate_w = [None] * NCH
    prev_gate_g = None
    prev_abz = None
    for ti in range(NPT):
        deps = [ev_xn_w[ti * CPT + i] for i in range(CPT)] + xnfm_guard
        xn_evs = [P.dma("sync", t_xnfm[:, dc * PT_:(dc + 1) * PT_],
                        s_xn[ti * PT_:(ti + 1) * PT_, dc * 128:(dc + 1) * 128],
                        deps=deps, transpose=True) for dc in range(8)]
        xnfm_guard = []
        b0, bd = bank(1)
        mme = [P.step("tensor", lambda e, kc=kc, b0=b0: e.matmul(
            psf(b0, 0, 64, 0, PT_), t_wab[:, kc * 64:(kc + 1) * 64],
            t_xnfm[:, kc * PT_:(kc + 1) * PT_], start=(kc == 0), stop=(kc == 7)),
            deps=[xn_evs[kc], wev["wab"][kc]] + (bd if kc == 0 else []))
            for kc in range(8)]
        ev_th = P.step("scalar", lambda e, b0=b0: e.activation(
            t_abz[:, :], psf(b0, 0, 64, 0, PT_), AF.Tanh,
            bias=t_bab[:, :], scale=0.5),
            deps=mme + [ev_bab] + ([prev_abz] if prev_abz else []))
        bank_done(b0, 1, [ev_th])
        pga = [ev_ab_tile[ti - 1]] if ti >= 1 else []
        eva_v = P.step("vector", lambda e: e.tensor_scalar(
            t_ab[:, :], t_abz[:, :], 0.5, 0.5,
            ALU.mult, ALU.add), deps=[ev_th] + pga)
        prev_abz = eva_v
        ev_ab_tile[ti] = P.dma("sync", s_ab[ti, :, :], t_ab[:, :],
                               deps=[eva_v])
        xnfm_guard = [mme[-1]]
        for ci in range(CPT):
            cix = ti * CPT + ci
            b0, bd = bank(2)
            mme = []
            for kc in range(8):
                for nh in range(2):
                    mme.append(P.step(
                        "tensor",
                        lambda e, kc=kc, nh=nh, b0=b0, ci=ci: e.matmul(
                            psf(b0 + nh, 0, 128, 0, 512),
                            t_xnfm[:, kc * PT_ + ci * 128:kc * PT_ + (ci + 1) * 128],
                            t_wt["wgt"][:, kc * D + nh * 512:kc * D + (nh + 1) * 512],
                            start=(kc == 0), stop=(kc == 7)),
                        deps=[xn_evs[kc], wev["wgt"][kc]] + (bd if kc == 0 else [])))
            gps = ps_t[0:128, 512 * b0:512 * b0 + 1024]
            pg = [prev_gate_g] if prev_gate_g else []
            pg2 = [ev_gate_w[cix - 1]] if cix >= 1 else []
            ev_t = P.step("scalar", lambda e, gps=gps: e.activation(
                t_gth[:, :], gps, AF.Tanh, scale=0.5), deps=mme + pg)
            ev_sg = P.step("vector", lambda e: e.tensor_scalar(
                t_gsg[:, :], t_gth[:, :], 0.5, 0.5, ALU.mult, ALU.add),
                deps=[ev_t] + pg)
            ev_g = P.step("vector", lambda e, gps=gps: e.tensor_mul(
                t_gout[:, :], gps, t_gsg[:, :]), deps=[ev_sg] + pg2)
            prev_gate_g = ev_g
            bank_done(b0, 2, [ev_g])
            ev_gate_w[cix] = P.dma("sync", s_gate[cix * 128:(cix + 1) * 128, :],
                                   t_gout[:, :], deps=[ev_g])
            xnfm_guard = [mme[-1]]

    if STAGE <= 2:
        return _cut(P.fence_all())
    a1_fence = P.fence_all()
    sub_ctx.close()
    t_abl = asb("t_abl", [64, 128], F32)
    t_kkdbg = (asb("t_kkdbg", [128, 512], F32) if DEBUG_SCRATCH else None)
    t_la = asb("t_la", [16, 128], F32)
    t_la2 = asb("t_la2", [16, 128], F32)
    t_z16 = asb("t_z16", [16, 128], F32)
    t_c = asb("t_c", [16, 128], F32)
    t_chb = asb("t_chb", [16, 128], BF16)
    t_chf = asb("t_chf", [16, 128], F32)
    t_clb = asb("t_clb", [16, 128], BF16)
    t_cprev = asb("t_cprev", [16, 128], F32)
    t_dend = asb("t_dend", [16, 128], F32)
    t_ecp = asb("t_ecp", [16, 128], F32)
    t_ecbf = asb("t_ecbf", [16, 128], BF16)
    t_tm1 = asb("t_tm1", [128, 80], F32)
    t_becptm = asb("t_becptm", [128, 16], F32)
    t_cath = asb("t_cath", [1, 512], BF16)
    t_catl = asb("t_catl", [1, 512], BF16)
    t_dbuf = asb("t_dbuf", [128, 1024], BF16)
    t_ebuf = asb("t_ebuf", [128, 512], F32)
    t_e1m = asb("t_e1m", [128, 512], F32)
    t_e2m = asb("t_e2m", [128, 512], F32)
    t_xb = asb("t_xb", [128, 2048], BF16)
    t_xp = [asb(f"t_xp{i}", [128, 2048], BF16) for i in range(2)]
    t_xpt = [asb(f"t_xpt{i}", [128, 2048], BF16) for i in range(2)]
    t_xtb = t_xpt[1]
    t_zb = [asb(f"t_zb{i}", [128, 2048], BF16) for i in range(2)]
    t_ftb = asb("t_ftb", [64, 2048], BF16)
    t_kdb = asb("t_kdb", [128, 1024], BF16)
    t_rb = asb("t_rb", [128, 2048], BF16)
    t_ptb = asb("t_ptb", [128, 2048], BF16)
    t_qtb = asb("t_qtb", [128, 1024], BF16)
    t_gib = asb("t_gib", [64, 1024], F32)
    t_atb = asb("t_atb", [64, 1024], BF16)
    t_ub = asb("t_ub", [64, 1024], BF16)
    t_acc = [asb(f"t_acc{i}", [64, 1024], BF16) for i in range(2)]
    t_acct = [asb(f"t_acct{i}", [64, 1024], BF16) for i in range(2)]
    t_ucc = [asb(f"t_ucc{i}", [64, 1024], BF16) for i in range(2)]

    ev_acc0 = P.step("vector", lambda e: e.tensor_copy(
        t_acc[1][:, :], t_con["i64rep_bf"][:, :]),
        deps=[evc0] + list(a1_fence))
    ev_acct0 = P.step("vector", lambda e: e.tensor_copy(
        t_acct[1][:, :], t_con["i64rep_bf"][:, :]), deps=[evc0])
    ev_ucc0 = P.step("vector", lambda e: e.memset(t_ucc[1][:, :], 0.0))
    ev_z16 = P.step("vector", lambda e: e.memset(t_z16[:, :], 0.0))
    compose_prev = (ev_acc0, ev_acct0, ev_ucc0)

    prev_rn = None
    tile_fence = [()] * NPT
    chunk_fence = a1_fence
    ev_chunk_done = [None] * NCH
    ev_gamma = [None] * NCH
    ev_kfm_w = [None] * NPT
    ktm_guard = [()] * 2
    eacctF = euccF = None

    for ti in range(NPT):
        par = ti % 2
        qof = par * 8 * PT_
        deps = [ev_xn_w[ti * CPT + i] for i in range(CPT)] + xnfm_guard
        xn_evs = [P.dma("sync", t_xnfm[:, dc * PT_:(dc + 1) * PT_],
                        s_xn[ti * PT_:(ti + 1) * PT_, dc * 128:(dc + 1) * 128],
                        deps=deps, transpose=True) for dc in range(8)]
        guard_prev = list(tile_fence[ti - 2]) if ti >= 2 else []
        tile_readers = []

        qn_apply = kn_apply = None
        for tname in ("q", "k"):
            wtile = t_wt["wq"] if tname == "q" else t_wt["wk"]
            wevs = wev["wq"] if tname == "q" else wev["wk"]
            dstb = t_qn if tname == "q" else t_kn
            sq_evs = []
            qbf_evs = []
            for oc in range(8):
                b0, bd = bank(1)
                mme = [P.step("tensor",
                              lambda e, oc=oc, kc=kc, b0=b0, wtile=wtile: e.matmul(
                    psf(b0, 0, 128, 0, PT_),
                    wtile[:, kc * D + oc * 128:kc * D + (oc + 1) * 128],
                    t_xnfm[:, kc * PT_:(kc + 1) * PT_],
                    start=(kc == 0), stop=(kc == 7)),
                    deps=[xn_evs[kc], wevs[kc]] + (bd if kc == 0 else []))
                    for kc in range(8)]
                e1 = P.step("vector", lambda e, oc=oc, b0=b0: e.tensor_copy(
                    t_qbf[:, oc * PT_:(oc + 1) * PT_], psf(b0, 0, 128, 0, PT_)),
                    deps=mme)
                e2 = P.step("scalar", lambda e, oc=oc: e.activation(
                    t_sqb[:, oc * PT_:(oc + 1) * PT_],
                    t_qbf[:, oc * PT_:(oc + 1) * PT_],
                    AF.Square), deps=[e1])
                bank_done(b0, 1, [e1])
                sq_evs.append(e2)
                qbf_evs.append(e1)
            nss = 8 * PT_ // 512
            b0, bd = bank(nss, hold=True)
            ssm = [P.step("tensor", lambda e, i4=i4, b0=b0: e.matmul(
                psf(b0 + i4, 0, 2, 0, 512), t_con["headones_bf"][:, :],
                t_sqb[:, 512 * i4:512 * (i4 + 1)], start=True, stop=True),
                deps=sq_evs + (bd if i4 == 0 else []) + [evc0])
                for i4 in range(nss)]
            eh4s = []
            applies = []
            hlen = 4 * PT_
            for hf in range(2):
                ssps = ps_t[0:2, 512 * b0 + hf * hlen:512 * b0 + (hf + 1) * hlen]
                eh1 = P.step("vector", lambda e, ssps=ssps, hf=hf:
                             e.tensor_scalar_max(
                    t_rn[:, 0:hlen], ssps, 1e-24),
                    deps=ssm + (eh4s if hf else []) +
                         ([prev_rn] if prev_rn else []))
                eh2 = P.step("scalar", lambda e: e.activation(
                    t_rn2[:, 0:hlen], t_rn[:, 0:hlen], AF.Ln), deps=[eh1])
                eh3 = P.step("scalar", lambda e: e.activation(
                    t_rn[:, 0:hlen], t_rn2[:, 0:hlen], AF.Exp, scale=-0.5),
                    deps=[eh2, eh1])
                eh4 = P.step("vector", lambda e, hf=hf: e.tensor_copy(
                    t_rnb[:, 0:hlen], t_rn[:, 0:hlen]),
                    deps=[eh3] + (applies[-2:] if hf else []))
                eh4s.append(eh4)
                prev_rn = eh3
                for oc in range(hf * 4, hf * 4 + 4):
                    b2, bd2 = bank(1)
                    em = P.step("tensor", lambda e, oc=oc, b2=b2, hf=hf: e.matmul(
                        psf(b2, 0, 128, 0, PT_), t_con["sel2t_bf"][:, :],
                        t_rnb[:, (oc - hf * 4) * PT_:(oc - hf * 4 + 1) * PT_],
                        start=True, stop=True),
                        deps=[eh4] + bd2)
                    ea = P.step("vector",
                                lambda e, oc=oc, b2=b2, dstb=dstb, qof=qof:
                                e.tensor_mul(
                        dstb[:, qof + oc * PT_:qof + (oc + 1) * PT_],
                        t_qbf[:, oc * PT_:(oc + 1) * PT_],
                        psf(b2, 0, 128, 0, PT_)),
                        deps=[em, qbf_evs[oc]] + (guard_prev if oc == 0 else []))
                    bank_done(b2, 1, [ea])
                    applies.append(ea)
            bank_done(b0, nss, eh4s)
            if tname == "q":
                qn_apply = applies
            else:
                kn_apply = applies

        ev_kfm_w[ti] = P.dma(
            "sync",
            s_kfm.rearrange("(dc p) t -> p dc t", p=128)[
                :, :, ti * PT_:(ti + 1) * PT_],
            t_kn[:, qof:qof + 8 * PT_].rearrange("p (dc t) -> p dc t", dc=8),
            deps=kn_apply)

        v_evs = []
        for ci in range(CPT):
            b0, bd = bank(2)
            mme = []
            for kc in range(8):
                for nh in range(2):
                    mme.append(P.step(
                        "tensor",
                        lambda e, kc=kc, nh=nh, b0=b0, ci=ci: e.matmul(
                            psf(b0 + nh, 0, 128, 0, 512),
                            t_xnfm[:, kc * PT_ + ci * 128:kc * PT_ + (ci + 1) * 128],
                            t_wt["wv"][:, kc * D + nh * 512:kc * D + (nh + 1) * 512],
                            start=(kc == 0), stop=(kc == 7)),
                        deps=[xn_evs[kc], wev["wv"][kc]] + (bd if kc == 0 else [])))
            ev = P.step("vector", lambda e, b0=b0, par=par, ci=ci: e.tensor_copy(
                t_vb[:, (par * CPT + ci) * D:(par * CPT + ci + 1) * D],
                ps_t[0:128, 512 * b0:512 * b0 + 1024]),
                deps=mme + (guard_prev if ci == 0 else []))
            bank_done(b0, 2, [ev])
            v_evs.append(ev)
        xnfm_guard = [mme[-1]]

        for ci in range(CPT if SUB >= 2 else 0):
            cix = ti * CPT + ci
            ks = cix % 2
            P.begin_fence()
            cf = list(chunk_fence)
            ev_ktm = P.dma("sync", t_ktm[:, ks * D:(ks + 1) * D],
                           s_kfm[:, cix * 128:(cix + 1) * 128],
                           deps=[ev_kfm_w[ti]] + list(ktm_guard[ks]),
                           transpose=True)

            if SUB <= 2:
                chunk_fence = P.end_fence()
                continue
            labl = P.dma("sync", t_abl[:, :],
                         s_ab[ti, :, ci * 128:(ci + 1) * 128],
                         deps=[ev_ab_tile[ti]] + cf)
            e1 = P.step("vector", lambda e: e.tensor_scalar_max(
                t_la[:, :], t_abl[0:16, :], 1e-37), deps=[labl] + cf)
            e2 = P.step("scalar", lambda e: e.activation(
                t_la2[:, :], t_la[:, :], AF.Ln), deps=[e1] + cf)
            e3 = P.step("vector", lambda e: e.tensor_tensor_scan(
                t_c[:, :], t_la2[:, :], t_z16[:, :], 0.0, ALU.add, ALU.add),
                deps=[e2, ev_z16])
            e4 = P.step("vector", lambda e: e.tensor_sub(
                t_cprev[:, :], t_c[:, :], t_la2[:, :]), deps=[e3])
            e6 = P.step("scalar", lambda e: e.activation(
                t_dend[:, :], t_c[:, :], AF.Exp,
                bias=t_c[:, 127:128], scale=-1.0), deps=[e3])
            e7 = P.step("scalar", lambda e: e.activation(
                t_ecp[:, :], t_cprev[:, :], AF.Exp), deps=[e4])
            e9 = P.step("scalar", lambda e: e.activation(
                t_ecbf[:, :], t_c[:, :], AF.Exp), deps=[e3])

            b0, bd = bank(2)
            trs = []
            for gi_, (tin, pb) in enumerate([(t_c[:, :], 0),
                                             (t_abl[32:48, :], 32),
                                             (t_cprev[:, :], 0),
                                             (t_dend[:, :], 0),
                                             (t_ecp[:, :], 0)]):
                ident = t_con["eye_f32"][pb:pb + 16, pb:pb + 16]
                dep_t = {0: [e3], 1: [labl], 2: [e4], 3: [e6],
                         4: [e7]}[gi_]
                bno = 1 if pb == 32 else 0
                trs.append(P.step("tensor",
                                  lambda e, gi_=gi_, tin=tin, ident=ident,
                                  b0=b0, bno=bno:
                                  e.transpose(
                    psf(b0 + bno, 0, 128, gi_ * 16, 16), tin, ident),
                    deps=dep_t + (bd if gi_ == 0 else []) + [evc0] + cf))
            e11a = P.step("vector", lambda e, b0=b0: e.tensor_copy(
                t_tm1[:, 0:16], psf(b0, 0, 128, 0, 16)),
                deps=[trs[0]] + cf)
            e11b = P.step("vector", lambda e, b0=b0: e.tensor_copy(
                t_tm1[:, 16:32], psf(b0 + 1, 0, 128, 16, 16)),
                deps=[trs[1]] + cf)
            e11c = P.step("vector", lambda e, b0=b0: e.tensor_copy(
                t_tm1[:, 32:80], psf(b0, 0, 128, 32, 48)),
                deps=trs + cf)
            e11 = P.step("vector", lambda e: e.engine_nop(),
                         deps=[e11a, e11b, e11c])
            bank_done(b0, 2, [e11a, e11b, e11c])
            e8 = P.step("vector", lambda e: e.tensor_mul(
                t_becptm[:, :], t_tm1[:, 16:32], t_tm1[:, 64:80]),
                deps=[e11] + cf)

            if DEBUG_SCRATCH:
                P.dma("sync", s_tmdbg[cix, :, :], t_tm1[:, :],
                      deps=[e11a, e11b, e11c, e8])
            ech = P.step("vector", lambda e: e.tensor_copy(
                t_chb[:, :], t_c[:, :]), deps=[e3] + cf)
            echf = P.step("vector", lambda e: e.tensor_copy(
                t_chf[:, :], t_chb[:, :]), deps=[ech])
            ecl = P.step("vector", lambda e: e.tensor_sub(
                t_clb[:, :], t_c[:, :], t_chf[:, :]), deps=[echf])
            e14a = P.dma("sync", s_cath[cix, :], t_chb[:, :], deps=[ech])
            e14b = P.dma("sync", s_catl[cix, :], t_clb[:, :], deps=[ecl])

            b0, bd = bank(1)
            gme = []
            for h in range(16):
                gme.append(P.step("tensor", lambda e, h=h, b0=b0: e.matmul(
                    ps_t[0:64, 512 * b0 + h:512 * b0 + h + 1],
                    t_con["selh_bf"][:, h * 64:(h + 1) * 64],
                    t_chb[:, 127:128], start=True, stop=False),
                    deps=[ech, evc0] + (bd if h == 0 else []) + cf))
                gme.append(P.step("tensor", lambda e, h=h, b0=b0: e.matmul(
                    ps_t[0:64, 512 * b0 + h:512 * b0 + h + 1],
                    t_con["selh_bf"][:, h * 64:(h + 1) * 64],
                    t_clb[:, 127:128], start=False, stop=True),
                    deps=[ecl]))
            ev_gamma[cix] = P.step("scalar", lambda e, b0=b0, cix=cix:
                                   e.activation(
                t_gcols[:, cix * 16:(cix + 1) * 16],
                ps_t[0:64, 512 * b0:512 * b0 + 16], AF.Exp), deps=gme)
            bank_done(b0, 1, [ev_gamma[cix]])

            if SUB <= 3:
                chunk_fence = P.end_fence()
                continue
            kd3 = t_kdb[:, :].rearrange("p (h e) -> p h e", h=16)
            ktm3 = t_ktm[:, ks * D:(ks + 1) * D].rearrange(
                "p (h e) -> p h e", h=16)
            dend_b = t_tm1[:, 48:64].unsqueeze(2).broadcast_to([128, 16, 64])
            e16 = P.step("vector", lambda e, kd3=kd3, ktm3=ktm3, dend_b=dend_b:
                         e.tensor_tensor(kd3, ktm3, dend_b, ALU.mult),
                         deps=[e11, ev_ktm] + cf)
            r4 = t_rb[:, :].rearrange("p (h t e) -> p h t e", h=16, t=2)
            v3 = t_vb[:, (par * CPT + ci) * D:(par * CPT + ci + 1) * D].rearrange(
                "p (h e) -> p h e", h=16)
            b_b = t_tm1[:, 16:32].unsqueeze(2).broadcast_to([128, 16, 64])
            becp_b = t_becptm[:, :].unsqueeze(2).broadcast_to([128, 16, 64])
            e17 = P.step("vector", lambda e, r4=r4, v3=v3, b_b=b_b:
                         e.tensor_tensor(r4[:, :, 0, :], v3, b_b, ALU.mult),
                         deps=[e11, v_evs[ci]] + cf)
            e18 = P.step("vector", lambda e, r4=r4, ktm3=ktm3, becp_b=becp_b:
                         e.tensor_tensor(r4[:, :, 1, :], ktm3, becp_b, ALU.mult),
                         deps=[e8, ev_ktm] + cf)
            tile_readers += [e16, e17, e18]

            x_evs = [None] * 16
            pt_evs = []
            prev_ecb = None
            prev_ee3 = None
            prev_ee4 = None
            prev_xg_gp = []
            prev_ptg_gp = []
            for g in range(4):
                e15a = P.dma("sync", t_cath[:, :],
                             s_cath[cix:cix + 1, g * 512:(g + 1) * 512],
                             deps=[e14a] + cf + ([prev_ecb] if prev_ecb else []))
                e15b = P.dma("sync", t_catl[:, :],
                             s_catl[cix:cix + 1, g * 512:(g + 1) * 512],
                             deps=[e14b] + cf + ([prev_ecb] if prev_ecb else []))
                bcb, bdc = bank(1)
                ecb1 = P.step("tensor", lambda e, g=g, bcb=bcb: e.matmul(
                    psf(bcb, 0, 128, 0, 512), t_con["ones1_bf"][:, :],
                    t_cath[:, :], start=True, stop=False),
                    deps=[e15a, evc0] + bdc + cf)
                ecb = P.step("tensor", lambda e, g=g, bcb=bcb: e.matmul(
                    psf(bcb, 0, 128, 0, 512), t_con["ones1_bf"][:, :],
                    t_catl[:, :], start=False, stop=True),
                    deps=[e15b, ecb1])
                prev_ecb = ecb
                d_evs = [P.step("vector", lambda e, hh=hh, h=g * 4 + hh, bcb=bcb:
                                e.tensor_scalar(
                    t_dbuf[:, hh * 128:(hh + 1) * 128],
                    psf(bcb, 0, 128, hh * 128, 128),
                    t_tm1[:, 32 + h:33 + h], 0.0, ALU.subtract, ALU.max),
                    deps=[ecb, e11] + ([prev_ee3] if prev_ee3 else []))
                    for hh in range(4)]
                if SUB <= 32:
                    continue
                ee1 = P.step("scalar", lambda e: e.activation(
                    t_ebuf[:, :], t_dbuf[:, 0:512], AF.Exp, scale=-1.0),
                    deps=d_evs + ([prev_ee4] if prev_ee4 else []))
                nl_b = t_con["neglmask_bf"][:, :].unsqueeze(1).broadcast_to(
                    [128, 4, 128])
                e3d = t_e1m[:, :].rearrange("p (hh t) -> p hh t", hh=4)
                eb3 = t_ebuf[:, :].rearrange("p (hh t) -> p hh t", hh=4)
                ee2 = P.step("vector", lambda e, e3d=e3d, eb3=eb3, nl_b=nl_b:
                             e.tensor_tensor(e3d, eb3, nl_b, ALU.mult),
                             deps=[ee1, evc0] + prev_xg_gp)
                d2_evs = [P.step("vector", lambda e, hh=hh, h=g * 4 + hh, bcb=bcb:
                                 e.tensor_scalar(
                    t_dbuf[:, hh * 128:(hh + 1) * 128],
                    psf(bcb, 0, 128, hh * 128, 128),
                    t_tm1[:, 0 + h:1 + h], 0.0, ALU.subtract, ALU.min),
                    deps=[ecb, e11, ee1]) for hh in range(4)]
                ee3 = P.step("scalar", lambda e: e.activation(
                    t_ebuf[:, :], t_dbuf[:, 0:512], AF.Exp),
                    deps=d2_evs + [ee2])
                bank_done(bcb, 1, [ee3])
                um_b = t_con["umask_bf"][:, :].unsqueeze(1).broadcast_to(
                    [128, 4, 128])
                e4d = t_e2m[:, :].rearrange("p (hh t) -> p hh t", hh=4)
                ee4 = P.step("vector", lambda e, e4d=e4d, eb3=eb3, um_b=um_b:
                             e.tensor_tensor(e4d, eb3, um_b, ALU.mult),
                             deps=[ee3] + prev_ptg_gp)
                prev_ee3 = ee3
                prev_ee4 = ee4

                if SUB <= 33:
                    continue
                if DEBUG_SCRATCH:
                    P.dma("sync", s_ddbg[cix, g, :, :].bitcast(BF16)[:, 0:512],
                          t_dbuf[:, 0:512], deps=d2_evs)
                    P.dma("sync", s_emdbg[cix, g, :, :], t_e1m[:, :],
                          deps=[ee2])
                bkk, bdk = bank(2)
                kk_evs = []
                for hh in range(4):
                    h = g * 4 + hh
                    pb = 64 * (h % 2)
                    dcq = h // 2
                    sl_k = t_kn[pb:pb + 64, qof + dcq * PT_ + ci * 128:
                                qof + dcq * PT_ + (ci + 1) * 128]
                    kk_evs.append(P.step("tensor",
                                         lambda e, hh=hh, bkk=bkk, sl_k=sl_k:
                                         e.matmul(
                        psf(bkk + hh % 2, 0, 128, (hh // 2) * 128, 128),
                        sl_k, sl_k, start=True, stop=True),
                        deps=kn_apply + (bdk if hh == 0 else [])))
                if SUB <= 335:
                    bank_done(bkk, 2, kk_evs)
                    continue
                if DEBUG_SCRATCH:
                    ekkd = P.step("vector", lambda e, bkk=bkk: e.tensor_copy(
                        t_kkdbg[:, :],
                        ps_t[0:128, 512 * bkk:512 * bkk + 512]),
                        deps=kk_evs)
                    edma1 = P.dma("sync", s_kkdbg[cix, g, :, :], t_kkdbg[:, :],
                                  deps=[ekkd])
                xg = []
                dbuf_bf = t_dbuf[:, 512:1024]
                for hh in range(4):
                    h = g * 4 + hh
                    if SUB == 336:
                        exg = P.step("vector", lambda e, hh=hh, h=h, bkk=bkk:
                                     e.tensor_copy(
                            t_xb[:, h * 128:(h + 1) * 128],
                            psf(bkk + hh % 2, 0, 128, (hh // 2) * 128, 128)),
                            deps=kk_evs + [ee2, e11])
                    elif SUB == 337:
                        exg = P.step("vector", lambda e, hh=hh, h=h, bkk=bkk:
                                     e.scalar_tensor_tensor(
                            t_xb[:, h * 128:(h + 1) * 128],
                            psf(bkk + hh % 2, 0, 128, (hh // 2) * 128, 128),
                            1.0, t_e1m[:, hh * 128:(hh + 1) * 128],
                            ALU.mult, ALU.mult),
                            deps=kk_evs + [ee2, e11])
                    elif SUB == 338:
                        exg = P.step("vector", lambda e, hh=hh, h=h, bkk=bkk:
                                     e.tensor_scalar(
                            t_xb[:, h * 128:(h + 1) * 128],
                            psf(bkk + hh % 2, 0, 128, (hh // 2) * 128, 128),
                            t_tm1[:, 16 + h:17 + h], None, ALU.mult),
                            deps=kk_evs + [ee2, e11])
                    else:
                        ex1 = P.step("vector",
                                     lambda e, hh=hh, h=h, bkk=bkk,
                                     dbuf_bf=dbuf_bf: e.tensor_scalar(
                            dbuf_bf[:, hh * 128:(hh + 1) * 128],
                            psf(bkk + hh % 2, 0, 128, (hh // 2) * 128, 128),
                            t_tm1[:, 16 + h:17 + h], None, ALU.mult),
                            deps=kk_evs + [ee3, e11])
                        exg = P.step("gpsimd",
                                     lambda e, hh=hh, h=h, dbuf_bf=dbuf_bf:
                                     e.tensor_tensor(
                            t_xb[:, h * 128:(h + 1) * 128],
                            dbuf_bf[:, hh * 128:(hh + 1) * 128],
                            t_e1m[:, hh * 128:(hh + 1) * 128], ALU.mult),
                            deps=[ex1, ee2])
                    xg.append(exg)
                    x_evs[h] = exg
                if DEBUG_SCRATCH:
                    ekkd2 = P.step("vector", lambda e, bkk=bkk: e.tensor_copy(
                        t_kkdbg[:, :],
                        ps_t[0:128, 512 * bkk:512 * bkk + 512]),
                        deps=xg + [edma1])
                    edma2 = P.dma("sync", s_kkdbg2[cix, g, :, :], t_kkdbg[:, :],
                                  deps=[ekkd2])
                    xg = xg + [ekkd2, edma2]
                bank_done(bkk, 2, xg)
                prev_xg_gp = [xg[-1]]
                tile_readers.append(kk_evs[-1])

                if SUB <= 34:
                    continue
                bkq, bdq = bank(2)
                kq_evs = []
                for hh in range(4):
                    h = g * 4 + hh
                    pb = 64 * (h % 2)
                    dcq = h // 2
                    sl_k = t_kn[pb:pb + 64, qof + dcq * PT_ + ci * 128:
                                qof + dcq * PT_ + (ci + 1) * 128]
                    sl_q = t_qn[pb:pb + 64, qof + dcq * PT_ + ci * 128:
                                qof + dcq * PT_ + (ci + 1) * 128]
                    kq_evs.append(P.step(
                        "tensor",
                        lambda e, hh=hh, bkq=bkq, sl_k=sl_k, sl_q=sl_q: e.matmul(
                            psf(bkq + hh % 2, 0, 128, (hh // 2) * 128, 128),
                            sl_k, sl_q, start=True, stop=True),
                        deps=kn_apply + qn_apply + (bdq if hh == 0 else [])))
                ptg = []
                for hh in range(4):
                    h = g * 4 + hh
                    pp1 = P.step("vector",
                                 lambda e, hh=hh, bkq=bkq, dbuf_bf=dbuf_bf:
                                 e.tensor_copy(
                        dbuf_bf[:, hh * 128:(hh + 1) * 128],
                        psf(bkq + hh % 2, 0, 128, (hh // 2) * 128, 128)),
                        deps=kq_evs + xg)
                    ptg.append(P.step("gpsimd",
                                      lambda e, hh=hh, h=h, dbuf_bf=dbuf_bf:
                                      e.tensor_tensor(
                        t_ptb[:, h * 128:(h + 1) * 128],
                        dbuf_bf[:, hh * 128:(hh + 1) * 128],
                        t_e2m[:, hh * 128:(hh + 1) * 128], ALU.mult),
                        deps=[pp1, ee4]))
                pt_evs += ptg
                prev_ptg_gp = [ptg[-1]]
                bank_done(bkq, 2, ptg)
                tile_readers.append(kq_evs[-1])

            if SUB <= 40:
                chunk_fence = P.end_fence()
                continue
            if DEBUG_SCRATCH:
                P.dma("sync", s_xdbg[cix, :, :], t_xb[:, :],
                      deps=[e for e in x_evs])
            xt_evs = []
            for half in range(2):
                tevs = [P.step("tensor", lambda e, h=half * 8 + hh, hh=hh:
                               e.transpose(
                    psb_t[0:128, hh * 128:(hh + 1) * 128],
                    t_xb[:, h * 128:(h + 1) * 128], t_con["eye_bf"][:, :]),
                    deps=[x_evs[half * 8 + hh], evc0] +
                         (list(psb_free[0]) if hh == 0 else []))
                    for hh in range(8)]
                exv = P.step("vector", lambda e, half=half: e.tensor_copy(
                    t_xtb[:, half * 1024:(half + 1) * 1024],
                    psb_t[0:128, 0:1024]), deps=tevs)
                psb_free[0] = (exv,)
                xt_evs.append(exv)

            zpar = 0
            zev = [None, None]
            xsrc, xtsrc = t_xb, t_xtb
            xsrc_ev = [ev for ev in x_evs]
            xtsrc_ev = xt_evs
            asrc_first = [e17, e18]
            for lev in range(NL):
                asrc = t_rb if lev == 0 else t_zb[1 - zpar]
                asrc_ev = asrc_first if lev == 0 else [zev[1 - zpar]]
                for half in range(2):
                    ba, bda = bank(2)
                    mms = [P.step(
                        "tensor",
                        lambda e, h=half * 8 + hh, hh=hh, ba=ba, asrc=asrc,
                        xtsrc=xtsrc: e.matmul(
                            ps_t[0:128,
                                 512 * ba + hh * 128:512 * ba + (hh + 1) * 128],
                            xtsrc[:, h * 128:(h + 1) * 128],
                            asrc[:, h * 128:(h + 1) * 128],
                            start=True, stop=True),
                        deps=(xtsrc_ev if hh == 0 else []) +
                             (asrc_ev if hh == 0 else []) +
                             (bda if hh == 0 else []))
                        for hh in range(8)]
                    eva = P.step("vector",
                                 lambda e, half=half, ba=ba, asrc=asrc, zp=zpar:
                                 e.scalar_tensor_tensor(
                        t_zb[zp][:, half * 1024:(half + 1) * 1024],
                        ps_t[0:128, 512 * ba:512 * ba + 1024], 1.0,
                        asrc[:, half * 1024:(half + 1) * 1024],
                        ALU.mult, ALU.add), deps=mms + asrc_ev +
                        ([zev[zpar]] if zev[zpar] else []))
                    bank_done(ba, 2, [eva])
                    zev[zpar] = eva
                zpar = 1 - zpar

                if lev < NL - 1:
                    xdst = t_xp[lev % 2]
                    xtdst = t_xpt[lev % 2]
                    new_x_ev = []
                    new_xt_ev = []
                    for which in range(2):
                        lop = xtsrc if which == 0 else xsrc
                        rop = xsrc if which == 0 else xtsrc
                        dst = xdst if which == 0 else xtdst
                        for half in range(2):
                            bs, bds = bank(2)
                            mms = [P.step(
                                "tensor",
                                lambda e, h=half * 8 + hh, hh=hh, bs=bs,
                                lop=lop, rop=rop: e.matmul(
                                    ps_t[0:128, 512 * bs + hh * 128:
                                         512 * bs + (hh + 1) * 128],
                                    lop[:, h * 128:(h + 1) * 128],
                                    rop[:, h * 128:(h + 1) * 128],
                                    start=True, stop=True),
                                deps=((xsrc_ev + xtsrc_ev) if hh == 0 else []) +
                                     (bds if hh == 0 else []))
                                for hh in range(8)]
                            if which == 0:
                                ev2 = P.step("vector",
                                             lambda e, half=half, bs=bs, dst=dst:
                                             e.tensor_copy(
                                    dst[:, half * 1024:(half + 1) * 1024],
                                    ps_t[0:128, 512 * bs:512 * bs + 1024]),
                                    deps=mms)
                                new_x_ev.append(ev2)
                            else:
                                ev2 = P.step("scalar",
                                             lambda e, half=half, bs=bs, dst=dst:
                                             e.copy(
                                    dst[:, half * 1024:(half + 1) * 1024],
                                    ps_t[0:128, 512 * bs:512 * bs + 1024]),
                                    deps=mms)
                                new_xt_ev.append(ev2)
                            bank_done(bs, 2, [ev2])
                    xsrc, xtsrc = xdst, xtdst
                    xsrc_ev, xtsrc_ev = new_x_ev, new_xt_ev
            zfin = t_zb[1 - zpar]
            zfin_ev = zev[1 - zpar]

            if SUB <= 5:
                chunk_fence = P.end_fence()
                continue
            ft_half = []
            for half in range(2):
                ftevs = [P.step("tensor", lambda e, h=half * 8 + hh, hh=hh,
                                zfin=zfin: e.transpose(
                    psb_t[0:64, hh * 128:(hh + 1) * 128],
                    zfin[:, h * 128 + 64:(h + 1) * 128], t_con["eye_bf"][:, :]),
                    deps=[zfin_ev, evc0] +
                         (list(psb_free[0]) if hh == 0 else []))
                    for hh in range(8)]
                efth = P.step("vector", lambda e, half=half: e.tensor_copy(
                    t_ftb[:, half * 1024:(half + 1) * 1024],
                    psb_t[0:64, 0:1024]), deps=ftevs)
                psb_free[0] = (efth,)
                ft_half.append(efth)
            eft = ft_half[-1]

            bq, bdq2 = bank(2)
            sel_evs = [P.step("tensor", lambda e, j=j, bq=bq: e.matmul(
                psf(bq, 0, 128, 0, 1024)[0:128, j * 128:(j + 1) * 128]
                if False else
                ps_t[0:128, 512 * bq + j * 128:512 * bq + (j + 1) * 128],
                t_con["selj_bf"][:, j * 128:(j + 1) * 128],
                t_ecbf[:, :], start=True, stop=True),
                deps=[e9, evc0] + (bdq2 if j == 0 else []))
                for j in range(8)]
            qn3 = t_qn[:, qof:qof + 8 * PT_].rearrange(
                "p (dc t) -> p dc t", dc=8)[:, :, ci * 128:(ci + 1) * 128]
            eqt = P.step("vector", lambda e, bq=bq, qn3=qn3: e.tensor_tensor(
                t_qtb[:, :].rearrange("p (dc t) -> p dc t", dc=8), qn3,
                ps_t[0:128, 512 * bq:512 * bq + 1024].rearrange(
                    "p (dc t) -> p dc t", dc=8), ALU.mult),
                deps=sel_evs + qn_apply)
            bank_done(bq, 2, [eqt])
            tile_readers.append(eqt)

            if SUB <= 6:
                chunk_fence = P.end_fence()
                continue
            bgg, bdg = bank(2)
            g_evs = [P.step("tensor", lambda e, h=h, bgg=bgg, zfin=zfin:
                            e.matmul(
                ps_t[0:64, 512 * bgg + h * 64:512 * bgg + (h + 1) * 64],
                zfin[:, h * 128 + 64:(h + 1) * 128],
                t_kdb[:, h * 64:(h + 1) * 64], start=True, stop=True),
                deps=[zfin_ev, e16] + (bdg if h == 0 else []))
                for h in range(16)]
            i64b = t_con["i64_f32"][:, :].unsqueeze(1).broadcast_to([64, 16, 64])
            gcb = t_gcols[:, cix * 16:(cix + 1) * 16].unsqueeze(2).broadcast_to(
                [64, 16, 64])
            egi = P.step("vector", lambda e, i64b=i64b, gcb=gcb: e.tensor_tensor(
                t_gib[:, :].rearrange("p (h e) -> p h e", h=16), i64b, gcb,
                ALU.mult), deps=[ev_gamma[cix], evc0] + cf)
            eat = P.step("vector", lambda e, bgg=bgg: e.scalar_tensor_tensor(
                t_atb[:, :], ps_t[0:64, 512 * bgg:512 * bgg + 1024], -1.0,
                t_gib[:, :], ALU.mult, ALU.add), deps=g_evs + [egi])
            bank_done(bgg, 2, [eat])
            bgu, bdu = bank(2)
            u_evs = [P.step("tensor", lambda e, h=h, bgu=bgu, zfin=zfin:
                            e.matmul(
                ps_t[0:64, 512 * bgu + h * 64:512 * bgu + (h + 1) * 64],
                t_kdb[:, h * 64:(h + 1) * 64],
                zfin[:, h * 128:h * 128 + 64], start=True, stop=True),
                deps=[zfin_ev, e16] + (bdu if h == 0 else []))
                for h in range(16)]
            eu = P.step("vector", lambda e, bgu=bgu: e.tensor_copy(
                t_ub[:, :], ps_t[0:64, 512 * bgu:512 * bgu + 1024]), deps=u_evs)
            bank_done(bgu, 2, [eu])

            cp = cix % 2
            cq = 1 - cp
            pa, pat, pu = compose_prev
            last = (cix == NCH - 1)
            bc1, bdc1 = bank(2)
            acc_m = [P.step("tensor", lambda e, h=h, bc1=bc1, cq=cq: e.matmul(
                ps_t[0:64, 512 * bc1 + h * 64:512 * bc1 + (h + 1) * 64],
                t_atb[:, h * 64:(h + 1) * 64],
                t_acc[cq][:, h * 64:(h + 1) * 64], start=True, stop=True),
                deps=[eat, pa] + (bdc1 if h == 0 else []))
                for h in range(16)]
            eacc = P.step("vector", lambda e, bc1=bc1, cp=cp: e.tensor_copy(
                t_acc[cp][:, :], ps_t[0:64, 512 * bc1:512 * bc1 + 1024]),
                deps=acc_m + [pa])
            bank_done(bc1, 2, [eacc])
            bc2, bdc2 = bank(2)
            acct_m = [P.step("tensor", lambda e, h=h, bc2=bc2, cq=cq: e.matmul(
                ps_t[0:64, 512 * bc2 + h * 64:512 * bc2 + (h + 1) * 64],
                t_acc[cq][:, h * 64:(h + 1) * 64],
                t_atb[:, h * 64:(h + 1) * 64], start=True, stop=True),
                deps=[eat, pa, pat] + (bdc2 if h == 0 else []))
                for h in range(16)]
            eacct = P.step("vector", lambda e, bc2=bc2, cp=cp: e.tensor_copy(
                t_acct[cp][:, :], ps_t[0:64, 512 * bc2:512 * bc2 + 1024]),
                deps=acct_m + [pat])
            evl2 = [eacct]
            if last:
                eacctF = P.step("vector", lambda e, bc2=bc2: e.tensor_copy(
                    t_accF[:, :], ps_t[0:64, 512 * bc2:512 * bc2 + 1024]),
                    deps=acct_m)
                evl2.append(eacctF)
            bank_done(bc2, 2, evl2)
            bc3, bdc3 = bank(2)
            ucc_m = []
            for h in range(16):
                ucc_m.append(P.step("tensor",
                                    lambda e, h=h, bc3=bc3, cq=cq: e.matmul(
                    ps_t[0:64, 512 * bc3 + h * 64:512 * bc3 + (h + 1) * 64],
                    t_atb[:, h * 64:(h + 1) * 64],
                    t_ucc[cq][:, h * 64:(h + 1) * 64], start=True, stop=False),
                    deps=[eat, pu] + (bdc3 if h == 0 else [])))
                ucc_m.append(P.step("tensor", lambda e, h=h, bc3=bc3: e.matmul(
                    ps_t[0:64, 512 * bc3 + h * 64:512 * bc3 + (h + 1) * 64],
                    t_con["i64rep_bf"][:, 0:64],
                    t_ub[:, h * 64:(h + 1) * 64], start=False, stop=True),
                    deps=[eu]))
            eucc = P.step("vector", lambda e, bc3=bc3, cp=cp: e.tensor_copy(
                t_ucc[cp][:, :], ps_t[0:64, 512 * bc3:512 * bc3 + 1024]),
                deps=ucc_m + [pu])
            evl3 = [eucc]
            if last:
                euccF = P.step("vector", lambda e, bc3=bc3: e.tensor_copy(
                    t_uccF[:, :], ps_t[0:64, 512 * bc3:512 * bc3 + 1024]),
                    deps=ucc_m)
                evl3.append(euccF)
            bank_done(bc3, 2, evl3)
            compose_prev = (eacc, eacct, eucc)

            w1 = P.dma("sync", s_z[cix, :, :], zfin[:, :], deps=[zfin_ev])
            w2 = P.dma("sync", s_ft[cix, :, :], t_ftb[:, :], deps=[eft])
            w3 = P.dma("sync", s_kd[cix, :, :], t_kdb[:, :], deps=[e16])
            w4 = P.dma("sync", s_pt[cix, :, :], t_ptb[:, :], deps=pt_evs)
            w5 = P.dma("sync",
                       s_qt[cix].rearrange("j p t -> p j t"),
                       t_qtb[:, :].rearrange("p (j t) -> p j t", j=8),
                       deps=[eqt])
            ev_chunk_done[cix] = (w1, w2, w3, w4, w5)
            chunk_fence = P.end_fence()
            ktm_guard[ks] = (e16, e18, w3)
        tile_fence[ti] = tuple(tile_readers)

    if STAGE <= 3 or SUB < 99:
        return _cut(P.fence_all())
    a_fence = P.fence_all()
    a_ctx.close()

    gb_ctx = ExitStack()
    t_s0 = gb_ctx.enter_context(nc.sbuf_tensor("t_s0", [64, 1024], F32))
    t_s0b = gb_ctx.enter_context(nc.sbuf_tensor("t_s0b", [64, 1024], BF16))
    g_ctx = ExitStack()
    t_gat = g_ctx.enter_context(nc.sbuf_tensor("t_gat", [64, 4096], F32))
    t_gau = g_ctx.enter_context(nc.sbuf_tensor("t_gau", [64, 4096], F32))
    t_sst = g_ctx.enter_context(nc.sbuf_tensor("t_sst", [64, 16 * 256], F32))
    t_selv = g_ctx.enter_context(nc.sbuf_tensor("t_selv", [1, 4], F32))
    t_sel64 = g_ctx.enter_context(nc.sbuf_tensor("t_sel64", [64, 4], F32))

    agi_at = s_agi.rearrange("(h ty r) e -> h ty r e", h=16, ty=2)
    ew1 = P.dma("sync", agi_at[:, 0, :, :].rearrange("h r e -> r h e"),
                t_accF[:, :].rearrange("r (h e) -> r h e", h=16),
                deps=[eacctF] + list(a_fence))
    ew2 = P.dma("sync", agi_at[:, 1, :, :].rearrange("h r e -> r h e"),
                t_uccF[:, :].rearrange("r (h e) -> r h e", h=16),
                deps=[euccF])
    ecc = P.step("gpsimd", lambda e: e.collective_compute(
        "AllGather", ALU.bypass,
        replica_groups=[[0, 1, 2, 3], [4, 5, 6, 7]],
        ins=[s_agi.opt()], outs=[s_ago.opt()]), deps=[ew1, ew2],
        semkey="scc")
    ago5 = s_ago.rearrange("(q h ty r) e -> q h ty r e", q=4, h=16, ty=2)
    gl_evs = []
    for q in range(4):
        gl_evs.append(P.dma(
            "sync",
            t_gat[:, q * 1024:(q + 1) * 1024].rearrange(
                "r (h e) -> r h e", h=16),
            ago5[q, :, 0, :, :].rearrange("h r e -> r h e"), deps=[ecc]))
        gl_evs.append(P.dma(
            "sync",
            t_gau[:, q * 1024:(q + 1) * 1024].rearrange(
                "r (h e) -> r h e", h=16),
            ago5[q, :, 1, :, :].rearrange("h r e -> r h e"), deps=[ecc]))
    ev_selv = P.dma("sync", t_selv[:, :], d_selv)
    bsl, bdsl = bank(1)
    esm = P.step("tensor", lambda e, bsl=bsl: e.matmul(
        ps_t[0:64, 512 * bsl:512 * bsl + 4], t_con["ones1"][0:1, 0:64],
        t_selv[:, :], start=True, stop=True), deps=[ev_selv, evc0] + bdsl)
    esel = P.step("vector", lambda e, bsl=bsl: e.tensor_copy(
        t_sel64[:, :], ps_t[0:64, 512 * bsl:512 * bsl + 4]), deps=[esm])
    bank_done(bsl, 1, [esel])

    sst5 = t_sst[:, :].rearrange("p (h e q) -> p h e q", h=16, q=4)
    bstage, bdst = bank(4, hold=True)
    ev_s0 = []
    prev_copy = None
    for h in range(16):
        prev_ev = prev_copy
        for q in range(4):
            pslot = ps_t[0:64, 512 * bstage + h * 64:512 * bstage + (h + 1) * 64]
            if q == 0:
                m1 = P.step("tensor", lambda e, h=h, pslot=pslot: e.matmul(
                    pslot, t_con["i64_f32"][:, :],
                    t_gau[:, h * 64:(h + 1) * 64],
                    start=True, stop=True),
                    deps=gl_evs + bdst + [evc0] +
                         ([prev_ev] if prev_ev else []))
            else:
                sst_prev = sst5[:, h, :, q - 1]
                m0 = P.step("tensor",
                            lambda e, h=h, q=q, pslot=pslot, sst_prev=sst_prev:
                            e.matmul(
                    pslot, t_gat[:, q * 1024 + h * 64:q * 1024 + (h + 1) * 64],
                    sst_prev, start=True, stop=False),
                    deps=gl_evs + [prev_ev])
                m1 = P.step("tensor", lambda e, h=h, q=q, pslot=pslot: e.matmul(
                    pslot, t_con["i64_f32"][:, :],
                    t_gau[:, q * 1024 + h * 64:q * 1024 + (h + 1) * 64],
                    start=False, stop=True), deps=[])
            prev_ev = P.step("vector", lambda e, pslot=pslot, h=h, q=q:
                             e.tensor_copy(sst5[:, h, :, q], pslot), deps=[m1])
        prev_copy = prev_ev
        ev_s0.append(prev_ev)
    bank_done(bstage, 4, ev_s0)
    sst4 = t_sst[:, :].rearrange("p (he q) -> p he q", q=4)
    sel_b = t_sel64[:, :].unsqueeze(1).broadcast_to([64, 1024, 4])
    emask = P.step("vector", lambda e, sst4=sst4, sel_b=sel_b: e.tensor_tensor(
        sst4, sst4, sel_b, ALU.mult), deps=ev_s0 + [esel])
    ered = P.step("vector", lambda e, sst4=sst4: e.tensor_reduce(
        t_s0[:, :].unsqueeze(2), sst4, AXX, ALU.add), deps=[emask])
    ev_s0f = P.step("vector", lambda e: e.tensor_copy(t_s0b[:, :], t_s0[:, :]),
                    deps=[ered])
    if STAGE <= 4:
        return _cut(P.fence_all())
    g_fence = P.fence_all()
    g_ctx.close()

    b_ctx = ExitStack()

    def bsb(name, shape, dtype):
        return b_ctx.enter_context(nc.sbuf_tensor(name, shape, dtype))

    t_wo = bsb("t_wo", [128, 8 * D], BF16)
    t_wld2 = bsb("t_wld2", [128, D], F32)
    t_zl = [bsb(f"t_zl{i}", [128, 2048], BF16) for i in range(2)]
    t_ftl = [bsb(f"t_ftl{i}", [64, 2048], BF16) for i in range(2)]
    t_kdl = [bsb(f"t_kdl{i}", [128, 1024], BF16) for i in range(2)]
    t_ptl = [bsb(f"t_ptl{i}", [128, 2048], BF16) for i in range(2)]
    t_qtl = [bsb(f"t_qtl{i}", [64, 2048], BF16) for i in range(2)]
    t_gl = [bsb(f"t_gl{i}", [128, D], BF16) for i in range(2)]
    t_wh = bsb("t_wh", [128, 1024], BF16)
    t_sg = bsb("t_sg", [64, 1024], BF16)
    t_sf = bsb("t_sf", [64, 1024], F32)
    t_sb2 = bsb("t_sb2", [64, 1024], BF16)
    t_ytm = bsb("t_ytm", [128, D], F32)
    t_ysq = bsb("t_ysq", [128, D], F32)
    t_yst = bsb("t_yst", [128, 4], F32)
    t_yout = bsb("t_yout", [128, D], BF16)

    woev = []
    wo_par = None
    for kc in range(8):
        ev1 = P.dma("sync", t_wld2[:, :], d_w["wo"][kc * 128:(kc + 1) * 128, :],
                    deps=[wo_par] if wo_par else list(g_fence))
        evcp = P.step("scalar", lambda e, kc=kc: e.copy(
            t_wo[:, kc * D:(kc + 1) * D], t_wld2[:, :]), deps=[ev1])
        wo_par = evcp
        woev.append(evcp)

    ev_sf = P.step("vector", lambda e: e.tensor_copy(t_sf[:, :], t_s0[:, :]),
                   deps=[ered] + list(g_fence))
    ev_sb = P.step("vector", lambda e: e.tensor_copy(t_sb2[:, :], t_s0b[:, :]),
                   deps=[ev_s0f] + list(g_fence))

    ld_guard = [(), ()]
    bfence = ()
    ev_y_w = [None] * NCH
    for cix in range(NCH):
        par = cix % 2
        P.begin_fence()
        bf = list(bfence)
        dch = list(ev_chunk_done[cix])
        lz = P.dma("sync", t_zl[par][:, :], s_z[cix, :, :],
                   deps=[dch[0]] + list(ld_guard[par]))
        lft = P.dma("sync", t_ftl[par][:, :], s_ft[cix, :, :], deps=[dch[1]])
        lkd = P.dma("sync", t_kdl[par][:, :], s_kd[cix, :, :], deps=[dch[2]])
        lpt = P.dma("sync", t_ptl[par][:, :], s_pt[cix, :, :], deps=[dch[3]])
        lqt = P.dma("sync",
                    t_qtl[par][:, :].rearrange("e (j hh t) -> e j hh t",
                                               j=8, hh=2),
                    s_qt[cix].rearrange("j (hh e) t -> e j hh t", hh=2),
                    deps=[dch[4]])
        lg = P.dma("sync", t_gl[par][:, :], s_gate[cix * 128:(cix + 1) * 128, :],
                   deps=[ev_gate_w[cix]])

        bfs, bdf = bank(2)
        fs_m = [P.step("tensor", lambda e, h=h, bfs=bfs, par=par: e.matmul(
            ps_t[0:128, 512 * bfs + h * 64:512 * bfs + (h + 1) * 64],
            t_ftl[par][:, h * 128:(h + 1) * 128],
            t_sb2[:, h * 64:(h + 1) * 64], start=True, stop=True),
            deps=[lft, ev_sb] + (bdf if h == 0 else []))
            for h in range(16)]
        w0v = t_zl[par][:, :].rearrange("p (h two e) -> p h two e",
                                        h=16, two=2)[:, :, 0, :]
        ewh = P.step("vector", lambda e, bfs=bfs, w0v=w0v: e.scalar_tensor_tensor(
            t_wh[:, :].rearrange("p (h e) -> p h e", h=16),
            ps_t[0:128, 512 * bfs:512 * bfs + 1024].rearrange(
                "p (h e) -> p h e", h=16),
            -1.0, w0v, ALU.mult, ALU.add), deps=fs_m + [lz] + bf)
        bank_done(bfs, 2, [ewh])

        bo, bdo = bank(2)
        o_m = []
        for h in range(16):
            o_m.append(P.step("tensor", lambda e, h=h, bo=bo, par=par: e.matmul(
                ps_t[0:128, 512 * bo + h * 64:512 * bo + (h + 1) * 64],
                t_qtl[par][:, h * 128:(h + 1) * 128],
                t_sb2[:, h * 64:(h + 1) * 64], start=True, stop=False),
                deps=[lqt, ev_sb] + (bdo if h == 0 else [])))
            o_m.append(P.step("tensor", lambda e, h=h, bo=bo, par=par: e.matmul(
                ps_t[0:128, 512 * bo + h * 64:512 * bo + (h + 1) * 64],
                t_ptl[par][:, h * 128:(h + 1) * 128],
                t_wh[:, h * 64:(h + 1) * 64], start=False, stop=True),
                deps=[lpt, ewh]))
        ey = P.step("vector", lambda e, bo=bo: e.tensor_copy(
            t_ytm[:, :], ps_t[0:128, 512 * bo:512 * bo + 1024]),
            deps=o_m + bf)
        bank_done(bo, 2, [ey])

        esg = P.step("vector", lambda e, cix=cix: e.tensor_tensor(
            t_sg[:, :].rearrange("p (h e) -> p h e", h=16),
            t_sb2[:, :].rearrange("p (h e) -> p h e", h=16),
            t_gcols[:, cix * 16:(cix + 1) * 16].unsqueeze(2).broadcast_to(
                [64, 16, 64]), ALU.mult), deps=[ev_sb, ev_gamma[cix]] + bf)
        bs2, bds2 = bank(2)
        s_m = []
        for h in range(16):
            s_m.append(P.step("tensor", lambda e, h=h, bs2=bs2, par=par:
                              e.matmul(
                ps_t[0:64, 512 * bs2 + h * 64:512 * bs2 + (h + 1) * 64],
                t_kdl[par][:, h * 64:(h + 1) * 64],
                t_wh[:, h * 64:(h + 1) * 64], start=True, stop=False),
                deps=[lkd, ewh] + (bds2 if h == 0 else [])))
            s_m.append(P.step("tensor", lambda e, h=h, bs2=bs2: e.matmul(
                ps_t[0:64, 512 * bs2 + h * 64:512 * bs2 + (h + 1) * 64],
                t_con["i64rep_bf"][:, 0:64],
                t_sg[:, h * 64:(h + 1) * 64], start=False, stop=True),
                deps=[esg]))
        ev_sf = P.step("vector", lambda e, bs2=bs2: e.tensor_copy(
            t_sf[:, :], ps_t[0:64, 512 * bs2:512 * bs2 + 1024]),
            deps=s_m + [ev_sf])
        ev_sb = P.step("vector", lambda e, bs2=bs2: e.tensor_copy(
            t_sb2[:, :], ps_t[0:64, 512 * bs2:512 * bs2 + 1024]),
            deps=s_m + [ev_sb] + [esg] + fs_m + o_m[:1])
        bank_done(bs2, 2, [ev_sf, ev_sb])

        er1 = P.step("scalar", lambda e: e.activation(
            t_ysq[:, :], t_ytm[:, :], AF.Square, accum_out=t_yst[:, 0:1]),
            deps=[ey] + bf)
        er2 = P.step("scalar", lambda e: e.activation(
            t_yst[:, 1:2], t_yst[:, 0:1], AF.Ln, scale=1.0 / D, bias=t_eps[:, :]),
            deps=[er1])
        er3 = P.step("scalar", lambda e: e.activation(
            t_yst[:, 2:3], t_yst[:, 1:2], AF.Exp, scale=-0.5), deps=[er2])
        er4 = P.step("vector", lambda e: e.scalar_tensor_tensor(
            t_ysq[:, :], t_ytm[:, :], t_yst[:, 2:3], t_gnb[:, :],
            ALU.mult, ALU.mult), deps=[er3, ev_gnb, er1])
        er5 = P.step("vector", lambda e, par=par: e.tensor_mul(
            t_yout[:, :], t_ysq[:, :], t_gl[par][:, :]), deps=[er4, lg] + bf)
        ev_y_w[cix] = P.dma("sync", s_y[cix * 128:(cix + 1) * 128, :],
                            t_yout[:, :], deps=[er5])
        bfence = P.end_fence()
        ld_guard[par] = (ewh, ey, er5, s_m[-1], o_m[-1], fs_m[-1], lqt)

    if STAGE <= 5:
        return _cut(P.fence_all())
    t_yfm = bsb("t_yfm", [128, 8 * G1K], BF16)
    t_xo = bsb("t_xo", [128, 2 * D], F32)
    t_att = bsb("t_att", [128, D], F32)
    t_asq = bsb("t_asq", [128, D], F32)
    t_ast = bsb("t_ast", [128, 32], F32)
    t_xm = bsb("t_xm", [128, D], F32)
    t_xn2b = bsb("t_xn2b", [128, D], BF16)

    ev_xmid_w = [None] * NCH
    ev_xn2_w = [None] * NCH
    yfm_guard = []
    xo_ev = [None, None]
    c1f = ()
    for gi in range(TOK // G1K):
        ydeps = [ev_y_w[gi * (G1K // 128) + i] for i in range(G1K // 128)]
        yevs = [P.dma("sync", t_yfm[:, dc * G1K:(dc + 1) * G1K],
                      s_y[gi * G1K:(gi + 1) * G1K, dc * 128:(dc + 1) * 128],
                      deps=ydeps + yfm_guard, transpose=True)
                for dc in range(8)]
        for cc in range(G1K // 128):
            cix = gi * (G1K // 128) + cc
            par = cix % 2
            so = par * 16
            P.begin_fence()
            cfl = list(c1f)
            b0, bd = bank(2)
            mme = []
            for kc in range(8):
                for nh in range(2):
                    mme.append(P.step(
                        "tensor", lambda e, kc=kc, nh=nh, b0=b0, cc=cc: e.matmul(
                            psf(b0 + nh, 0, 128, 0, 512),
                            t_yfm[:, kc * G1K + cc * 128:kc * G1K + (cc + 1) * 128],
                            t_wo[:, kc * D + nh * 512:kc * D + (nh + 1) * 512],
                            start=(kc == 0), stop=(kc == 7)),
                        deps=[yevs[kc], woev[kc]] + (bd if kc == 0 else [])))
            aps = ps_t[0:128, 512 * b0:512 * b0 + 1024]
            el1 = P.step("scalar", lambda e, aps=aps, so=so: e.activation(
                t_asq[:, :], aps, AF.Square, accum_out=t_ast[:, so:so + 1]),
                deps=mme + cfl)
            el2 = P.step("vector", lambda e, aps=aps, so=so: e.reduce_sum(
                t_ast[:, so + 1:so + 2], aps, AXX), deps=mme + cfl + [el1])
            el3 = P.step("vector", lambda e, so=so: e.tensor_scalar_mul(
                t_ast[:, so + 2:so + 3], t_ast[:, so + 1:so + 2], 1.0 / D),
                deps=[el2])
            el4 = P.step("vector", lambda e, so=so: e.tensor_mul(
                t_ast[:, so + 3:so + 4], t_ast[:, so + 2:so + 3],
                t_ast[:, so + 2:so + 3]), deps=[el3])
            el5 = P.step("vector", lambda e, so=so: e.tensor_scalar_mul(
                t_ast[:, so + 4:so + 5], t_ast[:, so:so + 1], 1.0 / D),
                deps=[el1])
            el6 = P.step("vector", lambda e, so=so: e.tensor_sub(
                t_ast[:, so + 5:so + 6], t_ast[:, so + 4:so + 5],
                t_ast[:, so + 3:so + 4]), deps=[el4, el5])
            el7 = P.step("scalar", lambda e, so=so: e.activation(
                t_ast[:, so + 6:so + 7], t_ast[:, so + 5:so + 6], AF.Ln,
                bias=t_eps[:, :]), deps=[el6])
            el8 = P.step("scalar", lambda e, so=so: e.activation(
                t_ast[:, so + 7:so + 8], t_ast[:, so + 6:so + 7], AF.Exp,
                scale=-0.5), deps=[el7])
            el9 = P.step("vector", lambda e, aps=aps, so=so: e.tensor_scalar(
                t_att[:, :], aps, t_ast[:, so + 2:so + 3],
                t_ast[:, so + 7:so + 8], ALU.subtract, ALU.mult),
                deps=[el8] + cfl)
            bank_done(b0, 2, [el9])
            exl = P.dma("sync", t_xo[:, par * D:(par + 1) * D],
                        d_x[cix * 128:(cix + 1) * 128, :],
                        deps=[xo_ev[par]] if xo_ev[par] else [])
            exm = P.step("vector", lambda e, par=par: e.tensor_add(
                t_xm[:, :], t_att[:, :], t_xo[:, par * D:(par + 1) * D]),
                deps=[el9, exl] + cfl)
            xo_ev[par] = exm
            ev_xmid_w[cix] = P.dma("sync", s_xmid[cix * 128:(cix + 1) * 128, :],
                                   t_xm[:, :], deps=[exm])
            em1 = P.step("scalar", lambda e, so=so: e.activation(
                t_asq[:, :], t_xm[:, :], AF.Square,
                accum_out=t_ast[:, so + 8:so + 9]), deps=[exm, el1])
            em2 = P.step("vector", lambda e, so=so: e.reduce_sum(
                t_ast[:, so + 9:so + 10], t_xm[:, :], AXX), deps=[exm])
            em3 = P.step("vector", lambda e, so=so: e.tensor_scalar_mul(
                t_ast[:, so + 10:so + 11], t_ast[:, so + 9:so + 10], 1.0 / D),
                deps=[em2])
            em4 = P.step("vector", lambda e, so=so: e.tensor_mul(
                t_ast[:, so + 11:so + 12], t_ast[:, so + 10:so + 11],
                t_ast[:, so + 10:so + 11]), deps=[em3])
            em5 = P.step("vector", lambda e, so=so: e.tensor_scalar_mul(
                t_ast[:, so + 12:so + 13], t_ast[:, so + 8:so + 9], 1.0 / D),
                deps=[em1])
            em6 = P.step("vector", lambda e, so=so: e.tensor_sub(
                t_ast[:, so + 13:so + 14], t_ast[:, so + 12:so + 13],
                t_ast[:, so + 11:so + 12]), deps=[em4, em5])
            em7 = P.step("scalar", lambda e, so=so: e.activation(
                t_ast[:, so + 14:so + 15], t_ast[:, so + 13:so + 14], AF.Ln,
                bias=t_eps[:, :]), deps=[em6])
            em8 = P.step("scalar", lambda e, so=so: e.activation(
                t_ast[:, so + 15:so + 16], t_ast[:, so + 14:so + 15], AF.Exp,
                scale=-0.5), deps=[em7])
            em9 = P.step("vector", lambda e, so=so: e.tensor_scalar(
                t_xn2b[:, :], t_xm[:, :], t_ast[:, so + 10:so + 11],
                t_ast[:, so + 15:so + 16], ALU.subtract, ALU.mult),
                deps=[em8] + cfl)
            ev_xn2_w[cix] = P.dma("sync", s_xn2[cix * 128:(cix + 1) * 128, :],
                                  t_xn2b[:, :], deps=[em9])
            c1f = P.end_fence()
        yfm_guard = [mme[-1]]

    if STAGE <= 6:
        return _cut(P.fence_all())
    b_fence = P.fence_all()
    b_ctx.close()
    gb_ctx.close()

    c_ctx = ExitStack()

    def csb(name, shape, dtype):
        return c_ctx.enter_context(nc.sbuf_tensor(name, shape, dtype))

    t_fwg = csb("t_fwg", [128, 8 * HID], BF16)
    t_fwu = csb("t_fwu", [128, 8 * HID], BF16)
    t_fwd = csb("t_fwd", [128, 24 * D], BF16)
    t_x2f = csb("t_x2f", [128, 8 * G1K], BF16)
    t_fth = csb("t_fth", [128, 256], F32)
    t_fsg = csb("t_fsg", [128, 256], BF16)
    t_hbf = csb("t_hbf", [128, 256], BF16)
    t_fo = csb("t_fo", [128, 2 * D], BF16)

    SFF = 1.0 / 127.0
    fw_deps = list(b_fence) + ffn_cast_ev
    fgw = [P.dma("sync", t_fwg[:, dc * HID:(dc + 1) * HID],
                 s_wgbf[:, dc * 128:(dc + 1) * 128], deps=fw_deps,
                 transpose=True) for dc in range(8)]
    fuw = [P.dma("sync", t_fwu[:, dc * HID:(dc + 1) * HID],
                 s_wubf[:, dc * 128:(dc + 1) * 128], deps=fw_deps,
                 transpose=True) for dc in range(8)]
    fdw = [P.dma("sync", t_fwd[:, hc * D:(hc + 1) * D],
                 s_wdbf[:, hc * 128:(hc + 1) * 128], deps=fw_deps,
                 transpose=True) for hc in range(24)]

    ev_fdown_w = [None] * NCH
    t_fl = csb("t_fl", [128, D], BF16)
    t_xml = csb("t_xml", [128, D], F32)
    t_osq = csb("t_osq", [128, D], F32)
    t_ost = csb("t_ost", [128, 16], F32)
    c2b_prev = {"eo9": None, "eoA": None, "ew": None}

    def emit_c2b(cix):
        par = cix % 2
        so = par * 8
        prev_eo9, prev_eoA, prev_ew = (c2b_prev["eo9"], c2b_prev["eoA"],
                                       c2b_prev["ew"])
        lf = P.dma("sync", t_fl[:, :],
                   s_fdown[cix * 128:(cix + 1) * 128, :],
                   deps=[ev_fdown_w[cix]] + ([prev_eo9] if prev_eo9 else []))
        lx = P.dma("sync", t_xml[:, :],
                   s_xmid[cix * 128:(cix + 1) * 128, :],
                   deps=[ev_xmid_w[cix]] + ([prev_eoA] if prev_eoA else []))
        fa = t_fl[:, :]
        eo1 = P.step("scalar", lambda e, fa=fa, so=so: e.activation(
            t_osq[:, :], fa, AF.Square, accum_out=t_ost[:, so:so + 1]),
            deps=[lf] + ([prev_ew] if prev_ew else []))
        eo2 = P.step("vector", lambda e, fa=fa, so=so: e.reduce_sum(
            t_ost[:, so + 1:so + 2], fa, AXX), deps=[lf])
        eo3 = P.step("vector", lambda e, so=so: e.tensor_scalar_mul(
            t_ost[:, so + 2:so + 3], t_ost[:, so + 1:so + 2], 1.0 / D),
            deps=[eo2])
        eo4 = P.step("vector", lambda e, so=so: e.tensor_mul(
            t_ost[:, so + 3:so + 4], t_ost[:, so + 2:so + 3],
            t_ost[:, so + 2:so + 3]), deps=[eo3])
        eo5 = P.step("vector", lambda e, so=so: e.tensor_scalar_mul(
            t_ost[:, so + 4:so + 5], t_ost[:, so:so + 1], 1.0 / D), deps=[eo1])
        eo6 = P.step("vector", lambda e, so=so: e.tensor_sub(
            t_ost[:, so + 5:so + 6], t_ost[:, so + 4:so + 5],
            t_ost[:, so + 3:so + 4]), deps=[eo4, eo5])
        eo7 = P.step("scalar", lambda e, so=so: e.activation(
            t_ost[:, so + 6:so + 7], t_ost[:, so + 5:so + 6], AF.Ln,
            bias=t_eps[:, :]), deps=[eo6])
        eo8 = P.step("scalar", lambda e, so=so: e.activation(
            t_ost[:, so + 7:so + 8], t_ost[:, so + 6:so + 7], AF.Exp,
            scale=-0.5), deps=[eo7])
        eo9 = P.step("vector", lambda e, fa=fa, so=so: e.tensor_scalar(
            t_osq[:, :], fa, t_ost[:, so + 2:so + 3], t_ost[:, so + 7:so + 8],
            ALU.subtract, ALU.mult), deps=[eo8])
        eoA = P.step("vector", lambda e: e.tensor_add(
            t_osq[:, :], t_osq[:, :], t_xml[:, :]),
            deps=[eo9, lx] + ([prev_ew] if prev_ew else []))
        ew = P.dma("sync", d_out[cix * 128:(cix + 1) * 128, :],
                   t_osq[:, :], deps=[eoA])
        c2b_prev["eo9"] = eo9
        c2b_prev["eoA"] = eoA
        c2b_prev["ew"] = ew

    x2f_guard = []
    fo_guard = []
    for gi in range(TOK // G1K):
        xdeps = [ev_xn2_w[gi * (G1K // 128) + i] for i in range(G1K // 128)]
        x2evs = [P.dma("sync", t_x2f[:, dc * G1K:(dc + 1) * G1K],
                       s_xn2[gi * G1K:(gi + 1) * G1K, dc * 128:(dc + 1) * 128],
                       deps=xdeps + x2f_guard, transpose=True)
                 for dc in range(8)]
        for bi in range(G1K // 256):
            tof = bi * 256
            bdn, bddn = bank(4, hold=True)
            dn_m = []
            h_prev = None
            prev_dn = None
            for hc in range(24):
                bg_, bdg_ = bank(1)
                g_m = [P.step("tensor",
                              lambda e, kc=kc, hc=hc, bg_=bg_, tof=tof: e.matmul(
                    psf(bg_, 0, 128, 0, 256),
                    t_fwg[:, kc * HID + hc * 128:kc * HID + (hc + 1) * 128],
                    t_x2f[:, kc * G1K + tof:kc * G1K + tof + 256],
                    start=(kc == 0), stop=(kc == 7)),
                    deps=[x2evs[kc], fgw[kc]] + (bdg_ if kc == 0 else []))
                    for kc in range(8)]
                bu_, bdu_ = bank(1)
                u_m = [P.step("tensor",
                              lambda e, kc=kc, hc=hc, bu_=bu_, tof=tof: e.matmul(
                    psf(bu_, 0, 128, 0, 256),
                    t_fwu[:, kc * HID + hc * 128:kc * HID + (hc + 1) * 128],
                    t_x2f[:, kc * G1K + tof:kc * G1K + tof + 256],
                    start=(kc == 0), stop=(kc == 7)),
                    deps=[x2evs[kc], fuw[kc]] + (bdu_ if kc == 0 else []))
                    for kc in range(8)]
                et = P.step("scalar", lambda e, bg_=bg_: e.activation(
                    t_fth[:, :], psf(bg_, 0, 128, 0, 256), AF.Tanh,
                    scale=SFF / 2),
                    deps=g_m + ([h_prev] if h_prev else []))
                es = P.step("vector", lambda e: e.tensor_scalar(
                    t_fsg[:, :], t_fth[:, :], 0.5, 0.5, ALU.mult, ALU.add),
                    deps=[et])
                es2 = P.step("vector", lambda e, bg_=bg_: e.scalar_tensor_tensor(
                    t_fsg[:, :], psf(bg_, 0, 128, 0, 256), SFF, t_fsg[:, :],
                    ALU.mult, ALU.mult), deps=[es])
                bank_done(bg_, 1, [es2])
                eh = P.step("vector", lambda e, bu_=bu_: e.scalar_tensor_tensor(
                    t_hbf[:, :], psf(bu_, 0, 128, 0, 256), SFF, t_fsg[:, :],
                    ALU.mult, ALU.mult),
                    deps=[es2] + u_m + ([prev_dn] if prev_dn else []))
                bank_done(bu_, 1, [eh])
                h_prev = eh
                for ts_ in range(2):
                    for nh in range(2):
                        dn_m.append(P.step(
                            "tensor",
                            lambda e, ts_=ts_, nh=nh, hc=hc, bdn=bdn: e.matmul(
                                psf(bdn + 2 * ts_ + nh, 0, 128, 0, 512),
                                t_hbf[:, ts_ * 128:(ts_ + 1) * 128],
                                t_fwd[:, hc * D + nh * 512:hc * D + (nh + 1) * 512],
                                start=(hc == 0), stop=(hc == 23)),
                            deps=[eh, fdw[hc]] + (bddn if hc == 0 and
                                                  ts_ == 0 and nh == 0 else [])))
                prev_dn = dn_m[-1]
            efo = [P.step("scalar", lambda e, ts_=ts_, bdn=bdn: e.mul(
                t_fo[:, ts_ * D:(ts_ + 1) * D],
                ps_t[0:128, 512 * (bdn + 2 * ts_):512 * (bdn + 2 * ts_) + 1024],
                SFF), deps=dn_m + fo_guard) for ts_ in range(2)]
            bank_done(bdn, 4, efo)
            cix0 = (gi * G1K + tof) // 128
            ev = P.dma("sync",
                       s_fdown[cix0 * 128:(cix0 + 2) * 128, :].rearrange(
                           "(ts p) d -> p ts d", ts=2),
                       t_fo[:, :].rearrange("p (ts d) -> p ts d", ts=2),
                       deps=efo)
            ev_fdown_w[cix0] = ev
            ev_fdown_w[cix0 + 1] = ev
            fo_guard = [ev]
        x2f_guard = [dn_m[-1], g_m[-1], u_m[-1]]
        for cix in range(gi * (G1K // 128), (gi + 1) * (G1K // 128)):
            emit_c2b(cix)

    if STAGE <= 7:
        return _cut(P.fence_all())
    c_ctx.close()
    P.finalize(block, sems)
    ctx.close()
    return nc


def kernel(L0_attn_x, L0_attn_Wq, L0_attn_Wk, L0_attn_Wv, L0_attn_Wa,
           L0_attn_ba, L0_attn_Wb, L0_attn_bb, L0_attn_Wg, L0_attn_Wo,
           L0_attn_g_norm, L0_ffn_w_gate, L0_ffn_w_up, L0_ffn_w_down):
    B, S, _ = L0_attn_x.shape
    TOK = B * S // NCORE
    nc = build_program(TOK)
    CON = build_consts()

    wab = np.zeros((D, 64), np.float32)
    wab[:, 0:16] = np.asarray(L0_attn_Wa, np.float32)
    wab[:, 32:48] = np.asarray(L0_attn_Wb, np.float32)
    bab = np.zeros((64, 1), np.float32)
    bab[0:16, 0] = np.asarray(L0_attn_ba, np.float32)
    bab[32:48, 0] = np.asarray(L0_attn_bb, np.float32)

    x = np.ascontiguousarray(
        np.asarray(L0_attn_x, np.float32).reshape(B, 4, S // 4, D))
    base = {
        "wq": np.ascontiguousarray(L0_attn_Wq, np.float32),
        "wk": np.ascontiguousarray(L0_attn_Wk, np.float32),
        "wv": np.ascontiguousarray(L0_attn_Wv, np.float32),
        "wgt": np.ascontiguousarray(L0_attn_Wg, np.float32),
        "wo": np.ascontiguousarray(L0_attn_Wo, np.float32),
        "wab": wab, "bab": bab,
        "gnorm": np.ascontiguousarray(
            np.asarray(L0_attn_g_norm, np.float32).reshape(1, D)),
        "ffn_g": np.ascontiguousarray(L0_ffn_w_gate, np.int32),
        "ffn_u": np.ascontiguousarray(L0_ffn_w_up, np.int32),
        "ffn_d": np.ascontiguousarray(L0_ffn_w_down, np.int32),
    }
    for k, v in CON.items():
        base["c_" + k] = v
    in_maps = []
    for core in range(NCORE):
        m = dict(base)
        m["x"] = np.ascontiguousarray(x[core // 4, core % 4])
        selv = np.zeros((1, 4), np.float32)
        if core % 4 > 0:
            selv[0, core % 4 - 1] = 1.0
        m["selv"] = selv
        in_maps.append(m)

    res = run_bass_kernel_spmd(nc, in_maps, core_ids=list(range(NCORE)))
    out = np.zeros((B, S, D), np.float32)
    for core in range(NCORE):
        out[core // 4, (core % 4) * TOK:(core % 4 + 1) * TOK] = \
            np.asarray(res.results[core]["out"], np.float32)
    return out


# version 11
# speedup vs baseline: 1.1369x; 1.1131x over previous
import numpy as np
import ml_dtypes
import concourse.bass as bass
import concourse.mybir as mybir
from concourse.bass_utils import run_bass_kernel_spmd
from contextlib import ExitStack

F32 = mybir.dt.float32
BF16 = mybir.dt.bfloat16
I32 = mybir.dt.int32
AF = mybir.ActivationFunctionType
ALU = mybir.AluOpType
AXX = mybir.AxisListType.X

D = 1024
HID = 3072
C = 128
NCORE = 8

ENG2SEM = {"tensor": "st", "vector": "sv", "scalar": "ss", "gpsimd": "sp"}
NLANE = 8
LANES = tuple(f"ld{i}" for i in range(NLANE))
NGLANE = 2
GLANES = tuple(f"gd{i}" for i in range(NGLANE))
SEMKEYS = ("st", "sv", "ss", "sp", "scc") + LANES + GLANES


class Prog:

    def __init__(self):
        self.sched = []
        self.cnt = {k: 0 for k in SEMKEYS}
        self.collect = None
        self.lane_rr = 0
        self.glane_rr = 0

    def step(self, engine, fn, deps=(), inc=1, semkey=None):
        key = semkey or ENG2SEM[engine]
        deps = tuple(d for d in deps if d is not None)
        self.sched.append((engine, fn, deps, inc, key))
        self.cnt[key] += inc
        ev = (key, self.cnt[key])
        if self.collect is not None:
            self.collect[key] = ev
        return ev

    def dma(self, engine, out, in_, deps=(), **kw):
        if engine == "gpsimd":
            lane = GLANES[self.glane_rr]
            self.glane_rr = (self.glane_rr + 1) % NGLANE
        else:
            lane = LANES[self.lane_rr]
            self.lane_rr = (self.lane_rr + 1) % NLANE
        deps = tuple(d for d in deps if d is not None)
        if self.cnt[lane] > 0:
            deps = deps + ((lane, self.cnt[lane]),)
        return self.step(
            engine,
            lambda e, out=out, in_=in_, kw=kw: e.dma_start(out=out, in_=in_, **kw),
            deps=deps, inc=16, semkey=lane)

    def begin_fence(self):
        self.collect = {}

    def end_fence(self):
        f = tuple(self.collect.values())
        self.collect = None
        return f

    def fence_all(self):
        return tuple((k, v) for k, v in self.cnt.items() if v > 0)

    def finalize(self, block, sems):
        water = {}

        def emit(this_engine, eng):
            for engine, fn, deps, inc, key in self.sched:
                if engine != this_engine:
                    continue
                for (k, v) in deps:
                    if water.get((this_engine, k), 0) < v:
                        eng.wait_ge(sems[k], v)
                        water[(this_engine, k)] = v
                inst = fn(eng)
                if inst is not None:
                    inst.then_inc(sems[key], inc)
                else:
                    eng.nop().then_inc(sems[key], inc)

        @block.sync
        def _(eng):
            emit("sync", eng)
            for k in LANES + GLANES + ("scc",):
                if self.cnt[k] > 0:
                    eng.wait_ge(sems[k], self.cnt[k])

        @block.tensor
        def _(eng):
            emit("tensor", eng)

        @block.vector
        def _(eng):
            emit("vector", eng)

        @block.scalar
        def _(eng):
            emit("scalar", eng)

        @block.gpsimd
        def _(eng):
            emit("gpsimd", eng)


def build_consts():
    con = {}
    con["ones1"] = np.ones((1, 128), np.float32)
    con["ones1_bf"] = np.ones((1, 128), ml_dtypes.bfloat16)
    con["eye_bf"] = np.eye(128, dtype=ml_dtypes.bfloat16)
    con["eye_f32"] = np.eye(128, dtype=np.float32)
    con["i64_f32"] = np.eye(64, dtype=np.float32)
    con["i64rep_bf"] = np.tile(np.eye(64), (1, 16)).astype(ml_dtypes.bfloat16)
    idx = np.arange(128)
    m = np.where(idx[:, None] > idx[None, :], -1.0, 0.0).astype(np.float32)
    con["neglmask_bf"] = m
    con["umask_bf"] = (idx[:, None] <= idx[None, :]).astype(
        np.float32)
    ho = np.zeros((128, 2), np.float32)
    ho[:64, 0] = 1.0
    ho[64:, 1] = 1.0
    con["headones_bf"] = ho.astype(ml_dtypes.bfloat16)
    s2 = np.zeros((2, 128), np.float32)
    s2[0, :64] = 1.0
    s2[1, 64:] = 1.0
    con["sel2t_bf"] = s2.astype(ml_dtypes.bfloat16)
    selh = np.zeros((16, 16 * 64), np.float32)
    for h in range(16):
        selh[h, h * 64:(h + 1) * 64] = 1.0
    con["selh_f32"] = selh
    con["selh_bf"] = selh.astype(ml_dtypes.bfloat16)
    selj = np.zeros((16, 8 * 128), np.float32)
    for j in range(8):
        selj[2 * j, j * 128:j * 128 + 64] = 1.0
        selj[2 * j + 1, j * 128 + 64:(j + 1) * 128] = 1.0
    con["selj_bf"] = selj.astype(ml_dtypes.bfloat16)
    return con


STAGE = 99
SUB = 999
DEBUG_SCRATCH = False


def build_program(TOK):
    NCH = TOK // C
    PT_ = 256
    NPT = TOK // PT_
    CPT = PT_ // C
    G1K = min(1024, TOK)
    NL = 7

    nc = bass.Bass("TRN2", target_bir_lowering=False, debug=False)
    P = Prog()
    CON = build_consts()

    def din(name, shape, dtype):
        return nc.dram_tensor(name, list(shape), dtype, kind="ExternalInput").ap()

    d_x = din("x", (TOK, D), F32)
    d_w = {n: din(n, (D, D), F32) for n in ("wq", "wk", "wv", "wgt", "wo")}
    d_wab = din("wab", (D, 64), F32)
    d_bab = din("bab", (64, 1), F32)
    d_gnorm = din("gnorm", (1, D), F32)
    d_fg = din("ffn_g", (HID, D), I32)
    d_fu = din("ffn_u", (HID, D), I32)
    d_fd = din("ffn_d", (D, HID), I32)
    d_selv = din("selv", (1, 4), F32)
    d_con = {k: din("c_" + k, v.shape,
                    F32 if v.dtype == np.float32 else BF16)
             for k, v in CON.items()}
    d_out = nc.dram_tensor("out", [TOK, D], F32, kind="ExternalOutput").ap()

    def scr(name, shape, dtype):
        kind = "ExternalOutput" if DEBUG_SCRATCH else "Internal"
        if name in ("s_agi", "s_ago"):
            kind = "Internal"
        return nc.dram_tensor(name, list(shape), dtype, kind=kind).ap()

    s_xn = scr("s_xn", (TOK, D), BF16)
    s_kfm = scr("s_kfm", (D, TOK), BF16)
    s_gate = scr("s_gate", (TOK, D), BF16)
    s_ab = scr("s_ab", (TOK // 256, 64, 256), F32)
    s_z = scr("s_z", (NCH, 128, 2048), BF16)
    s_ft = scr("s_ft", (NCH, 64, 2048), BF16)
    s_kd = scr("s_kd", (NCH, 128, 1024), BF16)
    s_pt = scr("s_pt", (NCH, 128, 2048), BF16)
    s_qt = scr("s_qt", (NCH, 8, 128, 128), BF16)
    s_cath = scr("s_cath", (NCH, 2048), BF16)
    s_catl = scr("s_catl", (NCH, 2048), BF16)
    s_agi = scr("s_agi", (2048, 64), F32)
    s_ago = scr("s_ago", (4 * 2048, 64), F32)
    s_y = scr("s_y", (TOK, D), BF16)
    s_xmid = scr("s_xmid", (TOK, D), F32)
    s_xn2 = scr("s_xn2", (TOK, D), BF16)
    s_wgbf = scr("s_wgbf", (HID, D), BF16)
    s_wubf = scr("s_wubf", (HID, D), BF16)
    s_wdbf = scr("s_wdbf", (D, HID), BF16)
    s_fdown = scr("s_fdown", (TOK, D), BF16)
    s_xdbg = scr("s_xdbg", (NCH, 128, 2048), BF16)
    s_emdbg = scr("s_emdbg", (NCH, 4, 128, 512), F32)
    s_tmdbg = scr("s_tmdbg", (NCH, 128, 80), F32)
    s_kkdbg = scr("s_kkdbg", (NCH, 4, 128, 512), F32)
    s_kkdbg2 = scr("s_kkdbg2", (NCH, 4, 128, 512), F32)
    s_ddbg = scr("s_ddbg", (NCH, 4, 128, 512), F32)

    ctx = ExitStack()
    ps_t = ctx.enter_context(nc.psum_tensor("ps", [128, 3584], F32))
    psb_t = ctx.enter_context(nc.psum_tensor("psb", [128, 1024], BF16))

    def psf(b, p0, p1, c0, n):
        return ps_t[p0:p1, 512 * b + c0:512 * b + c0 + n]

    bank_ev = [()] * 8
    bank_hold = [False] * 8
    rr = [0]
    psb_free = [()]

    def bank(n=1, hold=False):
        start = rr[0]
        for off in range(8):
            b0 = (start + off) % 7
            if b0 + n > 7:
                continue
            if any(bank_hold[b] for b in range(b0, b0 + n)):
                continue
            rr[0] = (b0 + n) % 7
            deps = sum((bank_ev[b] for b in range(b0, b0 + n)), ())
            if hold:
                for b in range(b0, b0 + n):
                    bank_hold[b] = True
            return b0, list(deps)
        raise RuntimeError("no banks")

    def bank_done(b0, n, evs):
        for b in range(b0, b0 + n):
            bank_ev[b] = tuple(evs)
            bank_hold[b] = False

    sems = {}
    for name in SEMKEYS:
        sems[name] = ctx.enter_context(nc.semaphore(name))
    block = ctx.enter_context(nc.Block())

    sb = nc.alloc_sbuf_tensor
    t_con = {k: sb("t_" + k, list(v.shape),
                   F32 if v.dtype == np.float32 else BF16)
             for k, v in CON.items()}
    t_gnb = sb("t_gnb", [128, D], F32)
    t_gcols = sb("t_gcols", [64, 16 * NCH], F32)
    t_accF = sb("t_accF", [64, 1024], F32)
    t_uccF = sb("t_uccF", [64, 1024], F32)
    t_bab = sb("t_bab", [64, 1], F32)

    t_eps = sb("t_eps", [128, 1], F32)
    ev_con = [P.dma("sync", t_con[k][:, :], d_con[k]) for k in CON]
    ev_eps = P.step("vector", lambda e: e.memset(t_eps[:, :], 1e-5))
    ev_con.append(ev_eps)
    evc0 = ev_con[-1]
    evb = P.dma("sync", t_bab[:, :], d_bab)
    ev_bab = P.step("scalar", lambda e: e.mul(t_bab[:, :], t_bab[:, :], 0.5),
                    deps=[evb])

    a_ctx = ExitStack()

    def asb(name, shape, dtype):
        return a_ctx.enter_context(nc.sbuf_tensor(name, shape, dtype))

    t_wt = {n: asb("t_" + n, [128, 8 * D], BF16)
            for n in ("wq", "wk", "wv")}
    t_xnfm = asb("t_xnfm", [128, 8 * PT_], BF16)
    t_qn = asb("t_qn", [128, 2 * 8 * PT_], BF16)
    t_kn = asb("t_kn", [128, 2 * 8 * PT_], BF16)
    t_vb = asb("t_vb", [128, 2 * CPT * D], BF16)
    t_ktm = asb("t_ktm", [128, 2 * D], BF16)
    t_sqb = asb("t_sqb", [128, 8 * PT_], BF16)
    t_qbf = asb("t_qbf", [128, 8 * PT_], BF16)
    t_rn = asb("t_rn", [2, 1024], F32)
    t_rn2 = asb("t_rn2", [2, 1024], F32)
    t_rnb = asb("t_rnb", [2, 4 * PT_], BF16)
    sub_ctx = ExitStack()

    def ssb(name, shape, dtype):
        return sub_ctx.enter_context(nc.sbuf_tensor(name, shape, dtype))

    t_wt["wgt"] = ssb("t_wgt", [128, 8 * D], BF16)
    t_wab = ssb("t_wab", [128, 8 * 64], BF16)
    t_wld = ssb("t_wld", [128, 2 * D], F32)
    t_ffw = ssb("t_ffw", [128, 2 * D], I32)
    t_ffb = ssb("t_ffb", [128, 2 * D], BF16)
    t_gn1 = ssb("t_gn1", [1, D], F32)

    ev_g1 = P.dma("sync", t_gn1[:, :], d_gnorm)
    b0, bd = bank(2)
    ev_gm = [P.step("tensor", lambda e, i=i, b0=b0: e.matmul(
        psf(b0 + i, 0, 128, 0, 512), t_con["ones1"][:, :],
        t_gn1[:, 512 * i:512 * (i + 1)], start=True, stop=True),
        deps=[ev_g1, evc0] + bd) for i in range(2)]
    ev_gnb = P.step("vector", lambda e, b0=b0: e.tensor_copy(
        t_gnb[:, :], ps_t[0:128, 512 * b0:512 * b0 + 1024]), deps=ev_gm)
    bank_done(b0, 2, [ev_gnb])

    def _cut(dep_evs):
        for cix in range(NCH):
            P.dma("sync", d_out[cix * 128:(cix + 1) * 128, :], t_gnb[:, :],
                  deps=list(dep_evs) + [ev_gnb])
        P.finalize(block, sems)
        ctx.close()
        return nc

    wev = {}
    wl_ev = [None, None]
    for wkey, cols in [("wq", D), ("wk", D), ("wv", D), ("wgt", D), ("wab", 64)]:
        dst = t_wab if wkey == "wab" else t_wt[wkey]
        src = d_wab if wkey == "wab" else d_w[wkey]
        evs = []
        for kc in range(8):
            par = kc % 2
            ev1 = P.dma("sync", t_wld[:, par * D:par * D + cols],
                        src[kc * 128:(kc + 1) * 128, :],
                        deps=[wl_ev[par]] if wl_ev[par] else [])
            evc = P.step("gpsimd", lambda e, kc=kc, par=par, dst=dst,
                         cols=cols:
                         e.tensor_copy(dst[:, kc * cols:(kc + 1) * cols],
                                       t_wld[:, par * D:par * D + cols]),
                         deps=[ev1])
            wl_ev[par] = evc
            evs.append(evc)
        wev[wkey] = evs

    ffn_cast_ev = []
    fc_ev = [None, None]
    fc_i = 0
    for (dsrc, ddst, rows, cols) in [(d_fg, s_wgbf, HID, D), (d_fu, s_wubf, HID, D),
                                     (d_fd, s_wdbf, D, HID)]:
        for r in range(rows // 128):
            for cc0 in range(cols // D):
                par = fc_i % 2
                fc_i += 1
                ev1 = P.dma("gpsimd", t_ffw[:, par * D:(par + 1) * D],
                            dsrc[r * 128:(r + 1) * 128, cc0 * D:(cc0 + 1) * D],
                            deps=[fc_ev[par]] if fc_ev[par] else [])
                ev2 = P.step("gpsimd", lambda e, par=par: e.tensor_copy(
                    t_ffb[:, par * D:(par + 1) * D],
                    t_ffw[:, par * D:(par + 1) * D]), deps=[ev1])
                ev3 = P.dma("gpsimd", ddst[r * 128:(r + 1) * 128,
                                           cc0 * D:(cc0 + 1) * D],
                            t_ffb[:, par * D:(par + 1) * D], deps=[ev2])
                fc_ev[par] = ev3
                ffn_cast_ev.append(ev3)

    t_x0 = ssb("t_x0", [128, 4 * D], F32)
    t_sq0 = ssb("t_sq0", [128, D], F32)
    t_st0 = ssb("t_st0", [128, 32], F32)
    t_xnb = ssb("t_xnb", [128, 4 * D], BF16)
    ev_xn_w = []
    prev_norms = []
    prev_stores = [None] * 4
    prev_e3s = None
    prev_e5 = None
    assert NCH % 4 == 0
    for gx in range(NCH // 4):
        loads = [P.dma("sync", t_x0[:, j * D:(j + 1) * D],
                       d_x[(gx * 4 + j) * 128:(gx * 4 + j + 1) * 128, :],
                       deps=prev_norms) for j in range(4)]
        sq = []
        for j in range(4):
            sq.append(P.step("scalar", lambda e, j=j: e.activation(
                t_sq0[:, :], t_x0[:, j * D:(j + 1) * D], AF.Square,
                accum_out=t_st0[:, j:j + 1]),
                deps=[loads[j]] + ([prev_e3s] if prev_e3s and j == 0 else []) +
                     ([sq[j - 1]] if j > 0 else [])))
        rd = [P.step("vector", lambda e, j=j: e.reduce_sum(
            t_st0[:, 4 + j:5 + j], t_x0[:, j * D:(j + 1) * D], AXX),
            deps=[loads[j]]) for j in range(4)]
        e1 = P.step("vector", lambda e: e.tensor_scalar_mul(
            t_st0[:, 8:12], t_st0[:, 4:8], 1.0 / D), deps=rd)
        e2 = P.step("vector", lambda e: e.tensor_mul(
            t_st0[:, 12:16], t_st0[:, 8:12], t_st0[:, 8:12]), deps=[e1])
        e3s = P.step("vector", lambda e: e.tensor_scalar_mul(
            t_st0[:, 16:20], t_st0[:, 0:4], 1.0 / D), deps=sq)
        e4 = P.step("vector", lambda e: e.tensor_sub(
            t_st0[:, 20:24], t_st0[:, 16:20], t_st0[:, 12:16]),
            deps=[e2, e3s] + ([prev_e5] if prev_e5 else []))
        e5 = P.step("scalar", lambda e: e.activation(
            t_st0[:, 24:28], t_st0[:, 20:24], AF.Ln, bias=t_eps[:, :]),
            deps=[e4])
        e6 = P.step("scalar", lambda e: e.activation(
            t_st0[:, 28:32], t_st0[:, 24:28], AF.Exp, scale=-0.5),
            deps=[e5] + prev_norms)
        norms = []
        for j in range(4):
            evn = P.step("vector", lambda e, j=j: e.tensor_scalar(
                t_xnb[:, j * D:(j + 1) * D], t_x0[:, j * D:(j + 1) * D],
                t_st0[:, 8 + j:9 + j], t_st0[:, 28 + j:29 + j],
                ALU.subtract, ALU.mult),
                deps=[e6] + ([prev_stores[j]] if prev_stores[j] else []))
            norms.append(evn)
            prev_stores[j] = P.dma(
                "sync", s_xn[(gx * 4 + j) * 128:(gx * 4 + j + 1) * 128, :],
                t_xnb[:, j * D:(j + 1) * D], deps=[evn])
            ev_xn_w.append(prev_stores[j])
        prev_norms = norms
        prev_e3s = e3s
        prev_e5 = e5

    if STAGE <= 1:
        return _cut(P.fence_all())
    t_abz = ssb("t_abz", [64, PT_], F32)
    t_ab = ssb("t_ab", [64, PT_], F32)
    t_gth = ssb("t_gth", [128, D], F32)
    t_gsg = ssb("t_gsg", [128, D], F32)
    t_gout = ssb("t_gout", [128, D], BF16)

    xnfm_guard = []
    ev_ab_tile = [None] * NPT
    ev_gate_w = [None] * NCH
    prev_gate_g = None
    prev_abz = None
    for ti in range(NPT):
        deps = [ev_xn_w[ti * CPT + i] for i in range(CPT)] + xnfm_guard
        xn_evs = [P.dma("sync", t_xnfm[:, dc * PT_:(dc + 1) * PT_],
                        s_xn[ti * PT_:(ti + 1) * PT_, dc * 128:(dc + 1) * 128],
                        deps=deps, transpose=True) for dc in range(8)]
        xnfm_guard = []
        b0, bd = bank(1)
        mme = [P.step("tensor", lambda e, kc=kc, b0=b0: e.matmul(
            psf(b0, 0, 64, 0, PT_), t_wab[:, kc * 64:(kc + 1) * 64],
            t_xnfm[:, kc * PT_:(kc + 1) * PT_], start=(kc == 0), stop=(kc == 7)),
            deps=[xn_evs[kc], wev["wab"][kc]] + (bd if kc == 0 else []))
            for kc in range(8)]
        ev_th = P.step("scalar", lambda e, b0=b0: e.activation(
            t_abz[:, :], psf(b0, 0, 64, 0, PT_), AF.Tanh,
            bias=t_bab[:, :], scale=0.5),
            deps=mme + [ev_bab] + ([prev_abz] if prev_abz else []))
        bank_done(b0, 1, [ev_th])
        pga = [ev_ab_tile[ti - 1]] if ti >= 1 else []
        eva_v = P.step("vector", lambda e: e.tensor_scalar(
            t_ab[:, :], t_abz[:, :], 0.5, 0.5,
            ALU.mult, ALU.add), deps=[ev_th] + pga)
        prev_abz = eva_v
        ev_ab_tile[ti] = P.dma("sync", s_ab[ti, :, :], t_ab[:, :],
                               deps=[eva_v])
        xnfm_guard = [mme[-1]]
        for ci in range(CPT):
            cix = ti * CPT + ci
            b0, bd = bank(2)
            mme = []
            for kc in range(8):
                for nh in range(2):
                    mme.append(P.step(
                        "tensor",
                        lambda e, kc=kc, nh=nh, b0=b0, ci=ci: e.matmul(
                            psf(b0 + nh, 0, 128, 0, 512),
                            t_xnfm[:, kc * PT_ + ci * 128:kc * PT_ + (ci + 1) * 128],
                            t_wt["wgt"][:, kc * D + nh * 512:kc * D + (nh + 1) * 512],
                            start=(kc == 0), stop=(kc == 7)),
                        deps=[xn_evs[kc], wev["wgt"][kc]] + (bd if kc == 0 else [])))
            gps = ps_t[0:128, 512 * b0:512 * b0 + 1024]
            pg = [prev_gate_g] if prev_gate_g else []
            pg2 = [ev_gate_w[cix - 1]] if cix >= 1 else []
            ev_t = P.step("scalar", lambda e, gps=gps: e.activation(
                t_gth[:, :], gps, AF.Tanh, scale=0.5), deps=mme + pg)
            ev_sg = P.step("vector", lambda e: e.tensor_scalar(
                t_gsg[:, :], t_gth[:, :], 0.5, 0.5, ALU.mult, ALU.add),
                deps=[ev_t] + pg)
            ev_g = P.step("vector", lambda e, gps=gps: e.tensor_mul(
                t_gout[:, :], gps, t_gsg[:, :]), deps=[ev_sg] + pg2)
            prev_gate_g = ev_g
            bank_done(b0, 2, [ev_g])
            ev_gate_w[cix] = P.dma("sync", s_gate[cix * 128:(cix + 1) * 128, :],
                                   t_gout[:, :], deps=[ev_g])
            xnfm_guard = [mme[-1]]

    if STAGE <= 2:
        return _cut(P.fence_all())
    a1_fence = P.fence_all()
    sub_ctx.close()
    t_abl = asb("t_abl", [64, 128], F32)
    t_kkdbg = (asb("t_kkdbg", [128, 512], F32) if DEBUG_SCRATCH else None)
    t_la = asb("t_la", [16, 128], F32)
    t_la2 = asb("t_la2", [16, 128], F32)
    t_z16 = asb("t_z16", [16, 128], F32)
    t_c = asb("t_c", [16, 128], F32)
    t_chb = asb("t_chb", [16, 128], BF16)
    t_chf = asb("t_chf", [16, 128], F32)
    t_clb = asb("t_clb", [16, 128], BF16)
    t_cprev = asb("t_cprev", [16, 128], F32)
    t_dend = asb("t_dend", [16, 128], F32)
    t_ecp = asb("t_ecp", [16, 128], F32)
    t_ecbf = asb("t_ecbf", [16, 128], BF16)
    t_tm1 = asb("t_tm1", [128, 80], F32)
    t_becptm = asb("t_becptm", [128, 16], F32)
    t_cath = asb("t_cath", [1, 512], BF16)
    t_catl = asb("t_catl", [1, 512], BF16)
    t_dbuf = asb("t_dbuf", [128, 1024], BF16)
    t_ebuf = asb("t_ebuf", [128, 512], F32)
    t_e1m = asb("t_e1m", [128, 512], F32)
    t_e2m = asb("t_e2m", [128, 512], F32)
    t_xb = asb("t_xb", [128, 2048], BF16)
    t_xp = [asb(f"t_xp{i}", [128, 2048], BF16) for i in range(2)]
    t_xpt = [asb(f"t_xpt{i}", [128, 2048], BF16) for i in range(2)]
    t_xtb = t_xpt[1]
    t_zb = [asb(f"t_zb{i}", [128, 2048], BF16) for i in range(2)]
    t_ftb = asb("t_ftb", [64, 2048], BF16)
    t_kdb = asb("t_kdb", [128, 1024], BF16)
    t_rb = asb("t_rb", [128, 2048], BF16)
    t_ptb = asb("t_ptb", [128, 2048], BF16)
    t_qtb = asb("t_qtb", [128, 1024], BF16)
    t_gib = asb("t_gib", [64, 1024], F32)
    t_atb = asb("t_atb", [64, 1024], BF16)
    t_ub = asb("t_ub", [64, 1024], BF16)
    t_acc = [asb(f"t_acc{i}", [64, 1024], BF16) for i in range(2)]
    t_acct = [asb(f"t_acct{i}", [64, 1024], BF16) for i in range(2)]
    t_ucc = [asb(f"t_ucc{i}", [64, 1024], BF16) for i in range(2)]

    ev_acc0 = P.step("vector", lambda e: e.tensor_copy(
        t_acc[1][:, :], t_con["i64rep_bf"][:, :]),
        deps=[evc0] + list(a1_fence))
    ev_acct0 = P.step("vector", lambda e: e.tensor_copy(
        t_acct[1][:, :], t_con["i64rep_bf"][:, :]), deps=[evc0])
    ev_ucc0 = P.step("vector", lambda e: e.memset(t_ucc[1][:, :], 0.0))
    ev_z16 = P.step("vector", lambda e: e.memset(t_z16[:, :], 0.0))
    compose_prev = (ev_acc0, ev_acct0, ev_ucc0)

    prev_rn = None
    tile_fence = [()] * NPT
    chunk_fence = a1_fence
    ev_chunk_done = [None] * NCH
    ev_gamma = [None] * NCH
    ev_kfm_w = [None] * NPT
    ktm_guard = [()] * 2
    eacctF = euccF = None

    for ti in range(NPT):
        par = ti % 2
        qof = par * 8 * PT_
        deps = [ev_xn_w[ti * CPT + i] for i in range(CPT)] + xnfm_guard
        xn_evs = [P.dma("sync", t_xnfm[:, dc * PT_:(dc + 1) * PT_],
                        s_xn[ti * PT_:(ti + 1) * PT_, dc * 128:(dc + 1) * 128],
                        deps=deps, transpose=True) for dc in range(8)]
        guard_prev = list(tile_fence[ti - 2]) if ti >= 2 else []
        tile_readers = []

        qn_apply = kn_apply = None
        for tname in ("q", "k"):
            wtile = t_wt["wq"] if tname == "q" else t_wt["wk"]
            wevs = wev["wq"] if tname == "q" else wev["wk"]
            dstb = t_qn if tname == "q" else t_kn
            sq_evs = []
            qbf_evs = []
            for oc in range(8):
                b0, bd = bank(1)
                mme = [P.step("tensor",
                              lambda e, oc=oc, kc=kc, b0=b0, wtile=wtile: e.matmul(
                    psf(b0, 0, 128, 0, PT_),
                    wtile[:, kc * D + oc * 128:kc * D + (oc + 1) * 128],
                    t_xnfm[:, kc * PT_:(kc + 1) * PT_],
                    start=(kc == 0), stop=(kc == 7)),
                    deps=[xn_evs[kc], wevs[kc]] + (bd if kc == 0 else []))
                    for kc in range(8)]
                e1 = P.step("vector", lambda e, oc=oc, b0=b0: e.tensor_copy(
                    t_qbf[:, oc * PT_:(oc + 1) * PT_], psf(b0, 0, 128, 0, PT_)),
                    deps=mme)
                e2 = P.step("scalar", lambda e, oc=oc: e.activation(
                    t_sqb[:, oc * PT_:(oc + 1) * PT_],
                    t_qbf[:, oc * PT_:(oc + 1) * PT_],
                    AF.Square), deps=[e1])
                bank_done(b0, 1, [e1])
                sq_evs.append(e2)
                qbf_evs.append(e1)
            nss = 8 * PT_ // 512
            b0, bd = bank(nss, hold=True)
            ssm = [P.step("tensor", lambda e, i4=i4, b0=b0: e.matmul(
                psf(b0 + i4, 0, 2, 0, 512), t_con["headones_bf"][:, :],
                t_sqb[:, 512 * i4:512 * (i4 + 1)], start=True, stop=True),
                deps=sq_evs + (bd if i4 == 0 else []) + [evc0])
                for i4 in range(nss)]
            eh4s = []
            applies = []
            hlen = 4 * PT_
            for hf in range(2):
                ssps = ps_t[0:2, 512 * b0 + hf * hlen:512 * b0 + (hf + 1) * hlen]
                eh1 = P.step("vector", lambda e, ssps=ssps, hf=hf:
                             e.tensor_scalar_max(
                    t_rn[:, 0:hlen], ssps, 1e-24),
                    deps=ssm + (eh4s if hf else []) +
                         ([prev_rn] if prev_rn else []))
                eh2 = P.step("scalar", lambda e: e.activation(
                    t_rn2[:, 0:hlen], t_rn[:, 0:hlen], AF.Ln), deps=[eh1])
                eh3 = P.step("scalar", lambda e: e.activation(
                    t_rn[:, 0:hlen], t_rn2[:, 0:hlen], AF.Exp, scale=-0.5),
                    deps=[eh2, eh1])
                eh4 = P.step("vector", lambda e, hf=hf: e.tensor_copy(
                    t_rnb[:, 0:hlen], t_rn[:, 0:hlen]),
                    deps=[eh3] + (applies[-2:] if hf else []))
                eh4s.append(eh4)
                prev_rn = eh3
                for oc in range(hf * 4, hf * 4 + 4):
                    b2, bd2 = bank(1)
                    em = P.step("tensor", lambda e, oc=oc, b2=b2, hf=hf: e.matmul(
                        psf(b2, 0, 128, 0, PT_), t_con["sel2t_bf"][:, :],
                        t_rnb[:, (oc - hf * 4) * PT_:(oc - hf * 4 + 1) * PT_],
                        start=True, stop=True),
                        deps=[eh4] + bd2)
                    ea = P.step("vector",
                                lambda e, oc=oc, b2=b2, dstb=dstb, qof=qof:
                                e.tensor_mul(
                        dstb[:, qof + oc * PT_:qof + (oc + 1) * PT_],
                        t_qbf[:, oc * PT_:(oc + 1) * PT_],
                        psf(b2, 0, 128, 0, PT_)),
                        deps=[em, qbf_evs[oc]] + (guard_prev if oc == 0 else []))
                    bank_done(b2, 1, [ea])
                    applies.append(ea)
            bank_done(b0, nss, eh4s)
            if tname == "q":
                qn_apply = applies
            else:
                kn_apply = applies

        ev_kfm_w[ti] = P.dma(
            "sync",
            s_kfm.rearrange("(dc p) t -> p dc t", p=128)[
                :, :, ti * PT_:(ti + 1) * PT_],
            t_kn[:, qof:qof + 8 * PT_].rearrange("p (dc t) -> p dc t", dc=8),
            deps=kn_apply)

        v_evs = []
        for ci in range(CPT):
            b0, bd = bank(2)
            mme = []
            for kc in range(8):
                for nh in range(2):
                    mme.append(P.step(
                        "tensor",
                        lambda e, kc=kc, nh=nh, b0=b0, ci=ci: e.matmul(
                            psf(b0 + nh, 0, 128, 0, 512),
                            t_xnfm[:, kc * PT_ + ci * 128:kc * PT_ + (ci + 1) * 128],
                            t_wt["wv"][:, kc * D + nh * 512:kc * D + (nh + 1) * 512],
                            start=(kc == 0), stop=(kc == 7)),
                        deps=[xn_evs[kc], wev["wv"][kc]] + (bd if kc == 0 else [])))
            ev = P.step("vector", lambda e, b0=b0, par=par, ci=ci: e.tensor_copy(
                t_vb[:, (par * CPT + ci) * D:(par * CPT + ci + 1) * D],
                ps_t[0:128, 512 * b0:512 * b0 + 1024]),
                deps=mme + (guard_prev if ci == 0 else []))
            bank_done(b0, 2, [ev])
            v_evs.append(ev)
        xnfm_guard = [mme[-1]]

        for ci in range(CPT if SUB >= 2 else 0):
            cix = ti * CPT + ci
            ks = cix % 2
            P.begin_fence()
            cf = list(chunk_fence)
            ev_ktm = P.dma("sync", t_ktm[:, ks * D:(ks + 1) * D],
                           s_kfm[:, cix * 128:(cix + 1) * 128],
                           deps=[ev_kfm_w[ti]] + list(ktm_guard[ks]),
                           transpose=True)

            if SUB <= 2:
                chunk_fence = P.end_fence()
                continue
            labl = P.dma("sync", t_abl[:, :],
                         s_ab[ti, :, ci * 128:(ci + 1) * 128],
                         deps=[ev_ab_tile[ti]] + cf)
            e1 = P.step("vector", lambda e: e.tensor_scalar_max(
                t_la[:, :], t_abl[0:16, :], 1e-37), deps=[labl] + cf)
            e2 = P.step("scalar", lambda e: e.activation(
                t_la2[:, :], t_la[:, :], AF.Ln), deps=[e1] + cf)
            e3 = P.step("vector", lambda e: e.tensor_tensor_scan(
                t_c[:, :], t_la2[:, :], t_z16[:, :], 0.0, ALU.add, ALU.add),
                deps=[e2, ev_z16])
            e4 = P.step("vector", lambda e: e.tensor_sub(
                t_cprev[:, :], t_c[:, :], t_la2[:, :]), deps=[e3])
            e6 = P.step("scalar", lambda e: e.activation(
                t_dend[:, :], t_c[:, :], AF.Exp,
                bias=t_c[:, 127:128], scale=-1.0), deps=[e3])
            e7 = P.step("scalar", lambda e: e.activation(
                t_ecp[:, :], t_cprev[:, :], AF.Exp), deps=[e4])
            e9 = P.step("scalar", lambda e: e.activation(
                t_ecbf[:, :], t_c[:, :], AF.Exp), deps=[e3])

            b0, bd = bank(2)
            trs = []
            for gi_, (tin, pb) in enumerate([(t_c[:, :], 0),
                                             (t_abl[32:48, :], 32),
                                             (t_cprev[:, :], 0),
                                             (t_dend[:, :], 0),
                                             (t_ecp[:, :], 0)]):
                ident = t_con["eye_f32"][pb:pb + 16, pb:pb + 16]
                dep_t = {0: [e3], 1: [labl], 2: [e4], 3: [e6],
                         4: [e7]}[gi_]
                bno = 1 if pb == 32 else 0
                trs.append(P.step("tensor",
                                  lambda e, gi_=gi_, tin=tin, ident=ident,
                                  b0=b0, bno=bno:
                                  e.transpose(
                    psf(b0 + bno, 0, 128, gi_ * 16, 16), tin, ident),
                    deps=dep_t + (bd if gi_ == 0 else []) + [evc0] + cf))
            e11a = P.step("vector", lambda e, b0=b0: e.tensor_copy(
                t_tm1[:, 0:16], psf(b0, 0, 128, 0, 16)),
                deps=[trs[0]] + cf)
            e11b = P.step("vector", lambda e, b0=b0: e.tensor_copy(
                t_tm1[:, 16:32], psf(b0 + 1, 0, 128, 16, 16)),
                deps=[trs[1]] + cf)
            e11c = P.step("vector", lambda e, b0=b0: e.tensor_copy(
                t_tm1[:, 32:80], psf(b0, 0, 128, 32, 48)),
                deps=trs + cf)
            e11 = P.step("vector", lambda e: e.engine_nop(),
                         deps=[e11a, e11b, e11c])
            bank_done(b0, 2, [e11a, e11b, e11c])
            e8 = P.step("vector", lambda e: e.tensor_mul(
                t_becptm[:, :], t_tm1[:, 16:32], t_tm1[:, 64:80]),
                deps=[e11] + cf)

            if DEBUG_SCRATCH:
                P.dma("sync", s_tmdbg[cix, :, :], t_tm1[:, :],
                      deps=[e11a, e11b, e11c, e8])
            ech = P.step("vector", lambda e: e.tensor_copy(
                t_chb[:, :], t_c[:, :]), deps=[e3] + cf)
            echf = P.step("vector", lambda e: e.tensor_copy(
                t_chf[:, :], t_chb[:, :]), deps=[ech])
            ecl = P.step("vector", lambda e: e.tensor_sub(
                t_clb[:, :], t_c[:, :], t_chf[:, :]), deps=[echf])
            e14a = P.dma("sync", s_cath[cix, :], t_chb[:, :], deps=[ech])
            e14b = P.dma("sync", s_catl[cix, :], t_clb[:, :], deps=[ecl])

            b0, bd = bank(1)
            gme = []
            for h in range(16):
                gme.append(P.step("tensor", lambda e, h=h, b0=b0: e.matmul(
                    ps_t[0:64, 512 * b0 + h:512 * b0 + h + 1],
                    t_con["selh_bf"][:, h * 64:(h + 1) * 64],
                    t_chb[:, 127:128], start=True, stop=False),
                    deps=[ech, evc0] + (bd if h == 0 else []) + cf))
                gme.append(P.step("tensor", lambda e, h=h, b0=b0: e.matmul(
                    ps_t[0:64, 512 * b0 + h:512 * b0 + h + 1],
                    t_con["selh_bf"][:, h * 64:(h + 1) * 64],
                    t_clb[:, 127:128], start=False, stop=True),
                    deps=[ecl]))
            ev_gamma[cix] = P.step("scalar", lambda e, b0=b0, cix=cix:
                                   e.activation(
                t_gcols[:, cix * 16:(cix + 1) * 16],
                ps_t[0:64, 512 * b0:512 * b0 + 16], AF.Exp), deps=gme)
            bank_done(b0, 1, [ev_gamma[cix]])

            if SUB <= 3:
                chunk_fence = P.end_fence()
                continue
            kd3 = t_kdb[:, :].rearrange("p (h e) -> p h e", h=16)
            ktm3 = t_ktm[:, ks * D:(ks + 1) * D].rearrange(
                "p (h e) -> p h e", h=16)
            dend_b = t_tm1[:, 48:64].unsqueeze(2).broadcast_to([128, 16, 64])
            e16 = P.step("vector", lambda e, kd3=kd3, ktm3=ktm3, dend_b=dend_b:
                         e.tensor_tensor(kd3, ktm3, dend_b, ALU.mult),
                         deps=[e11, ev_ktm] + cf)
            r4 = t_rb[:, :].rearrange("p (h t e) -> p h t e", h=16, t=2)
            v3 = t_vb[:, (par * CPT + ci) * D:(par * CPT + ci + 1) * D].rearrange(
                "p (h e) -> p h e", h=16)
            b_b = t_tm1[:, 16:32].unsqueeze(2).broadcast_to([128, 16, 64])
            becp_b = t_becptm[:, :].unsqueeze(2).broadcast_to([128, 16, 64])
            e17 = P.step("vector", lambda e, r4=r4, v3=v3, b_b=b_b:
                         e.tensor_tensor(r4[:, :, 0, :], v3, b_b, ALU.mult),
                         deps=[e11, v_evs[ci]] + cf)
            e18 = P.step("vector", lambda e, r4=r4, ktm3=ktm3, becp_b=becp_b:
                         e.tensor_tensor(r4[:, :, 1, :], ktm3, becp_b, ALU.mult),
                         deps=[e8, ev_ktm] + cf)
            tile_readers += [e16, e17, e18]

            x_evs = [None] * 16
            pt_evs = []
            prev_ecb = None
            prev_ee3 = None
            prev_ee4 = None
            prev_xg_gp = []
            prev_ptg_gp = []
            for g in range(4):
                e15a = P.dma("sync", t_cath[:, :],
                             s_cath[cix:cix + 1, g * 512:(g + 1) * 512],
                             deps=[e14a] + cf + ([prev_ecb] if prev_ecb else []))
                e15b = P.dma("sync", t_catl[:, :],
                             s_catl[cix:cix + 1, g * 512:(g + 1) * 512],
                             deps=[e14b] + cf + ([prev_ecb] if prev_ecb else []))
                bcb, bdc = bank(1)
                ecb1 = P.step("tensor", lambda e, g=g, bcb=bcb: e.matmul(
                    psf(bcb, 0, 128, 0, 512), t_con["ones1_bf"][:, :],
                    t_cath[:, :], start=True, stop=False),
                    deps=[e15a, evc0] + bdc + cf)
                ecb = P.step("tensor", lambda e, g=g, bcb=bcb: e.matmul(
                    psf(bcb, 0, 128, 0, 512), t_con["ones1_bf"][:, :],
                    t_catl[:, :], start=False, stop=True),
                    deps=[e15b, ecb1])
                prev_ecb = ecb
                d_evs = [P.step("vector", lambda e, hh=hh, h=g * 4 + hh, bcb=bcb:
                                e.tensor_scalar(
                    t_dbuf[:, hh * 128:(hh + 1) * 128],
                    psf(bcb, 0, 128, hh * 128, 128),
                    t_tm1[:, 32 + h:33 + h], 0.0, ALU.subtract, ALU.max),
                    deps=[ecb, e11] + ([prev_ee3] if prev_ee3 else []))
                    for hh in range(4)]
                if SUB <= 32:
                    continue
                ee1 = P.step("scalar", lambda e: e.activation(
                    t_ebuf[:, :], t_dbuf[:, 0:512], AF.Exp, scale=-1.0),
                    deps=d_evs + ([prev_ee4] if prev_ee4 else []))
                nl_b = t_con["neglmask_bf"][:, :].unsqueeze(1).broadcast_to(
                    [128, 4, 128])
                e3d = t_e1m[:, :].rearrange("p (hh t) -> p hh t", hh=4)
                eb3 = t_ebuf[:, :].rearrange("p (hh t) -> p hh t", hh=4)
                ee2 = P.step("vector", lambda e, e3d=e3d, eb3=eb3, nl_b=nl_b:
                             e.tensor_tensor(e3d, eb3, nl_b, ALU.mult),
                             deps=[ee1, evc0] + prev_xg_gp)
                d2_evs = [P.step("vector", lambda e, hh=hh, h=g * 4 + hh, bcb=bcb:
                                 e.tensor_scalar(
                    t_dbuf[:, hh * 128:(hh + 1) * 128],
                    psf(bcb, 0, 128, hh * 128, 128),
                    t_tm1[:, 0 + h:1 + h], 0.0, ALU.subtract, ALU.min),
                    deps=[ecb, e11, ee1]) for hh in range(4)]
                ee3 = P.step("scalar", lambda e: e.activation(
                    t_ebuf[:, :], t_dbuf[:, 0:512], AF.Exp),
                    deps=d2_evs + [ee2])
                bank_done(bcb, 1, [ee3])
                um_b = t_con["umask_bf"][:, :].unsqueeze(1).broadcast_to(
                    [128, 4, 128])
                e4d = t_e2m[:, :].rearrange("p (hh t) -> p hh t", hh=4)
                ee4 = P.step("vector", lambda e, e4d=e4d, eb3=eb3, um_b=um_b:
                             e.tensor_tensor(e4d, eb3, um_b, ALU.mult),
                             deps=[ee3] + prev_ptg_gp)
                prev_ee3 = ee3
                prev_ee4 = ee4

                if SUB <= 33:
                    continue
                if DEBUG_SCRATCH:
                    P.dma("sync", s_ddbg[cix, g, :, :].bitcast(BF16)[:, 0:512],
                          t_dbuf[:, 0:512], deps=d2_evs)
                    P.dma("sync", s_emdbg[cix, g, :, :], t_e1m[:, :],
                          deps=[ee2])
                bkk, bdk = bank(2)
                kk_evs = []
                for hh in range(4):
                    h = g * 4 + hh
                    pb = 64 * (h % 2)
                    dcq = h // 2
                    sl_k = t_kn[pb:pb + 64, qof + dcq * PT_ + ci * 128:
                                qof + dcq * PT_ + (ci + 1) * 128]
                    kk_evs.append(P.step("tensor",
                                         lambda e, hh=hh, bkk=bkk, sl_k=sl_k:
                                         e.matmul(
                        psf(bkk + hh % 2, 0, 128, (hh // 2) * 128, 128),
                        sl_k, sl_k, start=True, stop=True),
                        deps=kn_apply + (bdk if hh == 0 else [])))
                if SUB <= 335:
                    bank_done(bkk, 2, kk_evs)
                    continue
                if DEBUG_SCRATCH:
                    ekkd = P.step("vector", lambda e, bkk=bkk: e.tensor_copy(
                        t_kkdbg[:, :],
                        ps_t[0:128, 512 * bkk:512 * bkk + 512]),
                        deps=kk_evs)
                    edma1 = P.dma("sync", s_kkdbg[cix, g, :, :], t_kkdbg[:, :],
                                  deps=[ekkd])
                xg = []
                dbuf_bf = t_dbuf[:, 512:1024]
                for hh in range(4):
                    h = g * 4 + hh
                    if SUB == 336:
                        exg = P.step("vector", lambda e, hh=hh, h=h, bkk=bkk:
                                     e.tensor_copy(
                            t_xb[:, h * 128:(h + 1) * 128],
                            psf(bkk + hh % 2, 0, 128, (hh // 2) * 128, 128)),
                            deps=kk_evs + [ee2, e11])
                    elif SUB == 337:
                        exg = P.step("vector", lambda e, hh=hh, h=h, bkk=bkk:
                                     e.scalar_tensor_tensor(
                            t_xb[:, h * 128:(h + 1) * 128],
                            psf(bkk + hh % 2, 0, 128, (hh // 2) * 128, 128),
                            1.0, t_e1m[:, hh * 128:(hh + 1) * 128],
                            ALU.mult, ALU.mult),
                            deps=kk_evs + [ee2, e11])
                    elif SUB == 338:
                        exg = P.step("vector", lambda e, hh=hh, h=h, bkk=bkk:
                                     e.tensor_scalar(
                            t_xb[:, h * 128:(h + 1) * 128],
                            psf(bkk + hh % 2, 0, 128, (hh // 2) * 128, 128),
                            t_tm1[:, 16 + h:17 + h], None, ALU.mult),
                            deps=kk_evs + [ee2, e11])
                    else:
                        ex1 = P.step("vector",
                                     lambda e, hh=hh, h=h, bkk=bkk,
                                     dbuf_bf=dbuf_bf: e.tensor_scalar(
                            dbuf_bf[:, hh * 128:(hh + 1) * 128],
                            psf(bkk + hh % 2, 0, 128, (hh // 2) * 128, 128),
                            t_tm1[:, 16 + h:17 + h], None, ALU.mult),
                            deps=kk_evs + [ee3, e11])
                        exg = P.step("gpsimd",
                                     lambda e, hh=hh, h=h, dbuf_bf=dbuf_bf:
                                     e.tensor_tensor(
                            t_xb[:, h * 128:(h + 1) * 128],
                            dbuf_bf[:, hh * 128:(hh + 1) * 128],
                            t_e1m[:, hh * 128:(hh + 1) * 128], ALU.mult),
                            deps=[ex1, ee2])
                    xg.append(exg)
                    x_evs[h] = exg
                if DEBUG_SCRATCH:
                    ekkd2 = P.step("vector", lambda e, bkk=bkk: e.tensor_copy(
                        t_kkdbg[:, :],
                        ps_t[0:128, 512 * bkk:512 * bkk + 512]),
                        deps=xg + [edma1])
                    edma2 = P.dma("sync", s_kkdbg2[cix, g, :, :], t_kkdbg[:, :],
                                  deps=[ekkd2])
                    xg = xg + [ekkd2, edma2]
                bank_done(bkk, 2, xg)
                prev_xg_gp = [xg[-1]]
                tile_readers.append(kk_evs[-1])

                if SUB <= 34:
                    continue
                bkq, bdq = bank(2)
                kq_evs = []
                for hh in range(4):
                    h = g * 4 + hh
                    pb = 64 * (h % 2)
                    dcq = h // 2
                    sl_k = t_kn[pb:pb + 64, qof + dcq * PT_ + ci * 128:
                                qof + dcq * PT_ + (ci + 1) * 128]
                    sl_q = t_qn[pb:pb + 64, qof + dcq * PT_ + ci * 128:
                                qof + dcq * PT_ + (ci + 1) * 128]
                    kq_evs.append(P.step(
                        "tensor",
                        lambda e, hh=hh, bkq=bkq, sl_k=sl_k, sl_q=sl_q: e.matmul(
                            psf(bkq + hh % 2, 0, 128, (hh // 2) * 128, 128),
                            sl_k, sl_q, start=True, stop=True),
                        deps=kn_apply + qn_apply + (bdq if hh == 0 else [])))
                ptg = []
                for hh in range(4):
                    h = g * 4 + hh
                    pp1 = P.step("vector",
                                 lambda e, hh=hh, bkq=bkq, dbuf_bf=dbuf_bf:
                                 e.tensor_copy(
                        dbuf_bf[:, hh * 128:(hh + 1) * 128],
                        psf(bkq + hh % 2, 0, 128, (hh // 2) * 128, 128)),
                        deps=kq_evs + xg)
                    ptg.append(P.step("gpsimd",
                                      lambda e, hh=hh, h=h, dbuf_bf=dbuf_bf:
                                      e.tensor_tensor(
                        t_ptb[:, h * 128:(h + 1) * 128],
                        dbuf_bf[:, hh * 128:(hh + 1) * 128],
                        t_e2m[:, hh * 128:(hh + 1) * 128], ALU.mult),
                        deps=[pp1, ee4]))
                pt_evs += ptg
                prev_ptg_gp = [ptg[-1]]
                bank_done(bkq, 2, ptg)
                tile_readers.append(kq_evs[-1])

            if SUB <= 40:
                chunk_fence = P.end_fence()
                continue
            if DEBUG_SCRATCH:
                P.dma("sync", s_xdbg[cix, :, :], t_xb[:, :],
                      deps=[e for e in x_evs])
            xt_evs = []
            for half in range(2):
                tevs = [P.step("tensor", lambda e, h=half * 8 + hh, hh=hh:
                               e.transpose(
                    psb_t[0:128, hh * 128:(hh + 1) * 128],
                    t_xb[:, h * 128:(h + 1) * 128], t_con["eye_bf"][:, :]),
                    deps=[x_evs[half * 8 + hh], evc0] +
                         (list(psb_free[0]) if hh == 0 else []))
                    for hh in range(8)]
                exv = P.step("vector", lambda e, half=half: e.tensor_copy(
                    t_xtb[:, half * 1024:(half + 1) * 1024],
                    psb_t[0:128, 0:1024]), deps=tevs)
                psb_free[0] = (exv,)
                xt_evs.append(exv)

            zpar = 0
            zev = [None, None]
            xsrc, xtsrc = t_xb, t_xtb
            xsrc_ev = [ev for ev in x_evs]
            xtsrc_ev = xt_evs
            asrc_first = [e17, e18]
            for lev in range(NL):
                asrc = t_rb if lev == 0 else t_zb[1 - zpar]
                asrc_ev = asrc_first if lev == 0 else [zev[1 - zpar]]
                for half in range(2):
                    ba, bda = bank(2)
                    mms = [P.step(
                        "tensor",
                        lambda e, h=half * 8 + hh, hh=hh, ba=ba, asrc=asrc,
                        xtsrc=xtsrc: e.matmul(
                            ps_t[0:128,
                                 512 * ba + hh * 128:512 * ba + (hh + 1) * 128],
                            xtsrc[:, h * 128:(h + 1) * 128],
                            asrc[:, h * 128:(h + 1) * 128],
                            start=True, stop=True),
                        deps=(xtsrc_ev if hh == 0 else []) +
                             (asrc_ev if hh == 0 else []) +
                             (bda if hh == 0 else []))
                        for hh in range(8)]
                    eva = P.step("vector",
                                 lambda e, half=half, ba=ba, asrc=asrc, zp=zpar:
                                 e.scalar_tensor_tensor(
                        t_zb[zp][:, half * 1024:(half + 1) * 1024],
                        ps_t[0:128, 512 * ba:512 * ba + 1024], 1.0,
                        asrc[:, half * 1024:(half + 1) * 1024],
                        ALU.mult, ALU.add), deps=mms + asrc_ev +
                        ([zev[zpar]] if zev[zpar] else []))
                    bank_done(ba, 2, [eva])
                    zev[zpar] = eva
                zpar = 1 - zpar

                if lev < NL - 1:
                    xdst = t_xp[lev % 2]
                    xtdst = t_xpt[lev % 2]
                    new_x_ev = []
                    new_xt_ev = []
                    for which in range(2):
                        lop = xtsrc if which == 0 else xsrc
                        rop = xsrc if which == 0 else xtsrc
                        dst = xdst if which == 0 else xtdst
                        for half in range(2):
                            bs, bds = bank(2)
                            mms = [P.step(
                                "tensor",
                                lambda e, h=half * 8 + hh, hh=hh, bs=bs,
                                lop=lop, rop=rop: e.matmul(
                                    ps_t[0:128, 512 * bs + hh * 128:
                                         512 * bs + (hh + 1) * 128],
                                    lop[:, h * 128:(h + 1) * 128],
                                    rop[:, h * 128:(h + 1) * 128],
                                    start=True, stop=True),
                                deps=((xsrc_ev + xtsrc_ev) if hh == 0 else []) +
                                     (bds if hh == 0 else []))
                                for hh in range(8)]
                            if which == 0:
                                ev2 = P.step("vector",
                                             lambda e, half=half, bs=bs, dst=dst:
                                             e.tensor_copy(
                                    dst[:, half * 1024:(half + 1) * 1024],
                                    ps_t[0:128, 512 * bs:512 * bs + 1024]),
                                    deps=mms)
                                new_x_ev.append(ev2)
                            else:
                                ev2 = P.step("scalar",
                                             lambda e, half=half, bs=bs, dst=dst:
                                             e.copy(
                                    dst[:, half * 1024:(half + 1) * 1024],
                                    ps_t[0:128, 512 * bs:512 * bs + 1024]),
                                    deps=mms)
                                new_xt_ev.append(ev2)
                            bank_done(bs, 2, [ev2])
                    xsrc, xtsrc = xdst, xtdst
                    xsrc_ev, xtsrc_ev = new_x_ev, new_xt_ev
            zfin = t_zb[1 - zpar]
            zfin_ev = zev[1 - zpar]

            if SUB <= 5:
                chunk_fence = P.end_fence()
                continue
            ft_half = []
            for half in range(2):
                ftevs = [P.step("tensor", lambda e, h=half * 8 + hh, hh=hh,
                                zfin=zfin: e.transpose(
                    psb_t[0:64, hh * 128:(hh + 1) * 128],
                    zfin[:, h * 128 + 64:(h + 1) * 128], t_con["eye_bf"][:, :]),
                    deps=[zfin_ev, evc0] +
                         (list(psb_free[0]) if hh == 0 else []))
                    for hh in range(8)]
                efth = P.step("vector", lambda e, half=half: e.tensor_copy(
                    t_ftb[:, half * 1024:(half + 1) * 1024],
                    psb_t[0:64, 0:1024]), deps=ftevs)
                psb_free[0] = (efth,)
                ft_half.append(efth)
            eft = ft_half[-1]

            bq, bdq2 = bank(2)
            sel_evs = [P.step("tensor", lambda e, j=j, bq=bq: e.matmul(
                psf(bq, 0, 128, 0, 1024)[0:128, j * 128:(j + 1) * 128]
                if False else
                ps_t[0:128, 512 * bq + j * 128:512 * bq + (j + 1) * 128],
                t_con["selj_bf"][:, j * 128:(j + 1) * 128],
                t_ecbf[:, :], start=True, stop=True),
                deps=[e9, evc0] + (bdq2 if j == 0 else []))
                for j in range(8)]
            qn3 = t_qn[:, qof:qof + 8 * PT_].rearrange(
                "p (dc t) -> p dc t", dc=8)[:, :, ci * 128:(ci + 1) * 128]
            eqt = P.step("vector", lambda e, bq=bq, qn3=qn3: e.tensor_tensor(
                t_qtb[:, :].rearrange("p (dc t) -> p dc t", dc=8), qn3,
                ps_t[0:128, 512 * bq:512 * bq + 1024].rearrange(
                    "p (dc t) -> p dc t", dc=8), ALU.mult),
                deps=sel_evs + qn_apply)
            bank_done(bq, 2, [eqt])
            tile_readers.append(eqt)

            if SUB <= 6:
                chunk_fence = P.end_fence()
                continue
            bgg, bdg = bank(2)
            g_evs = [P.step("tensor", lambda e, h=h, bgg=bgg, zfin=zfin:
                            e.matmul(
                ps_t[0:64, 512 * bgg + h * 64:512 * bgg + (h + 1) * 64],
                zfin[:, h * 128 + 64:(h + 1) * 128],
                t_kdb[:, h * 64:(h + 1) * 64], start=True, stop=True),
                deps=[zfin_ev, e16] + (bdg if h == 0 else []))
                for h in range(16)]
            i64b = t_con["i64_f32"][:, :].unsqueeze(1).broadcast_to([64, 16, 64])
            gcb = t_gcols[:, cix * 16:(cix + 1) * 16].unsqueeze(2).broadcast_to(
                [64, 16, 64])
            egi = P.step("vector", lambda e, i64b=i64b, gcb=gcb: e.tensor_tensor(
                t_gib[:, :].rearrange("p (h e) -> p h e", h=16), i64b, gcb,
                ALU.mult), deps=[ev_gamma[cix], evc0] + cf)
            eat = P.step("vector", lambda e, bgg=bgg: e.scalar_tensor_tensor(
                t_atb[:, :], ps_t[0:64, 512 * bgg:512 * bgg + 1024], -1.0,
                t_gib[:, :], ALU.mult, ALU.add), deps=g_evs + [egi])
            bank_done(bgg, 2, [eat])
            bgu, bdu = bank(2)
            u_evs = [P.step("tensor", lambda e, h=h, bgu=bgu, zfin=zfin:
                            e.matmul(
                ps_t[0:64, 512 * bgu + h * 64:512 * bgu + (h + 1) * 64],
                t_kdb[:, h * 64:(h + 1) * 64],
                zfin[:, h * 128:h * 128 + 64], start=True, stop=True),
                deps=[zfin_ev, e16] + (bdu if h == 0 else []))
                for h in range(16)]
            eu = P.step("vector", lambda e, bgu=bgu: e.tensor_copy(
                t_ub[:, :], ps_t[0:64, 512 * bgu:512 * bgu + 1024]), deps=u_evs)
            bank_done(bgu, 2, [eu])

            cp = cix % 2
            cq = 1 - cp
            pa, pat, pu = compose_prev
            last = (cix == NCH - 1)
            bc1, bdc1 = bank(2)
            acc_m = [P.step("tensor", lambda e, h=h, bc1=bc1, cq=cq: e.matmul(
                ps_t[0:64, 512 * bc1 + h * 64:512 * bc1 + (h + 1) * 64],
                t_atb[:, h * 64:(h + 1) * 64],
                t_acc[cq][:, h * 64:(h + 1) * 64], start=True, stop=True),
                deps=[eat, pa] + (bdc1 if h == 0 else []))
                for h in range(16)]
            eacc = P.step("vector", lambda e, bc1=bc1, cp=cp: e.tensor_copy(
                t_acc[cp][:, :], ps_t[0:64, 512 * bc1:512 * bc1 + 1024]),
                deps=acc_m + [pa])
            bank_done(bc1, 2, [eacc])
            bc2, bdc2 = bank(2)
            acct_m = [P.step("tensor", lambda e, h=h, bc2=bc2, cq=cq: e.matmul(
                ps_t[0:64, 512 * bc2 + h * 64:512 * bc2 + (h + 1) * 64],
                t_acc[cq][:, h * 64:(h + 1) * 64],
                t_atb[:, h * 64:(h + 1) * 64], start=True, stop=True),
                deps=[eat, pa, pat] + (bdc2 if h == 0 else []))
                for h in range(16)]
            eacct = P.step("vector", lambda e, bc2=bc2, cp=cp: e.tensor_copy(
                t_acct[cp][:, :], ps_t[0:64, 512 * bc2:512 * bc2 + 1024]),
                deps=acct_m + [pat])
            evl2 = [eacct]
            if last:
                eacctF = P.step("vector", lambda e, bc2=bc2: e.tensor_copy(
                    t_accF[:, :], ps_t[0:64, 512 * bc2:512 * bc2 + 1024]),
                    deps=acct_m)
                evl2.append(eacctF)
            bank_done(bc2, 2, evl2)
            bc3, bdc3 = bank(2)
            ucc_m = []
            for h in range(16):
                ucc_m.append(P.step("tensor",
                                    lambda e, h=h, bc3=bc3, cq=cq: e.matmul(
                    ps_t[0:64, 512 * bc3 + h * 64:512 * bc3 + (h + 1) * 64],
                    t_atb[:, h * 64:(h + 1) * 64],
                    t_ucc[cq][:, h * 64:(h + 1) * 64], start=True, stop=False),
                    deps=[eat, pu] + (bdc3 if h == 0 else [])))
                ucc_m.append(P.step("tensor", lambda e, h=h, bc3=bc3: e.matmul(
                    ps_t[0:64, 512 * bc3 + h * 64:512 * bc3 + (h + 1) * 64],
                    t_con["i64rep_bf"][:, 0:64],
                    t_ub[:, h * 64:(h + 1) * 64], start=False, stop=True),
                    deps=[eu]))
            eucc = P.step("vector", lambda e, bc3=bc3, cp=cp: e.tensor_copy(
                t_ucc[cp][:, :], ps_t[0:64, 512 * bc3:512 * bc3 + 1024]),
                deps=ucc_m + [pu])
            evl3 = [eucc]
            if last:
                euccF = P.step("vector", lambda e, bc3=bc3: e.tensor_copy(
                    t_uccF[:, :], ps_t[0:64, 512 * bc3:512 * bc3 + 1024]),
                    deps=ucc_m)
                evl3.append(euccF)
            bank_done(bc3, 2, evl3)
            compose_prev = (eacc, eacct, eucc)

            w1 = P.dma("sync", s_z[cix, :, :], zfin[:, :], deps=[zfin_ev])
            w2 = P.dma("sync", s_ft[cix, :, :], t_ftb[:, :], deps=[eft])
            w3 = P.dma("sync", s_kd[cix, :, :], t_kdb[:, :], deps=[e16])
            w4 = P.dma("sync", s_pt[cix, :, :], t_ptb[:, :], deps=pt_evs)
            w5 = P.dma("sync",
                       s_qt[cix].rearrange("j p t -> p j t"),
                       t_qtb[:, :].rearrange("p (j t) -> p j t", j=8),
                       deps=[eqt])
            ev_chunk_done[cix] = (w1, w2, w3, w4, w5)
            chunk_fence = P.end_fence()
            ktm_guard[ks] = (e16, e18, w3)
        tile_fence[ti] = tuple(tile_readers)

    if STAGE <= 3 or SUB < 99:
        return _cut(P.fence_all())
    a_fence = P.fence_all()
    a_ctx.close()

    gb_ctx = ExitStack()
    t_s0 = gb_ctx.enter_context(nc.sbuf_tensor("t_s0", [64, 1024], F32))
    t_s0b = gb_ctx.enter_context(nc.sbuf_tensor("t_s0b", [64, 1024], BF16))
    g_ctx = ExitStack()
    t_gat = g_ctx.enter_context(nc.sbuf_tensor("t_gat", [64, 4096], F32))
    t_gau = g_ctx.enter_context(nc.sbuf_tensor("t_gau", [64, 4096], F32))
    t_sst = g_ctx.enter_context(nc.sbuf_tensor("t_sst", [64, 16 * 256], F32))
    t_selv = g_ctx.enter_context(nc.sbuf_tensor("t_selv", [1, 4], F32))
    t_sel64 = g_ctx.enter_context(nc.sbuf_tensor("t_sel64", [64, 4], F32))

    agi_at = s_agi.rearrange("(h ty r) e -> h ty r e", h=16, ty=2)
    ew1 = P.dma("sync", agi_at[:, 0, :, :].rearrange("h r e -> r h e"),
                t_accF[:, :].rearrange("r (h e) -> r h e", h=16),
                deps=[eacctF] + list(a_fence))
    ew2 = P.dma("sync", agi_at[:, 1, :, :].rearrange("h r e -> r h e"),
                t_uccF[:, :].rearrange("r (h e) -> r h e", h=16),
                deps=[euccF])
    ecc = P.step("gpsimd", lambda e: e.collective_compute(
        "AllGather", ALU.bypass,
        replica_groups=[[0, 1, 2, 3], [4, 5, 6, 7]],
        ins=[s_agi.opt()], outs=[s_ago.opt()]), deps=[ew1, ew2],
        semkey="scc")
    ago5 = s_ago.rearrange("(q h ty r) e -> q h ty r e", q=4, h=16, ty=2)
    gl_evs = []
    for q in range(4):
        gl_evs.append(P.dma(
            "sync",
            t_gat[:, q * 1024:(q + 1) * 1024].rearrange(
                "r (h e) -> r h e", h=16),
            ago5[q, :, 0, :, :].rearrange("h r e -> r h e"), deps=[ecc]))
        gl_evs.append(P.dma(
            "sync",
            t_gau[:, q * 1024:(q + 1) * 1024].rearrange(
                "r (h e) -> r h e", h=16),
            ago5[q, :, 1, :, :].rearrange("h r e -> r h e"), deps=[ecc]))
    ev_selv = P.dma("sync", t_selv[:, :], d_selv)
    bsl, bdsl = bank(1)
    esm = P.step("tensor", lambda e, bsl=bsl: e.matmul(
        ps_t[0:64, 512 * bsl:512 * bsl + 4], t_con["ones1"][0:1, 0:64],
        t_selv[:, :], start=True, stop=True), deps=[ev_selv, evc0] + bdsl)
    esel = P.step("vector", lambda e, bsl=bsl: e.tensor_copy(
        t_sel64[:, :], ps_t[0:64, 512 * bsl:512 * bsl + 4]), deps=[esm])
    bank_done(bsl, 1, [esel])

    sst5 = t_sst[:, :].rearrange("p (h e q) -> p h e q", h=16, q=4)
    bstage, bdst = bank(4, hold=True)
    ev_s0 = []
    prev_copy = None
    for h in range(16):
        prev_ev = prev_copy
        for q in range(4):
            pslot = ps_t[0:64, 512 * bstage + h * 64:512 * bstage + (h + 1) * 64]
            if q == 0:
                m1 = P.step("tensor", lambda e, h=h, pslot=pslot: e.matmul(
                    pslot, t_con["i64_f32"][:, :],
                    t_gau[:, h * 64:(h + 1) * 64],
                    start=True, stop=True),
                    deps=gl_evs + bdst + [evc0] +
                         ([prev_ev] if prev_ev else []))
            else:
                sst_prev = sst5[:, h, :, q - 1]
                m0 = P.step("tensor",
                            lambda e, h=h, q=q, pslot=pslot, sst_prev=sst_prev:
                            e.matmul(
                    pslot, t_gat[:, q * 1024 + h * 64:q * 1024 + (h + 1) * 64],
                    sst_prev, start=True, stop=False),
                    deps=gl_evs + [prev_ev])
                m1 = P.step("tensor", lambda e, h=h, q=q, pslot=pslot: e.matmul(
                    pslot, t_con["i64_f32"][:, :],
                    t_gau[:, q * 1024 + h * 64:q * 1024 + (h + 1) * 64],
                    start=False, stop=True), deps=[])
            prev_ev = P.step("vector", lambda e, pslot=pslot, h=h, q=q:
                             e.tensor_copy(sst5[:, h, :, q], pslot), deps=[m1])
        prev_copy = prev_ev
        ev_s0.append(prev_ev)
    bank_done(bstage, 4, ev_s0)
    sst4 = t_sst[:, :].rearrange("p (he q) -> p he q", q=4)
    sel_b = t_sel64[:, :].unsqueeze(1).broadcast_to([64, 1024, 4])
    emask = P.step("vector", lambda e, sst4=sst4, sel_b=sel_b: e.tensor_tensor(
        sst4, sst4, sel_b, ALU.mult), deps=ev_s0 + [esel])
    ered = P.step("vector", lambda e, sst4=sst4: e.tensor_reduce(
        t_s0[:, :].unsqueeze(2), sst4, AXX, ALU.add), deps=[emask])
    ev_s0f = P.step("vector", lambda e: e.tensor_copy(t_s0b[:, :], t_s0[:, :]),
                    deps=[ered])
    if STAGE <= 4:
        return _cut(P.fence_all())
    g_fence = P.fence_all()
    g_ctx.close()

    b_ctx = ExitStack()

    def bsb(name, shape, dtype):
        return b_ctx.enter_context(nc.sbuf_tensor(name, shape, dtype))

    t_wo = bsb("t_wo", [128, 8 * D], BF16)
    t_wld2 = bsb("t_wld2", [128, D], F32)
    t_zl = [bsb(f"t_zl{i}", [128, 2048], BF16) for i in range(2)]
    t_ftl = [bsb(f"t_ftl{i}", [64, 2048], BF16) for i in range(2)]
    t_kdl = [bsb(f"t_kdl{i}", [128, 1024], BF16) for i in range(2)]
    t_ptl = [bsb(f"t_ptl{i}", [128, 2048], BF16) for i in range(2)]
    t_qtl = [bsb(f"t_qtl{i}", [64, 2048], BF16) for i in range(2)]
    t_gl = [bsb(f"t_gl{i}", [128, D], BF16) for i in range(2)]
    t_wh = bsb("t_wh", [128, 1024], BF16)
    t_sg = bsb("t_sg", [64, 1024], BF16)
    t_sf = bsb("t_sf", [64, 1024], F32)
    t_sb2 = bsb("t_sb2", [64, 1024], BF16)
    t_ytm = bsb("t_ytm", [128, D], F32)
    t_ysq = bsb("t_ysq", [128, D], F32)
    t_yst = bsb("t_yst", [128, 4], F32)
    t_yout = bsb("t_yout", [128, D], BF16)

    woev = []
    wo_par = None
    for kc in range(8):
        ev1 = P.dma("sync", t_wld2[:, :], d_w["wo"][kc * 128:(kc + 1) * 128, :],
                    deps=[wo_par] if wo_par else list(g_fence))
        evcp = P.step("scalar", lambda e, kc=kc: e.copy(
            t_wo[:, kc * D:(kc + 1) * D], t_wld2[:, :]), deps=[ev1])
        wo_par = evcp
        woev.append(evcp)

    ev_sf = P.step("vector", lambda e: e.tensor_copy(t_sf[:, :], t_s0[:, :]),
                   deps=[ered] + list(g_fence))
    ev_sb = P.step("vector", lambda e: e.tensor_copy(t_sb2[:, :], t_s0b[:, :]),
                   deps=[ev_s0f] + list(g_fence))

    ld_guard = [(), ()]
    bfence = ()
    ev_y_w = [None] * NCH
    for cix in range(NCH):
        par = cix % 2
        P.begin_fence()
        bf = list(bfence)
        dch = list(ev_chunk_done[cix])
        lz = P.dma("sync", t_zl[par][:, :], s_z[cix, :, :],
                   deps=[dch[0]] + list(ld_guard[par]))
        lft = P.dma("sync", t_ftl[par][:, :], s_ft[cix, :, :], deps=[dch[1]])
        lkd = P.dma("sync", t_kdl[par][:, :], s_kd[cix, :, :], deps=[dch[2]])
        lpt = P.dma("sync", t_ptl[par][:, :], s_pt[cix, :, :], deps=[dch[3]])
        lqt = P.dma("sync",
                    t_qtl[par][:, :].rearrange("e (j hh t) -> e j hh t",
                                               j=8, hh=2),
                    s_qt[cix].rearrange("j (hh e) t -> e j hh t", hh=2),
                    deps=[dch[4]])
        lg = P.dma("sync", t_gl[par][:, :], s_gate[cix * 128:(cix + 1) * 128, :],
                   deps=[ev_gate_w[cix]])

        bfs, bdf = bank(2)
        fs_m = [P.step("tensor", lambda e, h=h, bfs=bfs, par=par: e.matmul(
            ps_t[0:128, 512 * bfs + h * 64:512 * bfs + (h + 1) * 64],
            t_ftl[par][:, h * 128:(h + 1) * 128],
            t_sb2[:, h * 64:(h + 1) * 64], start=True, stop=True),
            deps=[lft, ev_sb] + (bdf if h == 0 else []))
            for h in range(16)]
        w0v = t_zl[par][:, :].rearrange("p (h two e) -> p h two e",
                                        h=16, two=2)[:, :, 0, :]
        ewh = P.step("vector", lambda e, bfs=bfs, w0v=w0v: e.scalar_tensor_tensor(
            t_wh[:, :].rearrange("p (h e) -> p h e", h=16),
            ps_t[0:128, 512 * bfs:512 * bfs + 1024].rearrange(
                "p (h e) -> p h e", h=16),
            -1.0, w0v, ALU.mult, ALU.add), deps=fs_m + [lz] + bf)
        bank_done(bfs, 2, [ewh])

        bo, bdo = bank(2)
        o_m = []
        for h in range(16):
            o_m.append(P.step("tensor", lambda e, h=h, bo=bo, par=par: e.matmul(
                ps_t[0:128, 512 * bo + h * 64:512 * bo + (h + 1) * 64],
                t_qtl[par][:, h * 128:(h + 1) * 128],
                t_sb2[:, h * 64:(h + 1) * 64], start=True, stop=False),
                deps=[lqt, ev_sb] + (bdo if h == 0 else [])))
            o_m.append(P.step("tensor", lambda e, h=h, bo=bo, par=par: e.matmul(
                ps_t[0:128, 512 * bo + h * 64:512 * bo + (h + 1) * 64],
                t_ptl[par][:, h * 128:(h + 1) * 128],
                t_wh[:, h * 64:(h + 1) * 64], start=False, stop=True),
                deps=[lpt, ewh]))
        ey = P.step("vector", lambda e, bo=bo: e.tensor_copy(
            t_ytm[:, :], ps_t[0:128, 512 * bo:512 * bo + 1024]),
            deps=o_m + bf)
        bank_done(bo, 2, [ey])

        esg = P.step("vector", lambda e, cix=cix: e.tensor_tensor(
            t_sg[:, :].rearrange("p (h e) -> p h e", h=16),
            t_sb2[:, :].rearrange("p (h e) -> p h e", h=16),
            t_gcols[:, cix * 16:(cix + 1) * 16].unsqueeze(2).broadcast_to(
                [64, 16, 64]), ALU.mult), deps=[ev_sb, ev_gamma[cix]] + bf)
        bs2, bds2 = bank(2)
        s_m = []
        for h in range(16):
            s_m.append(P.step("tensor", lambda e, h=h, bs2=bs2, par=par:
                              e.matmul(
                ps_t[0:64, 512 * bs2 + h * 64:512 * bs2 + (h + 1) * 64],
                t_kdl[par][:, h * 64:(h + 1) * 64],
                t_wh[:, h * 64:(h + 1) * 64], start=True, stop=False),
                deps=[lkd, ewh] + (bds2 if h == 0 else [])))
            s_m.append(P.step("tensor", lambda e, h=h, bs2=bs2: e.matmul(
                ps_t[0:64, 512 * bs2 + h * 64:512 * bs2 + (h + 1) * 64],
                t_con["i64rep_bf"][:, 0:64],
                t_sg[:, h * 64:(h + 1) * 64], start=False, stop=True),
                deps=[esg]))
        ev_sf = P.step("vector", lambda e, bs2=bs2: e.tensor_copy(
            t_sf[:, :], ps_t[0:64, 512 * bs2:512 * bs2 + 1024]),
            deps=s_m + [ev_sf])
        ev_sb = P.step("vector", lambda e, bs2=bs2: e.tensor_copy(
            t_sb2[:, :], ps_t[0:64, 512 * bs2:512 * bs2 + 1024]),
            deps=s_m + [ev_sb] + [esg] + fs_m + o_m[:1])
        bank_done(bs2, 2, [ev_sf, ev_sb])

        er1 = P.step("scalar", lambda e: e.activation(
            t_ysq[:, :], t_ytm[:, :], AF.Square, accum_out=t_yst[:, 0:1]),
            deps=[ey] + bf)
        er2 = P.step("scalar", lambda e: e.activation(
            t_yst[:, 1:2], t_yst[:, 0:1], AF.Ln, scale=1.0 / D, bias=t_eps[:, :]),
            deps=[er1])
        er3 = P.step("scalar", lambda e: e.activation(
            t_yst[:, 2:3], t_yst[:, 1:2], AF.Exp, scale=-0.5), deps=[er2])
        er4 = P.step("vector", lambda e: e.scalar_tensor_tensor(
            t_ysq[:, :], t_ytm[:, :], t_yst[:, 2:3], t_gnb[:, :],
            ALU.mult, ALU.mult), deps=[er3, ev_gnb, er1])
        er5 = P.step("vector", lambda e, par=par: e.tensor_mul(
            t_yout[:, :], t_ysq[:, :], t_gl[par][:, :]), deps=[er4, lg] + bf)
        ev_y_w[cix] = P.dma("sync", s_y[cix * 128:(cix + 1) * 128, :],
                            t_yout[:, :], deps=[er5])
        bfence = P.end_fence()
        ld_guard[par] = (ewh, ey, er5, s_m[-1], o_m[-1], fs_m[-1], lqt)

    if STAGE <= 5:
        return _cut(P.fence_all())
    t_yfm = bsb("t_yfm", [128, 8 * G1K], BF16)
    t_xo = bsb("t_xo", [128, 2 * D], F32)
    t_att = bsb("t_att", [128, D], F32)
    t_asq = bsb("t_asq", [128, D], F32)
    t_ast = bsb("t_ast", [128, 32], F32)
    t_xm = bsb("t_xm", [128, D], F32)
    t_xn2b = bsb("t_xn2b", [128, D], BF16)

    ev_xmid_w = [None] * NCH
    ev_xn2_w = [None] * NCH
    yfm_guard = []
    xo_ev = [None, None]
    c1f = ()
    for gi in range(TOK // G1K):
        ydeps = [ev_y_w[gi * (G1K // 128) + i] for i in range(G1K // 128)]
        yevs = [P.dma("sync", t_yfm[:, dc * G1K:(dc + 1) * G1K],
                      s_y[gi * G1K:(gi + 1) * G1K, dc * 128:(dc + 1) * 128],
                      deps=ydeps + yfm_guard, transpose=True)
                for dc in range(8)]
        for cc in range(G1K // 128):
            cix = gi * (G1K // 128) + cc
            par = cix % 2
            so = par * 16
            P.begin_fence()
            cfl = list(c1f)
            b0, bd = bank(2)
            mme = []
            for kc in range(8):
                for nh in range(2):
                    mme.append(P.step(
                        "tensor", lambda e, kc=kc, nh=nh, b0=b0, cc=cc: e.matmul(
                            psf(b0 + nh, 0, 128, 0, 512),
                            t_yfm[:, kc * G1K + cc * 128:kc * G1K + (cc + 1) * 128],
                            t_wo[:, kc * D + nh * 512:kc * D + (nh + 1) * 512],
                            start=(kc == 0), stop=(kc == 7)),
                        deps=[yevs[kc], woev[kc]] + (bd if kc == 0 else [])))
            aps = ps_t[0:128, 512 * b0:512 * b0 + 1024]
            el1 = P.step("scalar", lambda e, aps=aps, so=so: e.activation(
                t_asq[:, :], aps, AF.Square, accum_out=t_ast[:, so:so + 1]),
                deps=mme + cfl)
            el2 = P.step("vector", lambda e, aps=aps, so=so: e.reduce_sum(
                t_ast[:, so + 1:so + 2], aps, AXX), deps=mme + cfl + [el1])
            el3 = P.step("vector", lambda e, so=so: e.tensor_scalar_mul(
                t_ast[:, so + 2:so + 3], t_ast[:, so + 1:so + 2], 1.0 / D),
                deps=[el2])
            el4 = P.step("vector", lambda e, so=so: e.tensor_mul(
                t_ast[:, so + 3:so + 4], t_ast[:, so + 2:so + 3],
                t_ast[:, so + 2:so + 3]), deps=[el3])
            el5 = P.step("vector", lambda e, so=so: e.tensor_scalar_mul(
                t_ast[:, so + 4:so + 5], t_ast[:, so:so + 1], 1.0 / D),
                deps=[el1])
            el6 = P.step("vector", lambda e, so=so: e.tensor_sub(
                t_ast[:, so + 5:so + 6], t_ast[:, so + 4:so + 5],
                t_ast[:, so + 3:so + 4]), deps=[el4, el5])
            el7 = P.step("scalar", lambda e, so=so: e.activation(
                t_ast[:, so + 6:so + 7], t_ast[:, so + 5:so + 6], AF.Ln,
                bias=t_eps[:, :]), deps=[el6])
            el8 = P.step("scalar", lambda e, so=so: e.activation(
                t_ast[:, so + 7:so + 8], t_ast[:, so + 6:so + 7], AF.Exp,
                scale=-0.5), deps=[el7])
            el9 = P.step("vector", lambda e, aps=aps, so=so: e.tensor_scalar(
                t_att[:, :], aps, t_ast[:, so + 2:so + 3],
                t_ast[:, so + 7:so + 8], ALU.subtract, ALU.mult),
                deps=[el8] + cfl)
            bank_done(b0, 2, [el9])
            exl = P.dma("sync", t_xo[:, par * D:(par + 1) * D],
                        d_x[cix * 128:(cix + 1) * 128, :],
                        deps=[xo_ev[par]] if xo_ev[par] else [])
            exm = P.step("vector", lambda e, par=par: e.tensor_add(
                t_xm[:, :], t_att[:, :], t_xo[:, par * D:(par + 1) * D]),
                deps=[el9, exl] + cfl)
            xo_ev[par] = exm
            ev_xmid_w[cix] = P.dma("sync", s_xmid[cix * 128:(cix + 1) * 128, :],
                                   t_xm[:, :], deps=[exm])
            em1 = P.step("scalar", lambda e, so=so: e.activation(
                t_asq[:, :], t_xm[:, :], AF.Square,
                accum_out=t_ast[:, so + 8:so + 9]), deps=[exm, el1])
            em2 = P.step("vector", lambda e, so=so: e.reduce_sum(
                t_ast[:, so + 9:so + 10], t_xm[:, :], AXX), deps=[exm])
            em3 = P.step("vector", lambda e, so=so: e.tensor_scalar_mul(
                t_ast[:, so + 10:so + 11], t_ast[:, so + 9:so + 10], 1.0 / D),
                deps=[em2])
            em4 = P.step("vector", lambda e, so=so: e.tensor_mul(
                t_ast[:, so + 11:so + 12], t_ast[:, so + 10:so + 11],
                t_ast[:, so + 10:so + 11]), deps=[em3])
            em5 = P.step("vector", lambda e, so=so: e.tensor_scalar_mul(
                t_ast[:, so + 12:so + 13], t_ast[:, so + 8:so + 9], 1.0 / D),
                deps=[em1])
            em6 = P.step("vector", lambda e, so=so: e.tensor_sub(
                t_ast[:, so + 13:so + 14], t_ast[:, so + 12:so + 13],
                t_ast[:, so + 11:so + 12]), deps=[em4, em5])
            em7 = P.step("scalar", lambda e, so=so: e.activation(
                t_ast[:, so + 14:so + 15], t_ast[:, so + 13:so + 14], AF.Ln,
                bias=t_eps[:, :]), deps=[em6])
            em8 = P.step("scalar", lambda e, so=so: e.activation(
                t_ast[:, so + 15:so + 16], t_ast[:, so + 14:so + 15], AF.Exp,
                scale=-0.5), deps=[em7])
            em9 = P.step("vector", lambda e, so=so: e.tensor_scalar(
                t_xn2b[:, :], t_xm[:, :], t_ast[:, so + 10:so + 11],
                t_ast[:, so + 15:so + 16], ALU.subtract, ALU.mult),
                deps=[em8] + cfl)
            ev_xn2_w[cix] = P.dma("sync", s_xn2[cix * 128:(cix + 1) * 128, :],
                                  t_xn2b[:, :], deps=[em9])
            c1f = P.end_fence()
        yfm_guard = [mme[-1]]

    if STAGE <= 6:
        return _cut(P.fence_all())
    b_fence = P.fence_all()
    b_ctx.close()
    gb_ctx.close()

    c_ctx = ExitStack()

    def csb(name, shape, dtype):
        return c_ctx.enter_context(nc.sbuf_tensor(name, shape, dtype))

    t_fwg = csb("t_fwg", [128, 8 * HID], BF16)
    t_fwu = csb("t_fwu", [128, 8 * HID], BF16)
    t_fwd = csb("t_fwd", [128, 24 * D], BF16)
    t_x2f = csb("t_x2f", [128, 8 * G1K], BF16)
    t_fth = csb("t_fth", [128, 256], F32)
    t_fsg = csb("t_fsg", [128, 256], BF16)
    t_hbf = csb("t_hbf", [128, 2 * 256], BF16)
    t_fo = csb("t_fo", [128, 2 * D], BF16)

    SFF = 1.0 / 127.0
    fw_deps = list(b_fence) + ffn_cast_ev
    fgw = [P.dma("sync", t_fwg[:, dc * HID:(dc + 1) * HID],
                 s_wgbf[:, dc * 128:(dc + 1) * 128], deps=fw_deps,
                 transpose=True) for dc in range(8)]
    fuw = [P.dma("sync", t_fwu[:, dc * HID:(dc + 1) * HID],
                 s_wubf[:, dc * 128:(dc + 1) * 128], deps=fw_deps,
                 transpose=True) for dc in range(8)]
    fdw = [P.dma("sync", t_fwd[:, hc * D:(hc + 1) * D],
                 s_wdbf[:, hc * 128:(hc + 1) * 128], deps=fw_deps,
                 transpose=True) for hc in range(24)]

    ev_fdown_w = [None] * NCH
    t_fl = csb("t_fl", [128, D], BF16)
    t_xml = csb("t_xml", [128, D], F32)
    t_osq = csb("t_osq", [128, D], F32)
    t_ost = csb("t_ost", [128, 16], F32)
    c2b_prev = {"eo9": None, "eoA": None, "ew": None}

    def emit_c2b(cix):
        par = cix % 2
        so = par * 8
        prev_eo9, prev_eoA, prev_ew = (c2b_prev["eo9"], c2b_prev["eoA"],
                                       c2b_prev["ew"])
        lf = P.dma("sync", t_fl[:, :],
                   s_fdown[cix * 128:(cix + 1) * 128, :],
                   deps=[ev_fdown_w[cix]] + ([prev_eo9] if prev_eo9 else []))
        lx = P.dma("sync", t_xml[:, :],
                   s_xmid[cix * 128:(cix + 1) * 128, :],
                   deps=[ev_xmid_w[cix]] + ([prev_eoA] if prev_eoA else []))
        fa = t_fl[:, :]
        eo1 = P.step("scalar", lambda e, fa=fa, so=so: e.activation(
            t_osq[:, :], fa, AF.Square, accum_out=t_ost[:, so:so + 1]),
            deps=[lf] + ([prev_ew] if prev_ew else []))
        eo2 = P.step("vector", lambda e, fa=fa, so=so: e.reduce_sum(
            t_ost[:, so + 1:so + 2], fa, AXX), deps=[lf])
        eo3 = P.step("vector", lambda e, so=so: e.tensor_scalar_mul(
            t_ost[:, so + 2:so + 3], t_ost[:, so + 1:so + 2], 1.0 / D),
            deps=[eo2])
        eo4 = P.step("vector", lambda e, so=so: e.tensor_mul(
            t_ost[:, so + 3:so + 4], t_ost[:, so + 2:so + 3],
            t_ost[:, so + 2:so + 3]), deps=[eo3])
        eo5 = P.step("vector", lambda e, so=so: e.tensor_scalar_mul(
            t_ost[:, so + 4:so + 5], t_ost[:, so:so + 1], 1.0 / D), deps=[eo1])
        eo6 = P.step("vector", lambda e, so=so: e.tensor_sub(
            t_ost[:, so + 5:so + 6], t_ost[:, so + 4:so + 5],
            t_ost[:, so + 3:so + 4]), deps=[eo4, eo5])
        eo7 = P.step("scalar", lambda e, so=so: e.activation(
            t_ost[:, so + 6:so + 7], t_ost[:, so + 5:so + 6], AF.Ln,
            bias=t_eps[:, :]), deps=[eo6])
        eo8 = P.step("scalar", lambda e, so=so: e.activation(
            t_ost[:, so + 7:so + 8], t_ost[:, so + 6:so + 7], AF.Exp,
            scale=-0.5), deps=[eo7])
        eo9 = P.step("vector", lambda e, fa=fa, so=so: e.tensor_scalar(
            t_osq[:, :], fa, t_ost[:, so + 2:so + 3], t_ost[:, so + 7:so + 8],
            ALU.subtract, ALU.mult), deps=[eo8])
        eoA = P.step("vector", lambda e: e.tensor_add(
            t_osq[:, :], t_osq[:, :], t_xml[:, :]),
            deps=[eo9, lx] + ([prev_ew] if prev_ew else []))
        ew = P.dma("sync", d_out[cix * 128:(cix + 1) * 128, :],
                   t_osq[:, :], deps=[eoA])
        c2b_prev["eo9"] = eo9
        c2b_prev["eoA"] = eoA
        c2b_prev["ew"] = ew

    x2f_guard = []
    fo_guard = []
    for gi in range(TOK // G1K):
        xdeps = [ev_xn2_w[gi * (G1K // 128) + i] for i in range(G1K // 128)]
        x2evs = [P.dma("sync", t_x2f[:, dc * G1K:(dc + 1) * G1K],
                       s_xn2[gi * G1K:(gi + 1) * G1K, dc * 128:(dc + 1) * 128],
                       deps=xdeps + x2f_guard, transpose=True)
                 for dc in range(8)]
        for bi in range(G1K // 256):
            tof = bi * 256
            bdn, bddn = bank(4, hold=True)
            dn_m = []
            h_prev = None
            prev_dn = None
            pend = None
            for hc in range(24):
                bg_, bdg_ = bank(1)
                g_m = [P.step("tensor",
                              lambda e, kc=kc, hc=hc, bg_=bg_, tof=tof: e.matmul(
                    psf(bg_, 0, 128, 0, 256),
                    t_fwg[:, kc * HID + hc * 128:kc * HID + (hc + 1) * 128],
                    t_x2f[:, kc * G1K + tof:kc * G1K + tof + 256],
                    start=(kc == 0), stop=(kc == 7)),
                    deps=[x2evs[kc], fgw[kc]] + (bdg_ if kc == 0 else []))
                    for kc in range(8)]
                bu_, bdu_ = bank(1)
                u_m = [P.step("tensor",
                              lambda e, kc=kc, hc=hc, bu_=bu_, tof=tof: e.matmul(
                    psf(bu_, 0, 128, 0, 256),
                    t_fwu[:, kc * HID + hc * 128:kc * HID + (hc + 1) * 128],
                    t_x2f[:, kc * G1K + tof:kc * G1K + tof + 256],
                    start=(kc == 0), stop=(kc == 7)),
                    deps=[x2evs[kc], fuw[kc]] + (bdu_ if kc == 0 else []))
                    for kc in range(8)]
                et = P.step("scalar", lambda e, bg_=bg_: e.activation(
                    t_fth[:, :], psf(bg_, 0, 128, 0, 256), AF.Tanh,
                    scale=SFF / 2),
                    deps=g_m + ([h_prev] if h_prev else []))
                es = P.step("vector", lambda e: e.tensor_scalar(
                    t_fsg[:, :], t_fth[:, :], 0.5, 0.5, ALU.mult, ALU.add),
                    deps=[et])
                es2 = P.step("vector", lambda e, bg_=bg_: e.scalar_tensor_tensor(
                    t_fsg[:, :], psf(bg_, 0, 128, 0, 256), SFF, t_fsg[:, :],
                    ALU.mult, ALU.mult), deps=[es])
                bank_done(bg_, 1, [es2])
                hpar = hc % 2
                eh = P.step("vector", lambda e, bu_=bu_, hpar=hpar:
                            e.scalar_tensor_tensor(
                    t_hbf[:, hpar * 256:hpar * 256 + 256],
                    psf(bu_, 0, 128, 0, 256), SFF, t_fsg[:, :],
                    ALU.mult, ALU.mult),
                    deps=[es2] + u_m + ([prev_dn] if prev_dn else []))
                bank_done(bu_, 1, [eh])
                h_prev = eh
                if pend is not None:
                    peh, phc = pend
                    for ts_ in range(2):
                        for nh in range(2):
                            dn_m.append(P.step(
                                "tensor",
                                lambda e, ts_=ts_, nh=nh, phc=phc, bdn=bdn,
                                peh=peh: e.matmul(
                                    psf(bdn + 2 * ts_ + nh, 0, 128, 0, 512),
                                    t_hbf[:, (phc % 2) * 256 + ts_ * 128:
                                          (phc % 2) * 256 + (ts_ + 1) * 128],
                                    t_fwd[:, phc * D + nh * 512:
                                          phc * D + (nh + 1) * 512],
                                    start=(phc == 0), stop=False),
                                deps=[peh, fdw[phc]] + (bddn if phc == 0 and
                                                        ts_ == 0 and nh == 0
                                                        else [])))
                    prev_dn = dn_m[-1]
                pend = (eh, hc)
            peh, phc = pend
            for ts_ in range(2):
                for nh in range(2):
                    dn_m.append(P.step(
                        "tensor",
                        lambda e, ts_=ts_, nh=nh, phc=phc, bdn=bdn,
                        peh=peh: e.matmul(
                            psf(bdn + 2 * ts_ + nh, 0, 128, 0, 512),
                            t_hbf[:, (phc % 2) * 256 + ts_ * 128:
                                  (phc % 2) * 256 + (ts_ + 1) * 128],
                            t_fwd[:, phc * D + nh * 512:
                                  phc * D + (nh + 1) * 512],
                            start=False, stop=True),
                        deps=[peh, fdw[phc]]))
            efo = [P.step("scalar", lambda e, ts_=ts_, bdn=bdn: e.mul(
                t_fo[:, ts_ * D:(ts_ + 1) * D],
                ps_t[0:128, 512 * (bdn + 2 * ts_):512 * (bdn + 2 * ts_) + 1024],
                SFF), deps=dn_m + fo_guard) for ts_ in range(2)]
            bank_done(bdn, 4, efo)
            cix0 = (gi * G1K + tof) // 128
            ev = P.dma("sync",
                       s_fdown[cix0 * 128:(cix0 + 2) * 128, :].rearrange(
                           "(ts p) d -> p ts d", ts=2),
                       t_fo[:, :].rearrange("p (ts d) -> p ts d", ts=2),
                       deps=efo)
            ev_fdown_w[cix0] = ev
            ev_fdown_w[cix0 + 1] = ev
            fo_guard = [ev]
        x2f_guard = [dn_m[-1], g_m[-1], u_m[-1]]
        for cix in range(gi * (G1K // 128), (gi + 1) * (G1K // 128)):
            emit_c2b(cix)

    if STAGE <= 7:
        return _cut(P.fence_all())
    c_ctx.close()
    P.finalize(block, sems)
    ctx.close()
    return nc


def kernel(L0_attn_x, L0_attn_Wq, L0_attn_Wk, L0_attn_Wv, L0_attn_Wa,
           L0_attn_ba, L0_attn_Wb, L0_attn_bb, L0_attn_Wg, L0_attn_Wo,
           L0_attn_g_norm, L0_ffn_w_gate, L0_ffn_w_up, L0_ffn_w_down):
    B, S, _ = L0_attn_x.shape
    TOK = B * S // NCORE
    nc = build_program(TOK)
    CON = build_consts()

    wab = np.zeros((D, 64), np.float32)
    wab[:, 0:16] = np.asarray(L0_attn_Wa, np.float32)
    wab[:, 32:48] = np.asarray(L0_attn_Wb, np.float32)
    bab = np.zeros((64, 1), np.float32)
    bab[0:16, 0] = np.asarray(L0_attn_ba, np.float32)
    bab[32:48, 0] = np.asarray(L0_attn_bb, np.float32)

    x = np.ascontiguousarray(
        np.asarray(L0_attn_x, np.float32).reshape(B, 4, S // 4, D))
    base = {
        "wq": np.ascontiguousarray(L0_attn_Wq, np.float32),
        "wk": np.ascontiguousarray(L0_attn_Wk, np.float32),
        "wv": np.ascontiguousarray(L0_attn_Wv, np.float32),
        "wgt": np.ascontiguousarray(L0_attn_Wg, np.float32),
        "wo": np.ascontiguousarray(L0_attn_Wo, np.float32),
        "wab": wab, "bab": bab,
        "gnorm": np.ascontiguousarray(
            np.asarray(L0_attn_g_norm, np.float32).reshape(1, D)),
        "ffn_g": np.ascontiguousarray(L0_ffn_w_gate, np.int32),
        "ffn_u": np.ascontiguousarray(L0_ffn_w_up, np.int32),
        "ffn_d": np.ascontiguousarray(L0_ffn_w_down, np.int32),
    }
    for k, v in CON.items():
        base["c_" + k] = v
    in_maps = []
    for core in range(NCORE):
        m = dict(base)
        m["x"] = np.ascontiguousarray(x[core // 4, core % 4])
        selv = np.zeros((1, 4), np.float32)
        if core % 4 > 0:
            selv[0, core % 4 - 1] = 1.0
        m["selv"] = selv
        in_maps.append(m)

    res = run_bass_kernel_spmd(nc, in_maps, core_ids=list(range(NCORE)))
    out = np.zeros((B, S, D), np.float32)
    for core in range(NCORE):
        out[core // 4, (core % 4) * TOK:(core % 4 + 1) * TOK] = \
            np.asarray(res.results[core]["out"], np.float32)
    return out
